# version 01
# baseline (speedup 1.0000x reference)
import numpy as np
import ml_dtypes
import concourse.bass as bass
import concourse.mybir as mybir
from concourse.bass_utils import run_bass_kernel_spmd

F32 = mybir.dt.float32
BF16 = mybir.dt.bfloat16
U32 = mybir.dt.uint32
AF = mybir.ActivationFunctionType
ALU = mybir.AluOpType
AX = mybir.AxisListType

NCORE = 8
D = 1024
NT = 32
TOK = NT * 128
NH = 16
HD = 64
DFF = 4096
LN_EPS = 1e-5
ATTN_EPS = 1e-6
RMS_EPS = float(np.finfo(np.float32).eps)
E0 = 8
I0 = 512
SCALE0 = 2.5
E1 = 8
I1R = 4096
I1S = 8192
CAP = 5120
W1W = 256

NEGINF = np.float32(-1e9)
BIGCAP = np.float32(1e30)

PASS_FILTER = None
DEBUG_DUMPS = []
STEP_LIMIT = {}
SLOTS = ["l0_attn", "l0_ffn", "l1_attn", "l1_ffn"]


class Pass:

    def __init__(self, bld, name):
        self.b = bld
        self.name = name
        self.steps = []
        self.gpsimd_prelude = None

    def eng(self, engine, fn):
        self.steps.append((engine, fn, "eng"))

    def dma(self, engine, fn):
        self.steps.append((engine, fn, "dma"))

    def coll(self, fn):
        self.steps.append(("gpsimd", fn, "coll"))

    def emit(self):
        if PASS_FILTER is not None and self.name not in PASS_FILTER:
            return
        if self.name in STEP_LIMIT:
            self.steps = self.steps[:STEP_LIMIT[self.name]]
        nc = self.b.nc
        with (
            nc.semaphore(f"{self.name}_ch") as ch,
            nc.semaphore(f"{self.name}_dc") as dc,
            nc.Block() as blk,
        ):
            E = []
            Dn = []
            e_cnt = 0
            d_cnt = 0
            for (_, _, kind) in self.steps:
                E.append(e_cnt)
                Dn.append(d_cnt)
                if kind == "dma":
                    d_cnt += 1
                else:
                    e_cnt += 1
            total_d = d_cnt

            by_eng = {}
            for k, (en, fn, kind) in enumerate(self.steps):
                by_eng.setdefault(en, []).append((k, fn, kind))

            def make_body(items, is_gpsimd=False):
                def body(eng):
                    if is_gpsimd and self.gpsimd_prelude is not None:
                        self.gpsimd_prelude(eng)
                    for (k, fn, kind) in items:
                        if E[k] > 0:
                            eng.wait_ge(ch, E[k])
                        if Dn[k] > 0:
                            eng.wait_ge(dc, 16 * Dn[k])
                        if kind == "dma":
                            ins = fn(eng)
                            ins.then_inc(dc, 16)
                        elif kind == "coll":
                            ins = fn(eng)
                            ins.then_inc(ch, 1)
                        else:
                            ins = fn(eng)
                            assert ins is not None, \
                                f"step {k} on {self.name} returned None"
                            eng.drain().then_inc(ch, 1)
                return body

            for en, items in by_eng.items():
                getattr(blk, en)(make_body(items, en == "gpsimd"))

            if total_d > 0:
                def tail(eng):
                    eng.wait_ge(dc, 16 * total_d)
                    eng.wait_ge(ch, e_cnt)
                blk.sync(tail)


class Builder:
    def __init__(self, nc):
        self.nc = nc
        self.inputs = {}
        self.consts = {}
        self.drams = {}

    def inp(self, name, shape, dtype=F32):
        if name not in self.inputs:
            self.inputs[name] = self.nc.dram_tensor(
                name, list(shape), dtype, kind="ExternalInput").ap()
        return self.inputs[name]

    def out(self, name, shape, dtype=F32):
        return self.nc.dram_tensor(name, list(shape), dtype,
                                   kind="ExternalOutput").ap()

    def dram(self, name, shape, dtype=F32):
        ap = self.nc.dram_tensor(name, list(shape), dtype,
                                 kind="Internal").ap()
        self.drams[name] = ap
        return ap

    def const(self, name, arr):
        if name not in self.consts:
            self.consts[name] = self.nc.inline_tensor(
                np.ascontiguousarray(arr), name=name).ap()
        return self.consts[name]


class Common:
    pass


def setup_common(b, ctx):
    nc = b.nc
    C = Common()
    C.ident_bf = ctx.enter_context(nc.sbuf_tensor("c_identbf", [128, 128], BF16))
    C.ident_f = ctx.enter_context(nc.sbuf_tensor("c_identf", [128, 128], F32))
    C.ones_col = ctx.enter_context(nc.sbuf_tensor("c_onescol", [1, 128], BF16))
    C.ones_t = ctx.enter_context(nc.sbuf_tensor("c_onest", [128, 1], BF16))
    C.ones_colf = ctx.enter_context(nc.sbuf_tensor("c_onescolf", [1, 128], F32))
    C.eps_ln = ctx.enter_context(nc.sbuf_tensor("c_epsln", [128, 1], F32))
    C.eps_rms = ctx.enter_context(nc.sbuf_tensor("c_epsrms", [128, 1], F32))
    idb = b.const("identbf", np.eye(128, dtype=ml_dtypes.bfloat16))
    idf = b.const("identf", np.eye(128, dtype=np.float32))

    p = Pass(b, "init")
    p.dma("sync", lambda e: e.dma_start(C.ident_bf[:], idb[:]))
    p.dma("sync", lambda e: e.dma_start(C.ident_f[:], idf[:]))
    p.eng("vector", lambda e: e.memset(C.ones_col[:], 1.0))
    p.eng("vector", lambda e: e.memset(C.ones_t[:], 1.0))
    p.eng("vector", lambda e: e.memset(C.ones_colf[:], 1.0))
    p.eng("vector", lambda e: e.memset(C.eps_ln[:], float(LN_EPS)))
    p.eng("vector", lambda e: e.memset(C.eps_rms[:], float(RMS_EPS)))
    return C, p



def tr8(p, C, src_bf, dst_bf, ps_tr, nchunk=8):
    def pe(e):
        ins = None
        for k in range(nchunk):
            ident = C.ident_bf if src_bf.dtype == BF16 else C.ident_f
            ins = e.transpose(ps_tr[:, k * 128:(k + 1) * 128],
                              src_bf[:, k * 128:(k + 1) * 128],
                              ident[:, :])
        return ins
    p.eng("tensor", pe)
    p.eng("vector", lambda e: e.tensor_copy(dst_bf[:, :nchunk * 128],
                                            ps_tr[:, :nchunk * 128]))


def ln_stats(p, x_tile, width, st, eps_ap):
    inv = 1.0 / width

    p.eng("scalar", lambda e: e.activation(st["scr"][:, :width],
                                           x_tile[:, :width], AF.Square))
    p.eng("vector", lambda e: e.reduce_sum(st["ssq"][:], st["scr"][:, :width],
                                           AX.X))
    p.eng("vector", lambda e: e.reduce_sum(st["ssum"][:], x_tile[:, :width], AX.X))
    p.eng("vector", lambda e: e.tensor_scalar(st["mu"][:], st["ssum"][:], inv,
                                              None, ALU.mult))
    p.eng("vector", lambda e: e.tensor_tensor(st["var"][:], st["mu"][:],
                                              st["mu"][:], ALU.mult))
    p.eng("vector", lambda e: e.tensor_scalar(st["var"][:], st["ssq"][:], inv,
                                              st["var"][:], ALU.mult,
                                              ALU.subtract))
    def rstd_fn(e):
        e.activation(st["var"][:], st["var"][:], AF.Ln, bias=eps_ap[:])
        return e.activation(st["rstd"][:], st["var"][:], AF.Exp, scale=-0.5)
    p.eng("scalar", rstd_fn)


def ln_apply_plain(p, x_tile, out_tile, st, width=D):
    p.eng("vector", lambda e: e.tensor_scalar(
        out_tile[:, :width], x_tile[:, :width], st["mu"][:], st["rstd"][:],
        ALU.subtract, ALU.mult))


def ln_apply_gb(p, x_tile, out_tile, st, g_bc, b_bc, tmp, width=D):
    p.eng("vector", lambda e: e.tensor_scalar(
        tmp[:, :width], x_tile[:, :width], st["mu"][:], st["rstd"][:],
        ALU.subtract, ALU.mult))
    def fin(e):
        e.tensor_tensor(tmp[:, :width], tmp[:, :width], g_bc[:, :width], ALU.mult)
        return e.tensor_tensor(out_tile[:, :width], tmp[:, :width],
                               b_bc[:, :width], ALU.add)
    p.eng("vector", fin)


def load_weight_bf16(p, src_ap, dst_sb, stage, rows, cols, cast_eng="scalar",
                     piece=1024):
    nchunk = rows // 128
    piece = min(piece, cols)
    for k in range(nchunk):
        for c0 in range(0, cols, piece):
            w = min(piece, cols - c0)
            p.dma("sync", lambda e, k=k, c0=c0, w=w: e.dma_start(
                stage[:, :w], src_ap[k * 128:(k + 1) * 128, c0:c0 + w]))
            if cast_eng == "scalar":
                p.eng("scalar", lambda e, k=k, c0=c0, w=w: e.copy(
                    dst_sb[:, k * cols + c0:k * cols + c0 + w], stage[:, :w]))
            else:
                p.eng("vector", lambda e, k=k, c0=c0, w=w: e.tensor_copy(
                    dst_sb[:, k * cols + c0:k * cols + c0 + w], stage[:, :w]))


def load_row_bf16(p, src_ap, dst_row, stage_row, n, piece=1024):
    piece = min(piece, n)
    for c0 in range(0, n, piece):
        w = min(piece, n - c0)
        p.dma("sync", lambda e, c0=c0, w=w: e.dma_start(
            stage_row[:1, :w], src_ap[None, c0:c0 + w]))
        p.eng("vector", lambda e, c0=c0, w=w: e.tensor_copy(
            dst_row[:1, c0:c0 + w], stage_row[:1, :w]))


def bcast_row(p, C, row_f32, out_bc, ps, n):
    ngr = (n + 511) // 512
    for g in range(ngr):
        w = min(512, n - g * 512)
        p.eng("tensor", lambda e, g=g, w=w: e.matmul(
            ps[:, :w], C.ones_colf[:1, :], row_f32[:1, g * 512:g * 512 + w],
            start=True, stop=True))
        p.eng("vector", lambda e, g=g, w=w: e.tensor_copy(
            out_bc[:, g * 512:g * 512 + w], ps[:, :w]))


def matmul_tm(p, C, xT, w_sb, w_cols, ps, fg0, nfg, kchunks, bias_row=None,
              start_clear=True):
    def pe(e):
        ins = None
        for i in range(nfg):
            fg = fg0 + i
            for d in range(kchunks):
                ins = e.matmul(
                    ps[:, i * 512:(i + 1) * 512],
                    xT[:, d * 128:(d + 1) * 128],
                    w_sb[:, d * w_cols + fg * 512: d * w_cols + fg * 512 + 512],
                    start=(d == 0 and start_clear),
                    stop=(d == kchunks - 1 and bias_row is None))
            if bias_row is not None:
                ins = e.matmul(ps[:, i * 512:(i + 1) * 512],
                               C.ones_col[:1, :],
                               bias_row[:1, fg * 512:fg * 512 + 512],
                               start=False, stop=True)
        return ins
    p.eng("tensor", pe)


def build_l0_attn(b, C, sctx, xs, ys):
    nc = b.nc
    qkv_w = b.inp("L0_attn_qkv_w", [D, 3 * D])
    qkv_b = b.inp("L0_attn_qkv_b", [3 * D])
    out_w = b.inp("L0_attn_out_w", [D, D])
    out_b = b.inp("L0_attn_out_b", [D])
    w1 = b.inp("L0_attn_w1", [D, DFF])
    b1 = b.inp("L0_attn_b1", [DFF])
    w2 = b.inp("L0_attn_w2", [DFF, D])
    b2 = b.inp("L0_attn_b2", [D])
    g1 = b.inp("L0_attn_g1", [D])
    be1 = b.inp("L0_attn_be1", [D])
    g2 = b.inp("L0_attn_g2", [D])
    be2 = b.inp("L0_attn_be2", [D])
    bsel = b.inp("aux_bsel", [128, 4])

    qf_sp = b.dram("l0a_qf", [TOK, D], BF16)
    x1_sp = b.dram("l0a_x1", [TOK, D], F32)
    xn2_sp = b.dram("l0a_xn2", [TOK, D], BF16)
    cc_in = b.dram("l0a_ccin", [4 * 65, D], F32)
    cc_out = b.dram("l0a_ccout", [4 * 65, D], F32)

    passes = []

    kv_mat = sctx.enter_context(nc.sbuf_tensor("l0a_kvmat", [128, 512], BF16))
    z_mat = sctx.enter_context(nc.sbuf_tensor("l0a_zmat", [128, 8], BF16))

    from contextlib import ExitStack
    with ExitStack() as actx:
        sb = lambda n, s, d=F32: actx.enter_context(nc.sbuf_tensor(n, s, d))
        ps = lambda n, s, d=F32: actx.enter_context(nc.psum_tensor(n, s, d))
        p = Pass(b, "l0a1")

        Wq = sb("a1_w", [128, 8 * 3072], BF16)
        stage = sb("a1_stage", [128, 1024])
        qkvb_row = sb("a1_qb", [1, 3072], BF16)
        rowstage = sb("a1_rs", [1, 1024])
        g1bc = sb("a1_g1bc", [128, D])
        be1bc = sb("a1_be1bc", [128, D])
        bselp = sb("a1_bsel", [128, 4])
        xin = sb("a1_xin", [128, D])
        xn_bf = sb("a1_xnbf", [128, D], BF16)
        xnT = sb("a1_xnT", [128, D], BF16)
        qmin = sb("a1_qmin", [128, 512])
        qexp = sb("a1_qexp", [128, 512])
        qf_sb = sb("a1_qf", [128, D], BF16)
        kf_sb = sb("a1_kf", [128, D], BF16)
        v_sb = sb("a1_v", [128, D], BF16)
        kv_f = sb("a1_kvf", [128, 512])
        z_f = sb("a1_zf", [128, 8])
        zrow = sb("a1_zrow", [1, D])
        kvs = sb("a1_kvs", [128, 512])
        zs = sb("a1_zs", [1, D])
        st = {k: sb("a1_st_" + k, [128, 1]) for k in
              ["ssq", "ssum", "mu", "var", "rstd"]}
        st["scr"] = sb("a1_scr", [128, D])
        tmp = sb("a1_tmp", [128, D])

        ps_tr = ps("a1_ptr", [128, 1024], BF16)
        ps_mm = ps("a1_pmm", [128, 2 * 512])
        ps_kv = ps("a1_pkv", [128, 512])
        ps_za = ps("a1_pza", [1, 512])
        ps_zb = ps("a1_pzb", [1, 512])
        ps_bc = ps("a1_pbc", [128, 512])

        load_weight_bf16(p, qkv_w, Wq, stage, D, 3072)
        load_row_bf16(p, qkv_b, qkvb_row, rowstage, 3072)
        grow = sb("a1_grow", [1, D])
        brow = sb("a1_brow", [1, D])
        p.dma("sync", lambda e: e.dma_start(grow[:1, :], g1[None, :]))
        p.dma("sync", lambda e: e.dma_start(brow[:1, :], be1[None, :]))
        p.dma("sync", lambda e: e.dma_start(bselp[:], bsel[:, :]))
        bcast_row(p, C, grow, g1bc, ps_bc, D)
        bcast_row(p, C, brow, be1bc, ps_bc, D)

        for j in range(NT):
            p.dma("sync", lambda e, j=j: e.dma_start(
                xin[:], xs[j * 128:(j + 1) * 128, :]))
            ln_stats(p, xin, D, st, C.eps_ln)
            ln_apply_gb(p, xin, xn_bf, st, g1bc, be1bc, tmp)
            tr8(p, C, xn_bf, xnT, ps_tr)
            for half in range(3):
                matmul_tm(p, C, xnT, Wq, 3072, ps_mm, 2 * half, 2, 8,
                          bias_row=qkvb_row)
                if half < 2:
                    dst = qf_sb if half == 0 else kf_sb
                    for gg in range(2):
                        col = gg * 512
                        p.eng("vector", lambda e, gg=gg: e.tensor_scalar(
                            qmin[:], ps_mm[:, gg * 512:(gg + 1) * 512],
                            0.0, None, ALU.min))
                        p.eng("scalar", lambda e: e.activation(
                            qexp[:], qmin[:], AF.Exp))
                        p.eng("vector", lambda e, gg=gg, dst=dst, col=col:
                              e.scalar_tensor_tensor(
                                  dst[:, col:col + 512],
                                  ps_mm[:, gg * 512:(gg + 1) * 512],
                                  0.0, qexp[:], ALU.max, ALU.add))
                else:
                    p.eng("vector", lambda e: e.tensor_copy(
                        v_sb[:], ps_mm[:]))
            def kvmm(e, j=j):
                ins = None
                for h in range(NH):
                    ins = e.matmul(
                        ps_kv[64 * (h % 2):64 * (h % 2) + 64,
                              (h // 2) * 64:(h // 2) * 64 + 64],
                        kf_sb[:, h * 64:(h + 1) * 64],
                        v_sb[:, h * 64:(h + 1) * 64],
                        start=(j == 0), stop=(j == NT - 1))
                ins = e.matmul(ps_za[:1, :], C.ones_t[:, :1],
                               kf_sb[:, 0:512], start=(j == 0),
                               stop=(j == NT - 1))
                ins = e.matmul(ps_zb[:1, :], C.ones_t[:, :1],
                               kf_sb[:, 512:1024], start=(j == 0),
                               stop=(j == NT - 1))
                return ins
            p.eng("tensor", kvmm)
            p.dma("gpsimd", lambda e, j=j: e.dma_start(
                qf_sp[j * 128:(j + 1) * 128, :], qf_sb[:]))

        p.eng("vector", lambda e: e.tensor_copy(kv_f[:], ps_kv[:]))
        def zev(e):
            e.tensor_copy(zrow[:1, 0:512], ps_za[:1, :])
            return e.tensor_copy(zrow[:1, 512:1024], ps_zb[:1, :])
        p.eng("vector", zev)
        for bb in range(4):
            p.eng("vector", lambda e, bb=bb: e.tensor_scalar(
                kvs[:], kv_f[:], bselp[:, bb:bb + 1], None, ALU.mult))
            p.eng("vector", lambda e, bb=bb: e.tensor_scalar(
                zs[:1, :], zrow[:1, :], bselp[0:1, bb:bb + 1], None, ALU.mult))
            for par in range(2):
                src = kvs[64 * par:64 * par + 64, :].rearrange(
                    "d (u e) -> d u e", e=64)
                dst = cc_in[bb * 65:bb * 65 + 64, :].rearrange(
                    "d (u two e) -> two d u e", two=2, e=64)[par]
                p.dma("gpsimd", lambda e, src=src, dst=dst: e.dma_start(dst, src))
            p.dma("gpsimd", lambda e, bb=bb: e.dma_start(
                cc_in[bb * 65 + 64:bb * 65 + 65, :], zs[:1, :]))
        p.coll(lambda e: e.collective_compute(
            "AllReduce", ALU.add, [list(range(NCORE))],
            [cc_in[:, :]], [cc_out[:, :]]))
        kvz_acc = sb("a1_kvza", [65, D])
        kvz_t = sb("a1_kvzt", [65, D])
        p.eng("vector", lambda e: e.memset(kvz_acc[:], 0.0))
        for bb in range(4):
            p.dma("sync", lambda e, bb=bb: e.dma_start(
                kvz_t[:], cc_out[bb * 65:(bb + 1) * 65, :]))
            p.eng("vector", lambda e, bb=bb: e.scalar_tensor_tensor(
                kvz_acc[:], kvz_t[:], bselp[0:65, bb:bb + 1], kvz_acc[:],
                ALU.mult, ALU.add))
        for par in range(2):
            src = kvz_acc[0:64, :].rearrange("d (u two e) -> d u two e",
                                             two=2, e=64)[:, :, par, :]
            p.eng("vector", lambda e, par=par, src=src: e.tensor_copy(
                kv_mat[64 * par:64 * par + 64, :].rearrange(
                    "d (u e) -> d u e", e=64), src))
        ps_z = ps("a1_pz", [128, 8])
        def ztr(e):
            ins = None
            for u in range(8):
                ins = e.transpose(ps_z[:, u:u + 1],
                                  kvz_acc[64:65, u * 128:(u + 1) * 128],
                                  C.ident_f[64:65, 64:65])
            return ins
        p.eng("tensor", ztr)
        p.eng("vector", lambda e: e.tensor_copy(z_mat[:], ps_z[:]))
        p.emit()
        passes.append("l0a1")

    with ExitStack() as actx:
        sb = lambda n, s, d=F32: actx.enter_context(nc.sbuf_tensor(n, s, d))
        ps = lambda n, s, d=F32: actx.enter_context(nc.psum_tensor(n, s, d))
        p = Pass(b, "l0a2")

        Wo = sb("a2_w", [128, 8 * D], BF16)
        stage = sb("a2_stage", [128, D])
        ob_row = sb("a2_ob", [1, D], BF16)
        rowstage = sb("a2_rs", [1, D])
        g2bc = sb("a2_g2bc", [128, D])
        be2bc = sb("a2_be2bc", [128, D])
        xin = sb("a2_xin", [128, D])
        qf_bf = sb("a2_qf", [128, D], BF16)
        qfT = sb("a2_qfT", [128, D], BF16)
        y_bf = sb("a2_y", [128, D], BF16)
        yT = sb("a2_yT", [128, D], BF16)
        x1 = sb("a2_x1", [128, D])
        xn2_bf = sb("a2_xn2", [128, D], BF16)
        recip = sb("a2_recip", [128, 16])
        den_t = sb("a2_dent", [128, 16])
        st = {k: sb("a2_st_" + k, [128, 1]) for k in
              ["ssq", "ssum", "mu", "var", "rstd"]}
        st["scr"] = sb("a2_scr", [128, D])
        tmp = sb("a2_tmp", [128, D])

        ps_tr = ps("a2_ptr", [128, 1024], BF16)
        ps_yn = ps("a2_pyn", [128, 1024])
        ps_yd = ps("a2_pyd", [128, 16])
        ps_o = ps("a2_po", [128, 2 * 512])
        ps_bc = ps("a2_pbc", [128, 512])

        load_weight_bf16(p, out_w, Wo, stage, D, D)
        load_row_bf16(p, out_b, ob_row, rowstage, D)
        grow = sb("a2_grow", [1, D])
        brow = sb("a2_brow", [1, D])
        p.dma("sync", lambda e: e.dma_start(grow[:1, :], g2[None, :]))
        p.dma("sync", lambda e: e.dma_start(brow[:1, :], be2[None, :]))
        bcast_row(p, C, grow, g2bc, ps_bc, D)
        bcast_row(p, C, brow, be2bc, ps_bc, D)

        for j in range(NT):
            p.dma("sync", lambda e, j=j: e.dma_start(
                qf_bf[:], qf_sp[j * 128:(j + 1) * 128, :]))
            p.dma("sync", lambda e, j=j: e.dma_start(
                xin[:], xs[j * 128:(j + 1) * 128, :]))
            tr8(p, C, qf_bf, qfT, ps_tr)
            def ymm(par):
                def fn(e):
                    ins = None
                    for h in range(par, NH, 2):
                        lhs = qfT[64 * par:64 * par + 64,
                                  (h // 2) * 128:(h // 2) * 128 + 128]
                        rhs = kv_mat[64 * par:64 * par + 64,
                                     (h // 2) * 64:(h // 2) * 64 + 64]
                        ins = e.matmul(ps_yn[:, h * 64:(h + 1) * 64], lhs,
                                       rhs, start=True, stop=True)
                        rz = z_mat[64 * par:64 * par + 64,
                                   (h // 2):(h // 2) + 1]
                        ins = e.matmul(ps_yd[:, h:h + 1], lhs, rz,
                                       start=True, stop=True)
                    return ins
                return fn
            p.eng("tensor", ymm(0))
            p.eng("tensor", ymm(1))
            p.eng("vector", lambda e: e.tensor_scalar(
                den_t[:], ps_yd[:], float(ATTN_EPS), None, ALU.max))
            p.eng("vector", lambda e: e.reciprocal(recip[:], den_t[:]))
            def ydiv(e):
                ins = None
                for h in range(NH):
                    ins = e.tensor_scalar(
                        y_bf[:, h * 64:(h + 1) * 64],
                        ps_yn[:, h * 64:(h + 1) * 64],
                        recip[:, h:h + 1], None, ALU.mult)
                return ins
            p.eng("vector", ydiv)
            tr8(p, C, y_bf, yT, ps_tr)
            matmul_tm(p, C, yT, Wo, D, ps_o, 0, 2, 8, bias_row=ob_row)
            def resid(e):
                e.tensor_tensor(x1[:, 0:512], ps_o[:, 0:512], xin[:, 0:512],
                                ALU.add)
                return e.tensor_tensor(x1[:, 512:1024], ps_o[:, 512:1024],
                                       xin[:, 512:1024], ALU.add)
            p.eng("vector", resid)
            ln_stats(p, x1, D, st, C.eps_ln)
            ln_apply_gb(p, x1, xn2_bf, st, g2bc, be2bc, tmp)
            p.dma("gpsimd", lambda e, j=j: e.dma_start(
                x1_sp[j * 128:(j + 1) * 128, :], x1[:]))
            p.dma("gpsimd", lambda e, j=j: e.dma_start(
                xn2_sp[j * 128:(j + 1) * 128, :], xn2_bf[:]))
        p.emit()
        passes.append("l0a2")

    with ExitStack() as actx:
        sb = lambda n, s, d=F32: actx.enter_context(nc.sbuf_tensor(n, s, d))
        ps = lambda n, s, d=F32: actx.enter_context(nc.psum_tensor(n, s, d))
        p = Pass(b, "l0a3")

        W1s = sb("a3_w1", [128, 8 * DFF], BF16)
        W2s = sb("a3_w2", [128, 32 * D], BF16)
        stage = sb("a3_stage", [128, 1024])
        b1_row = sb("a3_b1", [1, DFF], BF16)
        b2_row = sb("a3_b2", [1, D], BF16)
        rowstage = sb("a3_rs", [1, 1024])
        xin = sb("a3_xin", [128, D])
        x1t = sb("a3_x1", [128, D])
        xn2 = sb("a3_xn2", [128, D], BF16)
        xn2T = sb("a3_xn2T", [128, D], BF16)
        h_bf = sb("a3_h", [128, DFF], BF16)
        hT = sb("a3_hT", [128, DFF], BF16)
        arg = sb("a3_arg", [128, D])
        outt = sb("a3_out", [128, D])
        st = {k: sb("a3_st_" + k, [128, 1]) for k in
              ["ssq", "ssum", "mu", "var", "rstd"]}
        st["scr"] = sb("a3_scr", [128, D])

        ps_tr = ps("a3_ptr", [128, 1024], BF16)
        ps_h = ps("a3_ph", [128, 2 * 512])
        ps_y = ps("a3_py", [128, 2 * 512])

        load_weight_bf16(p, w1, W1s, stage, D, DFF)
        load_weight_bf16(p, w2, W2s, stage, DFF, D)
        load_row_bf16(p, b1, b1_row, rowstage, DFF)
        load_row_bf16(p, b2, b2_row, rowstage, D)

        for j in range(NT):
            p.dma("sync", lambda e, j=j: e.dma_start(
                xn2[:], xn2_sp[j * 128:(j + 1) * 128, :]))
            p.dma("sync", lambda e, j=j: e.dma_start(
                x1t[:], x1_sp[j * 128:(j + 1) * 128, :]))
            p.dma("sync", lambda e, j=j: e.dma_start(
                xin[:], xs[j * 128:(j + 1) * 128, :]))
            tr8(p, C, xn2, xn2T, ps_tr)
            for half in range(4):
                matmul_tm(p, C, xn2T, W1s, DFF, ps_h, 2 * half, 2, 8,
                          bias_row=b1_row)
                p.eng("scalar", lambda e, half=half: e.activation(
                    h_bf[:, half * 1024:(half + 1) * 1024], ps_h[:],
                    AF.Gelu))
            for q in range(4):
                tr8(p, C, h_bf[:, q * 1024:(q + 1) * 1024],
                    hT[:, q * 1024:(q + 1) * 1024], ps_tr)
            matmul_tm(p, C, hT, W2s, D, ps_y, 0, 2, 32, bias_row=b2_row)
            def res(e):
                e.tensor_tensor(arg[:, 0:512], ps_y[:, 0:512], x1t[:, 0:512],
                                ALU.add)
                e.tensor_tensor(arg[:, 512:1024], ps_y[:, 512:1024],
                                x1t[:, 512:1024], ALU.add)
                return e.tensor_tensor(arg[:], arg[:], xin[:], ALU.add)
            p.eng("vector", res)
            ln_stats(p, arg, D, st, C.eps_ln)
            ln_apply_plain(p, arg, outt, st)
            p.dma("gpsimd", lambda e, j=j: e.dma_start(
                ys[j * 128:(j + 1) * 128, :], outt[:]))
        p.emit()
        passes.append("l0a3")

    return passes


def build_l0_ffn(b, C, sctx, xs, ys):
    from contextlib import ExitStack
    nc = b.nc
    gate_w = b.inp("L0_ffn_gate_w", [E0, D])
    cb_in = b.inp("L0_ffn_correction_bias", [E0])
    Wg = b.inp("L0_ffn_Wg", [E0, D, I0])
    Wu = b.inp("L0_ffn_Wu", [E0, D, I0])
    Wd = b.inp("L0_ffn_Wd", [E0, I0, D])
    Wg_s = b.inp("L0_ffn_Wg_s", [D, I0])
    Wu_s = b.inp("L0_ffn_Wu_s", [D, I0])
    Wd_s = b.inp("L0_ffn_Wd_s", [I0, D])

    xT_sp = b.dram("l0f_xT", [TOK, D], BF16)
    racc = b.dram("l0f_racc", [TOK, D], F32)

    cw_em = sctx.enter_context(nc.sbuf_tensor("l0f_cw", [128, E0 * NT], F32))

    with ExitStack() as actx:
        sb = lambda n, s, d=F32: actx.enter_context(nc.sbuf_tensor(n, s, d))
        ps = lambda n, s, d=F32: actx.enter_context(nc.psum_tensor(n, s, d))
        p = Pass(b, "l0f0")

        gwT = sb("f0_gwT", [128, 8 * E0])
        gstage = sb("f0_gstage", [8, D])
        xTf = sb("f0_xTf", [128, D])
        cbrow = sb("f0_cbrow", [1, E0])
        cb_bc = sb("f0_cbbc", [128, E0])
        xin = sb("f0_xin", [128, D])
        x_bf = sb("f0_xbf", [128, D], BF16)
        xT = sb("f0_xT", [128, D], BF16)
        lgs_em = sb("f0_lgs", [128, E0 * NT])
        sc_em = sb("f0_sc", [128, E0 * NT])
        gs_em = sb("f0_gs", [128, 4 * NT])
        cnt4 = sb("f0_cnt4", [128, 4 * NT])
        tmpb = sb("f0_tmpb", [128, NT])
        smask = sb("f0_smask", [128, E0 * NT])
        sfc = sb("f0_sfc", [128, E0 * NT])
        cnt8 = sb("f0_cnt8", [128, E0 * NT])
        sel4 = sb("f0_sel4", [128, E0 * NT])
        w4 = sb("f0_w4", [128, E0 * NT])
        wsum = sb("f0_wsum", [128, NT])
        wfac = sb("f0_wfac", [128, NT])

        ps_tr = ps("f0_ptr", [128, 1024], BF16)
        ps_trf = ps("f0_ptrf", [128, 1024])
        ps_l = ps("f0_pl", [128, E0])
        ps_g8 = ps("f0_pg8", [128, 8 * E0])
        ps_bc = ps("f0_pbc", [128, 512])

        p.dma("sync", lambda e: e.dma_start(gstage[:, :], gate_w[:, :]))
        def gtr(e):
            ins = None
            for d in range(8):
                ins = e.transpose(ps_g8[:, d * E0:(d + 1) * E0],
                                  gstage[:, d * 128:(d + 1) * 128],
                                  C.ident_f[0:8, 0:8])
            return ins
        p.eng("tensor", gtr)
        p.eng("vector", lambda e: e.tensor_copy(gwT[:, :], ps_g8[:, :]))
        p.dma("sync", lambda e: e.dma_start(cbrow[:1, :], cb_in[None, :]))
        p.eng("tensor", lambda e: e.matmul(ps_bc[:, :E0], C.ones_colf[:1, :],
                                           cbrow[:1, :], start=True, stop=True))
        p.eng("vector", lambda e: e.tensor_copy(cb_bc[:, :], ps_bc[:, :E0]))

        for j in range(NT):
            p.dma("sync", lambda e, j=j: e.dma_start(
                xin[:], xs[j * 128:(j + 1) * 128, :]))
            p.eng("vector", lambda e: e.tensor_copy(x_bf[:], xin[:]))
            tr8(p, C, x_bf, xT, ps_tr)
            p.dma("gpsimd", lambda e, j=j: e.dma_start(
                xT_sp[j * 128:(j + 1) * 128, :], xT[:]))
            tr8(p, C, xin, xTf, ps_trf)
            def lmm(e):
                ins = None
                for d in range(8):
                    ins = e.matmul(ps_l[:, :], xTf[:, d * 128:(d + 1) * 128],
                                   gwT[:, d * E0:(d + 1) * E0],
                                   start=(d == 0), stop=(d == 7))
                return ins
            p.eng("tensor", lmm)
            p.eng("vector", lambda e, j=j: e.tensor_copy(
                lgs_em[:, j::NT], ps_l[:, :]))

        blk = lambda t, e0: t[:, e0 * NT:(e0 + 1) * NT]
        p.eng("scalar", lambda e: e.activation(sc_em[:, :], lgs_em[:, :],
                                               AF.Sigmoid))
        def addcb(e):
            ins = None
            for e0 in range(E0):
                ins = e.tensor_scalar(blk(sc_em, e0), blk(sc_em, e0),
                                      cb_bc[:, e0:e0 + 1], None, ALU.add)
            return ins
        p.eng("vector", addcb)
        def gsum(e):
            ins = None
            for g in range(4):
                ins = e.tensor_tensor(blk(gs_em, g), blk(sc_em, 2 * g),
                                      blk(sc_em, 2 * g + 1), ALU.add)
            return ins
        p.eng("vector", gsum)
        p.eng("vector", lambda e: e.memset(cnt4[:, :], 0.0))
        def grank(e):
            ins = None
            for g in range(4):
                for gp in range(4):
                    if gp == g:
                        continue
                    op = ALU.is_ge if gp < g else ALU.is_gt
                    e.tensor_tensor(tmpb[:, :], blk(gs_em, gp), blk(gs_em, g),
                                    op)
                    ins = e.tensor_tensor(blk(cnt4, g), blk(cnt4, g),
                                          tmpb[:, :], ALU.add)
            return ins
        p.eng("vector", grank)
        def gmask(e):
            e.tensor_single_scalar(cnt4[:, :], cnt4[:, :], 2.0, ALU.is_lt)
            ins = None
            for e0 in range(E0):
                ins = e.tensor_copy(blk(smask, e0), blk(cnt4, e0 // 2))
            return ins
        p.eng("vector", gmask)
        p.eng("vector", lambda e: e.tensor_tensor(sfc[:, :], smask[:, :],
                                                  sc_em[:, :], ALU.mult))
        p.eng("vector", lambda e: e.memset(cnt8[:, :], 0.0))
        for e0 in range(E0):
            def erank(e, e0=e0):
                ins = None
                for ep in range(E0):
                    if ep == e0:
                        continue
                    op = ALU.is_ge if ep < e0 else ALU.is_gt
                    e.tensor_tensor(tmpb[:, :], blk(sfc, ep), blk(sfc, e0), op)
                    ins = e.tensor_tensor(blk(cnt8, e0), blk(cnt8, e0),
                                          tmpb[:, :], ALU.add)
                return ins
            p.eng("vector", erank)
        def finw(e):
            e.tensor_single_scalar(sel4[:, :], cnt8[:, :], 4.0, ALU.is_lt)
            return sel4
        p.eng("vector", lambda e: e.tensor_single_scalar(
            sel4[:, :], cnt8[:, :], 4.0, ALU.is_lt))
        def subcb(e):
            ins = None
            for e0 in range(E0):
                ins = e.tensor_scalar(blk(sc_em, e0), blk(sc_em, e0),
                                      cb_bc[:, e0:e0 + 1], None, ALU.subtract)
            return ins
        p.eng("vector", subcb)
        p.eng("vector", lambda e: e.tensor_tensor(w4[:, :], sel4[:, :],
                                                  sc_em[:, :], ALU.mult))
        def wsumf(e):
            e.tensor_tensor(wsum[:, :], blk(w4, 0), blk(w4, 1), ALU.add)
            for e0 in range(2, E0):
                e.tensor_tensor(wsum[:, :], wsum[:, :], blk(w4, e0), ALU.add)
            return e.tensor_single_scalar(wsum[:, :], wsum[:, :], 1e-20,
                                          ALU.add)
        p.eng("vector", wsumf)
        p.eng("vector", lambda e: e.reciprocal(wfac[:, :], wsum[:, :]))
        p.eng("vector", lambda e: e.tensor_single_scalar(
            wfac[:, :], wfac[:, :], float(SCALE0), ALU.mult))
        def cwf(e):
            ins = None
            for e0 in range(E0):
                ins = e.tensor_tensor(blk(cw_em, e0), blk(w4, e0),
                                      wfac[:, :], ALU.mult)
            return ins
        p.eng("vector", cwf)
        p.emit()

    with ExitStack() as actx:
        sb = lambda n, s, d=F32: actx.enter_context(nc.sbuf_tensor(n, s, d))
        ps = lambda n, s, d=F32: actx.enter_context(nc.psum_tensor(n, s, d))
        p = Pass(b, "l0f1")

        Wgs = sb("f1_wg", [128, 8 * I0], BF16)
        Wus = sb("f1_wu", [128, 8 * I0], BF16)
        Wds = sb("f1_wd", [128, 4 * D], BF16)
        stage = sb("f1_stage", [128, 1024])
        xT = sb("f1_xT", [128, D], BF16)
        gs_bf = sb("f1_gs", [128, I0], BF16)
        h_bf = sb("f1_h", [128, I0], BF16)
        hT = sb("f1_hT", [128, I0], BF16)
        ysb = sb("f1_ysb", [128, D])
        ps_g = ps("f1_pg", [128, I0])
        ps_u = ps("f1_pu", [128, I0])
        ps_tr = ps("f1_ptr", [128, 512], BF16)
        ps_y = ps("f1_py", [128, 2 * 512])

        for ex in range(E0 + 1):
            shared = ex == E0
            wg_src = Wg_s if shared else Wg[ex]
            wu_src = Wu_s if shared else Wu[ex]
            wd_src = Wd_s if shared else Wd[ex]
            load_weight_bf16(p, wg_src, Wgs, stage, D, I0)
            load_weight_bf16(p, wu_src, Wus, stage, D, I0)
            load_weight_bf16(p, wd_src, Wds, stage, I0, D)
            for j in range(NT):
                p.dma("sync", lambda e, j=j: e.dma_start(
                    xT[:], xT_sp[j * 128:(j + 1) * 128, :]))
                def gumm(e):
                    ins = None
                    for d in range(8):
                        xTd = xT[:, d * 128:(d + 1) * 128]
                        ins = e.matmul(ps_g[:, :], xTd,
                                       Wgs[:, d * I0:(d + 1) * I0],
                                       start=(d == 0), stop=(d == 7))
                        ins = e.matmul(ps_u[:, :], xTd,
                                       Wus[:, d * I0:(d + 1) * I0],
                                       start=(d == 0), stop=(d == 7))
                    return ins
                p.eng("tensor", gumm)
                p.eng("scalar", lambda e: e.activation(gs_bf[:], ps_g[:],
                                                       AF.Silu))
                if shared:
                    p.eng("vector", lambda e: e.tensor_tensor(
                        h_bf[:], ps_u[:], gs_bf[:], ALU.mult))
                else:
                    p.eng("vector", lambda e, ex=ex, j=j:
                          e.scalar_tensor_tensor(
                              h_bf[:], ps_u[:], cw_em[:, ex * NT + j:
                                                      ex * NT + j + 1],
                              gs_bf[:], ALU.mult, ALU.mult))
                tr8(p, C, h_bf, hT, ps_tr, nchunk=4)
                def ymm(e):
                    ins = None
                    for fg in range(2):
                        for i in range(4):
                            ins = e.matmul(
                                ps_y[:, fg * 512:(fg + 1) * 512],
                                hT[:, i * 128:(i + 1) * 128],
                                Wds[:, i * D + fg * 512:i * D + fg * 512 + 512],
                                start=(i == 0), stop=(i == 3))
                    return ins
                p.eng("tensor", ymm)
                p.eng("vector", lambda e: e.tensor_copy(ysb[:], ps_y[:]))
                if ex == 0:
                    p.dma("gpsimd", lambda e, j=j: e.dma_start(
                        racc[j * 128:(j + 1) * 128, :], ysb[:]))
                else:
                    p.dma("gpsimd", lambda e, j=j: e.dma_start(
                        racc[j * 128:(j + 1) * 128, :], ysb[:],
                        accum_op=ALU.add))
        p.emit()

    with ExitStack() as actx:
        sb = lambda n, s, d=F32: actx.enter_context(nc.sbuf_tensor(n, s, d))
        p = Pass(b, "l0f2")
        xin = sb("f2_xin", [128, D])
        rin = sb("f2_rin", [128, D])
        arg = sb("f2_arg", [128, D])
        outt = sb("f2_out", [128, D])
        st = {k: sb("f2_st_" + k, [128, 1]) for k in
              ["ssq", "ssum", "mu", "var", "rstd"]}
        st["scr"] = sb("f2_scr", [128, D])
        for j in range(NT):
            p.dma("sync", lambda e, j=j: e.dma_start(
                xin[:], xs[j * 128:(j + 1) * 128, :]))
            p.dma("sync", lambda e, j=j: e.dma_start(
                rin[:], racc[j * 128:(j + 1) * 128, :]))
            p.eng("vector", lambda e: e.tensor_tensor(arg[:], xin[:], rin[:],
                                                      ALU.add))
            ln_stats(p, arg, D, st, C.eps_ln)
            ln_apply_plain(p, arg, outt, st)
            p.dma("gpsimd", lambda e, j=j: e.dma_start(
                ys[j * 128:(j + 1) * 128, :], outt[:]))
        p.emit()
    return ["l0f0", "l0f1", "l0f2"]


def build_l1_attn(b, C, sctx, xs, ys):
    from contextlib import ExitStack
    nc = b.nc
    ln1_g = b.inp("L1_attn_ln1_g", [D])
    ln1_b = b.inp("L1_attn_ln1_b", [D])
    ln2_g = b.inp("L1_attn_ln2_g", [D])
    ln2_b = b.inp("L1_attn_ln2_b", [D])
    W1 = b.inp("L1_attn_W1", [D, DFF])
    b1 = b.inp("L1_attn_b1", [DFF])
    W2 = b.inp("L1_attn_W2", [DFF, D])
    b2 = b.inp("L1_attn_b2", [D])
    lsel = b.inp("aux_lsel", [128, NCORE])
    rsel = b.inp("aux_rsel", [128, NCORE])
    maskin = b.inp("aux_maskcap", [6 * 128, 768])

    hal_in = b.dram("l1a_halin", [512, D], BF16)
    b.dram("l1a_attn", [TOK, D], F32)
    hal_out = b.dram("l1a_halout", [NCORE * 512, D], BF16)
    x1_sp = b.dram("l1a_x1", [TOK, D], F32)

    with ExitStack() as actx:
        sb = lambda n, s, d=F32: actx.enter_context(nc.sbuf_tensor(n, s, d))
        ps = lambda n, s, d=F32: actx.enter_context(nc.psum_tensor(n, s, d))
        p = Pass(b, "l1a1")

        mask_sb = sb("c1_mask", [128, 6 * 768])
        selp = sb("c1_selp", [128, 2 * NCORE])
        xin = sb("c1_xin", [128, D])
        halo_x = [sb(f"c1_hx{i}", [128, D], BF16) for i in range(4)]
        halo_xT = [sb(f"c1_hxT{i}", [128, D], BF16) for i in range(4)]
        ring_x = [sb(f"c1_rx{i}", [128, D], BF16) for i in range(8)]
        ring_xT = [sb(f"c1_rxT{i}", [128, D], BF16) for i in range(8)]
        accf = sb("c1_accf", [128, D])
        tmpbf = sb("c1_tmpbf", [128, D], BF16)
        s_sb = sb("c1_ssb", [128, 768])
        negmax = sb("c1_negmax", [128, 1])
        den = sb("c1_den", [128, 1])
        recip = sb("c1_recip", [128, 1])
        p_bf = sb("c1_pbf", [128, 768], BF16)
        pT = sb("c1_pT", [128, 768], BF16)
        attn_t = sb("c1_attn", [128, D])
        t1 = sb("c1_t1", [128, D])
        x1 = sb("c1_x1", [128, D])
        g1bc = sb("c1_g1bc", [128, D])
        b1bc = sb("c1_b1bc", [128, D])
        grow = sb("c1_grow", [1, D])
        tmp = sb("c1_tmp", [128, D])
        st = {k: sb("c1_st_" + k, [128, 1]) for k in
              ["ssq", "ssum", "mu", "var", "rstd"]}
        st["scr"] = sb("c1_scr", [128, D])

        ps_tr = ps("c1_ptr", [128, 1024], BF16)
        ps_s = ps("c1_ps", [128, 768])
        ps_o = ps("c1_po", [128, 64])
        ps_bc = ps("c1_pbc", [128, 512])

        for i in range(6):
            p.dma("sync", lambda e, i=i: e.dma_start(
                mask_sb[:, i * 768:(i + 1) * 768],
                maskin[i * 128:(i + 1) * 128, :]))
        p.dma("sync", lambda e: e.dma_start(selp[:, 0:NCORE], lsel[:, :]))
        p.dma("sync", lambda e: e.dma_start(selp[:, NCORE:], rsel[:, :]))
        p.dma("sync", lambda e: e.dma_start(grow[:1, :], ln1_g[None, :]))
        bcast_row(p, C, grow, g1bc, ps_bc, D)
        p.dma("sync", lambda e: e.dma_start(grow[:1, :], ln1_b[None, :]))
        bcast_row(p, C, grow, b1bc, ps_bc, D)

        for i, j in enumerate([0, 1, 30, 31]):
            p.dma("sync", lambda e, j=j: e.dma_start(
                xin[:], xs[j * 128:(j + 1) * 128, :]))
            p.eng("vector", lambda e: e.tensor_copy(tmpbf[:], xin[:]))
            p.dma("gpsimd", lambda e, i=i: e.dma_start(
                hal_in[i * 128:(i + 1) * 128, :], tmpbf[:]))
        p.coll(lambda e: e.collective_compute(
            "AllGather", ALU.bypass, [list(range(NCORE))],
            [hal_in[:, :]], [hal_out[:, :]]))
        for side in range(2):
            for r in range(2):
                p.eng("vector", lambda e: e.memset(accf[:], 0.0))
                for cc in range(NCORE):
                    src = hal_out[cc * 512 + (1 - side) * 256 + r * 128:
                                  cc * 512 + (1 - side) * 256 + r * 128 + 128, :]
                    p.dma("sync", lambda e, src=src: e.dma_start(
                        tmpbf[:], src))
                    p.eng("vector", lambda e, cc=cc, side=side:
                          e.scalar_tensor_tensor(
                              accf[:], tmpbf[:],
                              selp[:, side * NCORE + cc:side * NCORE + cc + 1],
                              accf[:], ALU.mult, ALU.add))
                idx = side * 2 + r
                p.eng("vector", lambda e, idx=idx: e.tensor_copy(
                    halo_x[idx][:], accf[:]))
            for r in range(2):
                idx = side * 2 + r
                tr8(p, C, halo_x[idx], halo_xT[idx], ps_tr)

        def wtile(w):
            if w < 0:
                return halo_x[w + 2], halo_xT[w + 2]
            if w >= NT:
                return halo_x[w - NT + 2], halo_xT[w - NT + 2]
            return ring_x[w % 8], ring_xT[w % 8]

        def materialize(j):
            p.dma("sync", lambda e, j=j: e.dma_start(
                xin[:], xs[j * 128:(j + 1) * 128, :]))
            p.eng("vector", lambda e, j=j: e.tensor_copy(
                ring_x[j % 8][:], xin[:]))
            tr8(p, C, ring_x[j % 8], ring_xT[j % 8], ps_tr)

        for j in range(4):
            materialize(j)

        for j in range(NT):
            if j + 4 < NT:
                materialize(j + 4)
            ws = j - 2 - (j % 2)
            if j <= 1:
                mslot = j
            elif j >= 30:
                mslot = j - 26
            else:
                mslot = 2 + (j % 2)
            mask_ap = mask_sb[:, mslot * 768:(mslot + 1) * 768]
            for h in range(NH):
                par = h % 2
                hi = h // 2
                def smm(e, h=h, par=par, hi=hi, ws=ws, j=j):
                    ins = None
                    for wi in range(6):
                        _, xT_w = wtile(ws + wi)
                        ins = e.matmul(
                            ps_s[:, wi * 128:(wi + 1) * 128],
                            wtile(j)[1][64 * par:64 * par + 64,
                                        hi * 128:hi * 128 + 128],
                            xT_w[64 * par:64 * par + 64,
                                 hi * 128:hi * 128 + 128],
                            start=True, stop=True)
                    return ins
                p.eng("tensor", smm)
                p.eng("vector", lambda e, mask_ap=mask_ap: e.tensor_tensor(
                    s_sb[:], ps_s[:], mask_ap, ALU.min))
                p.eng("vector", lambda e: e.reduce_max(
                    negmax[:], s_sb[:], AX.X, negate=True))
                p.eng("scalar", lambda e: e.activation(
                    p_bf[:], s_sb[:], AF.Exp, bias=negmax[:],
                    accum_out=den[:]))
                p.eng("vector", lambda e: e.reciprocal(recip[:], den[:]))
                def ptr(e):
                    ins = None
                    for wi in range(6):
                        ins = e.transpose(
                            ps_tr[:, wi * 128:(wi + 1) * 128],
                            p_bf[:, wi * 128:(wi + 1) * 128],
                            C.ident_bf[:, :])
                    return ins
                p.eng("tensor", ptr)
                p.eng("vector", lambda e: e.tensor_copy(
                    pT[:], ps_tr[:, 0:768]))
                def omm(e, h=h, ws=ws):
                    ins = None
                    for wi in range(6):
                        x_w, _ = wtile(ws + wi)
                        ins = e.matmul(
                            ps_o[:, :], pT[:, wi * 128:(wi + 1) * 128],
                            x_w[:, h * 64:(h + 1) * 64],
                            start=(wi == 0), stop=(wi == 5))
                    return ins
                p.eng("tensor", omm)
                p.eng("vector", lambda e, h=h: e.tensor_scalar(
                    attn_t[:, h * 64:(h + 1) * 64], ps_o[:, :], recip[:],
                    None, ALU.mult))
            if DEBUG_DUMPS:
                p.dma("gpsimd", lambda e, j=j: e.dma_start(
                    b.drams["l1a_attn"][j * 128:(j + 1) * 128, :], attn_t[:]))
            p.dma("sync", lambda e, j=j: e.dma_start(
                xin[:], xs[j * 128:(j + 1) * 128, :]))
            p.eng("vector", lambda e: e.tensor_tensor(t1[:], attn_t[:],
                                                      xin[:], ALU.add))
            ln_stats(p, t1, D, st, C.eps_ln)
            ln_apply_gb(p, t1, x1, st, g1bc, b1bc, tmp)
            p.dma("gpsimd", lambda e, j=j: e.dma_start(
                x1_sp[j * 128:(j + 1) * 128, :], x1[:]))
        p.emit()

    with ExitStack() as actx:
        sb = lambda n, s, d=F32: actx.enter_context(nc.sbuf_tensor(n, s, d))
        ps = lambda n, s, d=F32: actx.enter_context(nc.psum_tensor(n, s, d))
        p = Pass(b, "l1a2")

        W1s = sb("c2_w1", [128, 8 * DFF], BF16)
        W2s = sb("c2_w2", [128, 32 * D], BF16)
        stage = sb("c2_stage", [128, 1024])
        b1_row = sb("c2_b1", [1, DFF], BF16)
        b2_row = sb("c2_b2", [1, D], BF16)
        rowstage = sb("c2_rs", [1, 1024])
        g2bc = sb("c2_g2bc", [128, D])
        b2bc = sb("c2_b2bc", [128, D])
        grow = rowstage
        xin = sb("c2_xin", [128, D])
        x1t = sb("c2_x1", [128, D])
        x1bf = sb("c2_x1bf", [128, D], BF16)
        x1T = sb("c2_x1T", [128, D], BF16)
        h_bf = sb("c2_h", [128, DFF], BF16)
        hT = sb("c2_hT", [128, DFF], BF16)
        t2 = sb("c2_t2", [128, D])
        so = sb("c2_so", [128, D])
        arg = sb("c2_arg", [128, D])
        outt = sb("c2_out", [128, D])
        st = {k: sb("c2_st_" + k, [128, 1]) for k in
              ["ssq", "ssum", "mu", "var", "rstd"]}
        st["scr"] = sb("c2_scr", [128, D])
        tmp = st["scr"]

        ps_tr = ps("c2_ptr", [128, 1024], BF16)
        ps_h = ps("c2_ph", [128, 2 * 512])
        ps_y = ps("c2_py", [128, 2 * 512])
        ps_bc = ps("c2_pbc", [128, 512])

        load_weight_bf16(p, W1, W1s, stage, D, DFF)
        load_weight_bf16(p, W2, W2s, stage, DFF, D)
        load_row_bf16(p, b1, b1_row, rowstage, DFF)
        load_row_bf16(p, b2, b2_row, rowstage, D)
        p.dma("sync", lambda e: e.dma_start(grow[:1, :], ln2_g[None, :]))
        bcast_row(p, C, grow, g2bc, ps_bc, D)
        p.dma("sync", lambda e: e.dma_start(grow[:1, :], ln2_b[None, :]))
        bcast_row(p, C, grow, b2bc, ps_bc, D)

        for j in range(NT):
            p.dma("sync", lambda e, j=j: e.dma_start(
                x1t[:], x1_sp[j * 128:(j + 1) * 128, :]))
            p.eng("vector", lambda e: e.tensor_copy(x1bf[:], x1t[:]))
            tr8(p, C, x1bf, x1T, ps_tr)
            for half in range(4):
                matmul_tm(p, C, x1T, W1s, DFF, ps_h, 2 * half, 2, 8,
                          bias_row=b1_row)
                p.eng("scalar", lambda e, half=half: e.activation(
                    h_bf[:, half * 1024:(half + 1) * 1024], ps_h[:],
                    AF.Relu))
            for q in range(4):
                tr8(p, C, h_bf[:, q * 1024:(q + 1) * 1024],
                    hT[:, q * 1024:(q + 1) * 1024], ps_tr)
            matmul_tm(p, C, hT, W2s, D, ps_y, 0, 2, 32, bias_row=b2_row)
            def res(e):
                e.tensor_tensor(t2[:, 0:512], ps_y[:, 0:512], x1t[:, 0:512],
                                ALU.add)
                return e.tensor_tensor(t2[:, 512:1024], ps_y[:, 512:1024],
                                       x1t[:, 512:1024], ALU.add)
            p.eng("vector", res)
            ln_stats(p, t2, D, st, C.eps_ln)
            ln_apply_gb(p, t2, so, st, g2bc, b2bc, tmp)
            p.dma("sync", lambda e, j=j: e.dma_start(
                xin[:], xs[j * 128:(j + 1) * 128, :]))
            p.eng("vector", lambda e: e.tensor_tensor(arg[:], so[:], xin[:],
                                                      ALU.add))
            ln_stats(p, arg, D, st, C.eps_ln)
            ln_apply_plain(p, arg, outt, st)
            p.dma("gpsimd", lambda e, j=j: e.dma_start(
                ys[j * 128:(j + 1) * 128, :], outt[:]))
        p.emit()
    return ["l1a1", "l1a2"]


def build_l1_ffn(b, C, sctx, xs, ys):
    from contextlib import ExitStack
    nc = b.nc
    router_w = b.inp("L1_ffn_router_w", [E1, D])
    my_rg = b.inp("aux_my_rg", [D, I1R])
    my_ru = b.inp("aux_my_ru", [D, I1R])
    my_rd = b.inp("aux_my_rd", [I1R, D])
    sg = b.inp("L1_ffn_sg", [D, I1S])
    su = b.inp("L1_ffn_su", [D, I1S])
    sd = b.inp("L1_ffn_sd", [I1S, D])
    pm_in = b.inp("aux_pm", [E1, 1])

    xT_sp = b.dram("l1f_xT", [TOK, D], BF16)
    xbf_sp = b.dram("l1f_xbf", [TOK, D], BF16)
    g_sp = b.dram("l1f_g", [TOK, I1S], BF16)
    h_sp = b.dram("l1f_h", [TOK, I1S], BF16)
    hR_sp = b.dram("l1f_hR", [CAP, I1R], BF16)
    comp_d = b.dram("l1f_comp", [E1 * CAP, D], BF16)
    mycomp = b.dram("l1f_mycomp", [CAP, D], BF16)
    contrib = b.dram("l1f_contrib", [CAP, D], BF16)
    ag_contrib = b.dram("l1f_agc", [E1 * CAP, D], BF16)
    cnt_d = b.dram("l1f_cnt", [E1, 1], F32)
    cnt_ag = b.dram("l1f_cntag", [NCORE * E1, 1], F32)

    weff_em = sctx.enter_context(nc.sbuf_tensor("l1f_weff", [128, E1 * NT], F32))
    idx_em = sctx.enter_context(nc.sbuf_tensor("l1f_idx", [128, E1 * NT], U32))
    idx2_em = sctx.enter_context(nc.sbuf_tensor("l1f_idx2", [128, E1 * NT], U32))

    EM = lambda t, e0: t[:, e0 * NT:(e0 + 1) * NT]

    with ExitStack() as actx:
        sb = lambda n, s, d=F32: actx.enter_context(nc.sbuf_tensor(n, s, d))
        ps = lambda n, s, d=F32: actx.enter_context(nc.psum_tensor(n, s, d))
        p = Pass(b, "l1f0")
        rwT = sb("r0_rwT", [128, 8 * E1])
        rstage = sb("r0_rstage", [E1, D])
        xin = sb("r0_xin", [128, D])
        x_bf = sb("r0_xbf", [128, D], BF16)
        xT = sb("r0_xT", [128, D], BF16)
        xnf = sb("r0_xnf", [128, D])
        xnT = sb("r0_xnT", [128, D])
        lgs_tm = sb("r0_lgs", [128, NT * E1])
        st = {k: sb("r0_st_" + k, [128, 1]) for k in ["ssq", "rstd"]}
        scr = sb("r0_scr", [128, D])
        ps_tr = ps("r0_ptr", [128, 1024], BF16)
        ps_trf = ps("r0_ptrf", [128, 1024])
        ps_r = ps("r0_pr", [128, E1])
        ps_g8 = ps("r0_pg8", [128, 8 * E1])

        p.dma("sync", lambda e: e.dma_start(rstage[:, :], router_w[:, :]))
        def rtr(e):
            ins = None
            for d in range(8):
                ins = e.transpose(ps_g8[:, d * E1:(d + 1) * E1],
                                  rstage[:, d * 128:(d + 1) * 128],
                                  C.ident_f[0:E1, 0:E1])
            return ins
        p.eng("tensor", rtr)
        p.eng("vector", lambda e: e.tensor_copy(rwT[:, :], ps_g8[:, :]))

        for j in range(NT):
            p.dma("sync", lambda e, j=j: e.dma_start(
                xin[:], xs[j * 128:(j + 1) * 128, :]))
            p.eng("vector", lambda e: e.tensor_copy(x_bf[:], xin[:]))
            tr8(p, C, x_bf, xT, ps_tr)
            p.dma("gpsimd", lambda e, j=j: e.dma_start(
                xbf_sp[j * 128:(j + 1) * 128, :], x_bf[:]))
            p.dma("gpsimd", lambda e, j=j: e.dma_start(
                xT_sp[j * 128:(j + 1) * 128, :], xT[:]))
            p.eng("scalar", lambda e: e.activation(scr[:], xin[:], AF.Square))
            p.eng("vector", lambda e: e.reduce_sum(st["ssq"][:], scr[:], AX.X))
            def rstd_fn(e):
                e.activation(st["rstd"][:], st["ssq"][:], AF.Ln,
                             bias=C.eps_rms[:], scale=1.0 / D)
                return e.activation(st["rstd"][:], st["rstd"][:], AF.Exp,
                                    scale=-0.5)
            p.eng("scalar", rstd_fn)
            p.eng("vector", lambda e: e.tensor_scalar(
                xnf[:], xin[:], st["rstd"][:], None, ALU.mult))
            tr8(p, C, xnf, xnT, ps_trf)
            def rmm(e):
                ins = None
                for d in range(8):
                    ins = e.matmul(ps_r[:, :], xnT[:, d * 128:(d + 1) * 128],
                                   rwT[:, d * E1:(d + 1) * E1],
                                   start=(d == 0), stop=(d == 7))
                return ins
            p.eng("tensor", rmm)
            p.eng("vector", lambda e, j=j: e.tensor_copy(
                lgs_tm[:, j * E1:(j + 1) * E1], ps_r[:, :]))

        maxs = sb("r0_maxs", [128, NT])
        esub = sb("r0_esub", [128, NT * E1])
        ep_tm = sb("r0_eptm", [128, NT * E1])
        ep_em = sb("r0_epem", [128, E1 * NT])
        cnt_em = sb("r0_cnt", [128, E1 * NT])
        sel_em = sb("r0_sel", [128, E1 * NT])
        wsel = sb("r0_wsel", [128, E1 * NT])
        wsum = sb("r0_wsum", [128, NT])
        recipw = sb("r0_recipw", [128, NT])
        rw_em = sb("r0_rw", [128, E1 * NT])
        tmpb = sb("r0_tmpb", [128, NT])
        colsums = sb("r0_cols", [128, E1])
        cnt8 = sb("r0_cnt8", [E1, 1])
        offs_row = sb("r0_offs", [1, E1])
        offs_bc = sb("r0_offsbc", [128, E1])
        cntl = sb("r0_cntl", [E1, E1])
        pm_sb = sb("r0_pm", [E1, 1])
        selT = sb("r0_selT", [32, 128])
        zeros32 = sb("r0_z32", [32, 128])
        rsT = sb("r0_rsT", [32, 128])
        totrow = sb("r0_totrow", [1, 32])
        inclrow = sb("r0_incl", [1, 32])
        exclcol = sb("r0_excl", [32, 1])
        rank_b = sb("r0_rankb", [128, NT])
        grank_b = sb("r0_grankb", [128, NT])
        valid_b = sb("r0_validb", [128, NT])
        sv_b = sb("r0_svb", [128, NT])
        inv_b = sb("r0_invb", [128, NT])
        idxf_b = sb("r0_idxfb", [128, NT])
        ps_sm = ps("r0_psm", [128, 512])
        ps_t32 = ps_sm[0:32, 0:128]
        ps_t128 = ps_sm[:, 0:32]
        ps_r1 = ps_sm[0:1, 128:160]
        ps_c1 = ps_sm[0:32, 160:161]
        ps_off = ps_sm[0:1, 168:168 + E1]
        ps_obc = ps_sm[:, 176:176 + E1]

        TM = lambda t, e0: t[:, e0::E1]
        p.eng("vector", lambda e: e.reduce_max(
            maxs[:], lgs_tm[:, :].rearrange("p (j e) -> p j e", e=E1), AX.X))
        def esubf(e):
            ins = None
            for e0 in range(E1):
                ins = e.tensor_tensor(TM(esub, e0), TM(lgs_tm, e0), maxs[:],
                                      ALU.subtract)
            return ins
        p.eng("vector", esubf)
        p.eng("scalar", lambda e: e.activation(ep_tm[:, :], esub[:, :], AF.Exp))
        p.eng("vector", lambda e: e.memset(cnt_em[:, :], 0.0))
        for e0 in range(E1):
            def erank(e, e0=e0):
                ins = None
                for ep in range(E1):
                    if ep == e0:
                        continue
                    op = ALU.is_ge if ep < e0 else ALU.is_gt
                    e.tensor_tensor(tmpb[:, :], TM(ep_tm, ep), TM(ep_tm, e0),
                                    op)
                    ins = e.tensor_tensor(EM(cnt_em, e0), EM(cnt_em, e0),
                                          tmpb[:, :], ALU.add)
                return ins
            p.eng("vector", erank)
        p.eng("vector", lambda e: e.tensor_single_scalar(
            sel_em[:, :], cnt_em[:, :], 2.0, ALU.is_lt))
        def epem(e):
            ins = None
            for e0 in range(E1):
                ins = e.tensor_copy(EM(ep_em, e0), TM(ep_tm, e0))
            return ins
        p.eng("vector", epem)
        p.eng("vector", lambda e: e.tensor_tensor(wsel[:, :], sel_em[:, :],
                                                  ep_em[:, :], ALU.mult))
        def wsumf(e):
            e.tensor_tensor(wsum[:, :], EM(wsel, 0), EM(wsel, 1), ALU.add)
            ins = None
            for e0 in range(2, E1):
                ins = e.tensor_tensor(wsum[:, :], wsum[:, :], EM(wsel, e0),
                                      ALU.add)
            return ins
        p.eng("vector", wsumf)
        p.eng("vector", lambda e: e.reciprocal(recipw[:, :], wsum[:, :]))
        def rwf(e):
            ins = None
            for e0 in range(E1):
                ins = e.tensor_tensor(EM(rw_em, e0), EM(wsel, e0),
                                      recipw[:, :], ALU.mult)
            return ins
        p.eng("vector", rwf)
        def colsf(e):
            ins = None
            for e0 in range(E1):
                ins = e.reduce_sum(colsums[:, e0:e0 + 1], EM(sel_em, e0),
                                   AX.X)
            return ins
        p.eng("vector", colsf)
        p.eng("tensor", lambda e: e.transpose(
            ps_t32[0:E1, :], colsums[:, :], C.ident_f[:, :]))
        p.eng("vector", lambda e: e.reduce_sum(cnt8[:, :], ps_t32[0:E1, :],
                                               AX.X))
        p.dma("gpsimd", lambda e: e.dma_start(cnt_d[:, :], cnt8[:, :]))
        p.coll(lambda e: e.collective_compute(
            "AllGather", ALU.bypass, [list(range(NCORE))],
            [cnt_d[:, :]], [cnt_ag[:, :]]))
        p.dma("sync", lambda e: e.dma_start(
            cntl[:, :], cnt_ag.rearrange("(c e) one -> c (e one)", e=E1)))
        p.dma("sync", lambda e: e.dma_start(pm_sb[:, :], pm_in[:, :]))
        p.eng("tensor", lambda e: e.matmul(ps_off[:, :], pm_sb[:, :],
                                           cntl[:, :], start=True, stop=True))
        p.eng("vector", lambda e: e.tensor_copy(offs_row[:, :], ps_off[:, :]))
        p.eng("tensor", lambda e: e.matmul(ps_obc[:, :], C.ones_colf[:1, :],
                                           offs_row[:1, :], start=True,
                                           stop=True))
        p.eng("vector", lambda e: e.tensor_copy(offs_bc[:, :], ps_obc[:, :]))
        p.eng("vector", lambda e: e.memset(zeros32[:, :], 0.0))
        for e0 in range(E1):
            p.eng("tensor", lambda e, e0=e0: e.transpose(
                ps_t32[:, :], EM(sel_em, e0), C.ident_f[:, :]))
            p.eng("vector", lambda e: e.tensor_copy(selT[:, :], ps_t32[:, :]))
            p.eng("vector", lambda e: e.tensor_tensor_scan(
                rsT[:, :], zeros32[:, :], selT[:, :], 0.0, ALU.add, ALU.add))
            p.eng("tensor", lambda e: e.transpose(
                ps_r1[:, :], rsT[:, 127:128], C.ident_f[0:32, 0:32]))
            p.eng("vector", lambda e: e.tensor_copy(totrow[:, :], ps_r1[:, :]))
            p.eng("vector", lambda e: e.tensor_tensor_scan(
                inclrow[:, :], zeros32[0:1, 0:32], totrow[:, :], 0.0,
                ALU.add, ALU.add))
            p.eng("vector", lambda e: e.tensor_tensor(
                inclrow[:, :], inclrow[:, :], totrow[:, :], ALU.subtract))
            p.eng("tensor", lambda e: e.transpose(
                ps_c1[:, :], inclrow[:, :], C.ident_f[0:1, 0:1]))
            p.eng("vector", lambda e: e.tensor_copy(exclcol[:, :],
                                                    ps_c1[:, :]))
            p.eng("vector", lambda e: e.tensor_scalar(
                rsT[:, :], rsT[:, :], exclcol[:, :], None, ALU.add))
            p.eng("tensor", lambda e: e.transpose(
                ps_t128[:, :], rsT[:, :], C.ident_f[0:32, 0:32]))
            p.eng("vector", lambda e: e.tensor_copy(rank_b[:, :],
                                                    ps_t128[:, :]))
            p.eng("vector", lambda e, e0=e0: e.tensor_tensor(
                grank_b[:, :], rank_b[:, :], EM(sel_em, e0), ALU.subtract))
            p.eng("vector", lambda e, e0=e0: e.tensor_scalar(
                grank_b[:, :], grank_b[:, :], offs_bc[:, e0:e0 + 1], None,
                ALU.add))
            p.eng("vector", lambda e: e.tensor_single_scalar(
                valid_b[:, :], grank_b[:, :], float(CAP), ALU.is_lt))
            p.eng("vector", lambda e, e0=e0: e.tensor_tensor(
                sv_b[:, :], EM(sel_em, e0), valid_b[:, :], ALU.mult))
            p.eng("vector", lambda e, e0=e0: e.tensor_tensor(
                EM(weff_em, e0), EM(rw_em, e0), sv_b[:, :], ALU.mult))
            p.eng("vector", lambda e: e.tensor_tensor(
                idxf_b[:, :], grank_b[:, :], sv_b[:, :], ALU.mult))
            p.eng("vector", lambda e: e.tensor_scalar(
                inv_b[:, :], sv_b[:, :], -1e9, 1e9, ALU.mult, ALU.add))
            p.eng("vector", lambda e: e.tensor_tensor(
                idxf_b[:, :], idxf_b[:, :], inv_b[:, :], ALU.add))
            p.eng("vector", lambda e, e0=e0: e.tensor_copy(
                EM(idx_em, e0), idxf_b[:, :]))
            p.eng("vector", lambda e, e0=e0: e.tensor_scalar(
                idxf_b[:, :], idxf_b[:, :], float(e0 * CAP), None, ALU.add))
            p.eng("vector", lambda e, e0=e0: e.tensor_copy(
                EM(idx2_em, e0), idxf_b[:, :]))
        p.emit()

    with ExitStack() as actx:
        sb = lambda n, s, d=F32: actx.enter_context(nc.sbuf_tensor(n, s, d))
        ps = lambda n, s, d=F32: actx.enter_context(nc.psum_tensor(n, s, d))
        p = Pass(b, "l1f2a")
        bcreg = {}
        def prelude(g):
            r = g.alloc_register("bcreg2a")
            g.reg_mov(r, E1 * CAP - 1)
            bcreg["r"] = r
        p.gpsimd_prelude = prelude
        zz = sb("r2_zz", [128, 4096], BF16)
        xbt = sb("r2_xbt", [128, D], BF16)
        Rg = sb("r2_rg", [128, 8 * I1R], BF16)
        Ru = sb("r2_ru", [128, 8 * I1R], BF16)
        stage = sb("r2_stage", [128, 1024])
        xgT = sb("r2_xgT", [128, D], BF16)
        gs_bf = sb("r2_gs", [128, 512], BF16)
        hT_t = sb("r2_hT", [128, I1R], BF16)
        ps_tr = ps("r2_ptr", [128, 1024], BF16)
        ps_g = ps("r2_pg", [128, 512])
        ps_u = ps("r2_pu", [128, 512])

        p.eng("vector", lambda e: e.memset(zz[:, :], 0.0))
        for n in range(0, E1 * CAP // 128, 4):
            p.dma("gpsimd", lambda e, n=n: e.dma_start(
                comp_d.rearrange("(m p) d -> p m d", p=128)[:, n:n + 4, :],
                zz[:, :].rearrange("p (m d) -> p m d", d=D)))
        for j in range(NT):
            p.dma("sync", lambda e, j=j: e.dma_start(
                xbt[:], xbf_sp[j * 128:(j + 1) * 128, :]))
            for e0 in range(E1):
                p.dma("gpsimd", lambda e, j=j, e0=e0: e.indirect_dma_start(
                    comp_d[:, :],
                    bass.IndirectOffsetOnAxis(
                        idx2_em[:, e0 * NT + j:e0 * NT + j + 1], 0),
                    xbt[:, :], None,
                    bounds_check=bcreg["r"], oob_is_err=False))
        p.coll(lambda e: e.collective_compute(
            "ReduceScatter", ALU.add, [list(range(NCORE))],
            [comp_d[:, :]], [mycomp[:, :]]))
        load_weight_bf16(p, my_rg, Rg, stage, D, I1R)
        load_weight_bf16(p, my_ru, Ru, stage, D, I1R)
        for t in range(CAP // 128):
            p.dma("sync", lambda e, t=t: e.dma_start(
                xbt[:], mycomp[t * 128:(t + 1) * 128, :]))
            tr8(p, C, xbt, xgT, ps_tr)
            for fg in range(8):
                def gumm(e, fg=fg):
                    ins = None
                    for d in range(8):
                        xTd = xgT[:, d * 128:(d + 1) * 128]
                        ins = e.matmul(ps_g[:, :], xTd,
                                       Rg[:, d * I1R + fg * 512:
                                          d * I1R + fg * 512 + 512],
                                       start=(d == 0), stop=(d == 7))
                        ins = e.matmul(ps_u[:, :], xTd,
                                       Ru[:, d * I1R + fg * 512:
                                          d * I1R + fg * 512 + 512],
                                       start=(d == 0), stop=(d == 7))
                    return ins
                p.eng("tensor", gumm)
                p.eng("scalar", lambda e: e.activation(gs_bf[:], ps_g[:],
                                                       AF.Silu))
                p.eng("vector", lambda e: e.tensor_tensor(
                    gs_bf[:, :], ps_u[:], gs_bf[:], ALU.mult))
                def htr(e, fg=fg):
                    ins = None
                    for k in range(4):
                        ins = e.transpose(
                            ps_tr[:, k * 128:(k + 1) * 128],
                            gs_bf[:, k * 128:(k + 1) * 128],
                            C.ident_bf[:, :])
                    return ins
                p.eng("tensor", htr)
                p.eng("vector", lambda e, fg=fg: e.tensor_copy(
                    hT_t[:, fg * 512:(fg + 1) * 512], ps_tr[:, 0:512]))
            p.dma("gpsimd", lambda e, t=t: e.dma_start(
                hR_sp[t * 128:(t + 1) * 128, :], hT_t[:]))
        p.emit()

    with ExitStack() as actx:
        sb = lambda n, s, d=F32: actx.enter_context(nc.sbuf_tensor(n, s, d))
        ps = lambda n, s, d=F32: actx.enter_context(nc.psum_tensor(n, s, d))
        p = Pass(b, "l1f2b")
        Rd = sb("r2b_rd", [128, 32 * D], BF16)
        stage = sb("r2b_stage", [128, 1024])
        hT_t = sb("r2b_hT", [128, I1R], BF16)
        ctile = sb("r2b_ct", [128, D], BF16)
        ps_y = ps("r2b_py", [128, 2 * 512])
        load_weight_bf16(p, my_rd, Rd, stage, I1R, D)
        for t in range(CAP // 128):
            p.dma("sync", lambda e, t=t: e.dma_start(
                hT_t[:], hR_sp[t * 128:(t + 1) * 128, :]))
            def ymm(e):
                ins = None
                for i in range(32):
                    for fg in range(2):
                        ins = e.matmul(
                            ps_y[:, fg * 512:(fg + 1) * 512],
                            hT_t[:, i * 128:(i + 1) * 128],
                            Rd[:, i * D + fg * 512:i * D + fg * 512 + 512],
                            start=(i == 0), stop=(i == 31))
                return ins
            p.eng("tensor", ymm)
            p.eng("vector", lambda e: e.tensor_copy(ctile[:], ps_y[:]))
            p.dma("gpsimd", lambda e, t=t: e.dma_start(
                contrib[t * 128:(t + 1) * 128, :], ctile[:]))
        p.coll(lambda e: e.collective_compute(
            "AllGather", ALU.bypass, [list(range(NCORE))],
            [contrib[:, :]], [ag_contrib[:, :]]))
        p.emit()

    with ExitStack() as actx:
        sb = lambda n, s, d=F32: actx.enter_context(nc.sbuf_tensor(n, s, d))
        ps = lambda n, s, d=F32: actx.enter_context(nc.psum_tensor(n, s, d))
        p = Pass(b, "l1f3")
        Sg = sb("r3_sg", [128, 8 * I1S], BF16)
        stage = sb("r3_stage", [128, 1024])
        xT = sb("r3_xT", [128, D], BF16)
        g_t = sb("r3_gt", [128, I1S], BF16)
        ps_g = ps("r3_pg", [128, 512])
        load_weight_bf16(p, sg, Sg, stage, D, I1S)
        for j in range(NT):
            p.dma("sync", lambda e, j=j: e.dma_start(
                xT[:], xT_sp[j * 128:(j + 1) * 128, :]))
            for fg in range(16):
                def gmm(e, fg=fg):
                    ins = None
                    for d in range(8):
                        ins = e.matmul(ps_g[:, :],
                                       xT[:, d * 128:(d + 1) * 128],
                                       Sg[:, d * I1S + fg * 512:
                                          d * I1S + fg * 512 + 512],
                                       start=(d == 0), stop=(d == 7))
                    return ins
                p.eng("tensor", gmm)
                p.eng("scalar", lambda e, fg=fg: e.activation(
                    g_t[:, fg * 512:(fg + 1) * 512], ps_g[:], AF.Silu))
            p.dma("gpsimd", lambda e, j=j: e.dma_start(
                g_sp[j * 128:(j + 1) * 128, :], g_t[:]))
        p.emit()

    with ExitStack() as actx:
        sb = lambda n, s, d=F32: actx.enter_context(nc.sbuf_tensor(n, s, d))
        ps = lambda n, s, d=F32: actx.enter_context(nc.psum_tensor(n, s, d))
        p = Pass(b, "l1f4")
        Su = sb("r4_su", [128, 8 * I1S], BF16)
        stage = sb("r4_stage", [128, 1024])
        xT = sb("r4_xT", [128, D], BF16)
        g_t = sb("r4_gt", [128, I1S], BF16)
        h_bf = sb("r4_h", [128, 512], BF16)
        hT_t = sb("r4_hT", [128, I1S], BF16)
        ps_u = ps("r4_pu", [128, 512])
        ps_tr = ps("r4_ptr", [128, 512], BF16)
        load_weight_bf16(p, su, Su, stage, D, I1S)
        for j in range(NT):
            p.dma("sync", lambda e, j=j: e.dma_start(
                xT[:], xT_sp[j * 128:(j + 1) * 128, :]))
            p.dma("sync", lambda e, j=j: e.dma_start(
                g_t[:], g_sp[j * 128:(j + 1) * 128, :]))
            for fg in range(16):
                def umm(e, fg=fg):
                    ins = None
                    for d in range(8):
                        ins = e.matmul(ps_u[:, :],
                                       xT[:, d * 128:(d + 1) * 128],
                                       Su[:, d * I1S + fg * 512:
                                          d * I1S + fg * 512 + 512],
                                       start=(d == 0), stop=(d == 7))
                    return ins
                p.eng("tensor", umm)
                p.eng("vector", lambda e, fg=fg: e.tensor_tensor(
                    h_bf[:, :], ps_u[:], g_t[:, fg * 512:(fg + 1) * 512],
                    ALU.mult))
                def htr(e):
                    ins = None
                    for k in range(4):
                        ins = e.transpose(ps_tr[:, k * 128:(k + 1) * 128],
                                          h_bf[:, k * 128:(k + 1) * 128],
                                          C.ident_bf[:, :])
                    return ins
                p.eng("tensor", htr)
                p.eng("vector", lambda e, fg=fg: e.tensor_copy(
                    hT_t[:, fg * 512:(fg + 1) * 512], ps_tr[:, 0:512]))
            p.dma("gpsimd", lambda e, j=j: e.dma_start(
                h_sp[j * 128:(j + 1) * 128, :], hT_t[:]))
        p.emit()

    with ExitStack() as actx:
        sb = lambda n, s, d=F32: actx.enter_context(nc.sbuf_tensor(n, s, d))
        ps = lambda n, s, d=F32: actx.enter_context(nc.psum_tensor(n, s, d))
        p = Pass(b, "l1f5")
        bcreg = {}
        def prelude(g):
            r = g.alloc_register("bcreg5")
            g.reg_mov(r, E1 * CAP - 1)
            bcreg["r"] = r
        p.gpsimd_prelude = prelude
        Sd = sb("r5_sd", [128, 64 * D], BF16)
        stage = sb("r5_stage", [128, 1024])
        hT_t = sb("r5_hT", [128, I1S], BF16)
        gath = sb("r5_gath", [128, D], BF16)
        acc = sb("r5_acc", [128, D])
        xin = sb("r5_xin", [128, D])
        arg = sb("r5_arg", [128, D])
        outt = sb("r5_out", [128, D])
        st = {k: sb("r5_st_" + k, [128, 1]) for k in
              ["ssq", "ssum", "mu", "var", "rstd"]}
        st["scr"] = sb("r5_scr", [128, D])
        ps_y = ps("r5_py", [128, 2 * 512])
        load_weight_bf16(p, sd, Sd, stage, I1S, D)
        p.eng("vector", lambda e: e.memset(gath[:, :], 0.0))
        for j in range(NT):
            p.dma("sync", lambda e, j=j: e.dma_start(
                hT_t[:], h_sp[j * 128:(j + 1) * 128, :]))
            def ymm(e):
                ins = None
                for i in range(64):
                    for fg in range(2):
                        ins = e.matmul(
                            ps_y[:, fg * 512:(fg + 1) * 512],
                            hT_t[:, i * 128:(i + 1) * 128],
                            Sd[:, i * D + fg * 512:i * D + fg * 512 + 512],
                            start=(i == 0), stop=(i == 63))
                return ins
            p.eng("tensor", ymm)
            p.eng("vector", lambda e: e.tensor_copy(acc[:], ps_y[:]))
            for e0 in range(E1):
                p.dma("gpsimd", lambda e, j=j, e0=e0: e.indirect_dma_start(
                    gath[:, :], None, ag_contrib[:, :],
                    bass.IndirectOffsetOnAxis(
                        idx2_em[:, e0 * NT + j:e0 * NT + j + 1], 0),
                    bounds_check=bcreg["r"], oob_is_err=False))
                p.eng("vector", lambda e, j=j, e0=e0: e.scalar_tensor_tensor(
                    acc[:, :], gath[:, :],
                    weff_em[:, e0 * NT + j:e0 * NT + j + 1], acc[:, :],
                    ALU.mult, ALU.add))
            p.dma("sync", lambda e, j=j: e.dma_start(
                xin[:], xs[j * 128:(j + 1) * 128, :]))
            p.eng("vector", lambda e: e.tensor_tensor(arg[:], acc[:], xin[:],
                                                      ALU.add))
            ln_stats(p, arg, D, st, C.eps_ln)
            ln_apply_plain(p, arg, outt, st)
            p.dma("gpsimd", lambda e, j=j: e.dma_start(
                ys[j * 128:(j + 1) * 128, :], outt[:]))
        p.emit()
    return ["l1f0", "l1f2a", "l1f2b", "l1f3", "l1f4", "l1f5"]


def build_program(slots, fused):
    from contextlib import ExitStack
    nc = bass.Bass("TRN2", target_bir_lowering=False)
    b = Builder(nc)
    xs_ext = b.inp("xs", [TOK, D], F32)
    ys_ext = b.out("ys", [TOK, D], F32)
    bufs = [xs_ext]
    for i in range(len(slots) - 1):
        bufs.append(b.dram(f"stream{i}", [TOK, D], F32))
    bufs.append(ys_ext)

    with ExitStack() as sctx:
        C, initp = setup_common(b, sctx)
        initp.emit()
        for i, s in enumerate(slots):
            src, dst = bufs[i], bufs[i + 1]
            if s == "l0_attn":
                build_l0_attn(b, C, sctx, src, dst)
            elif s == "l0_ffn":
                build_l0_ffn(b, C, sctx, src, dst)
            elif s == "l1_attn":
                build_l1_attn(b, C, sctx, src, dst)
            elif s == "l1_ffn":
                build_l1_ffn(b, C, sctx, src, dst)
            else:
                raise ValueError(s)
        if DEBUG_DUMPS:
            p = Pass(b, "dbgdump")
            for name in DEBUG_DUMPS:
                ap = b.drams[name]
                o = b.out("dump_" + name, ap.shape, ap.dtype)
                p.dma("gpsimd", lambda e, o=o, ap=ap: e.dma_start(o[:, :],
                                                                 ap[:, :]))
            p.steps.append(("sync", lambda e: e.nop(), "eng"))
            p.emit()
    return nc, b


def build_maskcap(core):
    out = np.empty((6, 128, 768), np.float32)
    col = np.arange(768)[None, :]
    pp = np.arange(128)[:, None]
    for s, j in enumerate([0, 1, 2, 3, 30, 31]):
        pj = j % 2
        rel = (col // 128 - 2 - pj) * 128 + (col % 128) - pp
        valid = np.abs(rel) <= W1W
        if j <= 1 and core % 2 == 0:
            valid = valid & (col >= 256)
        if j >= 30 and core % 2 == 1:
            valid = valid & (col < 512)
        out[s] = np.where(valid, 1e30, -1e9)
    return np.ascontiguousarray(out.reshape(6 * 128, 768).astype(np.float32))


def aux_inputs(core, inputs):
    bsel = np.zeros((128, 4), np.float32)
    bsel[:, core // 2] = 1.0
    lsel = np.zeros((128, NCORE), np.float32)
    rsel = np.zeros((128, NCORE), np.float32)
    if core % 2 == 1:
        lsel[:, core - 1] = 1.0
    else:
        rsel[:, core + 1] = 1.0
    pm = np.zeros((E1, 1), np.float32)
    pm[:core, 0] = 1.0
    out = {"aux_bsel": bsel, "aux_lsel": lsel, "aux_rsel": rsel,
           "aux_maskcap": build_maskcap(core), "aux_pm": pm}
    if "L1_ffn_rg" in inputs:
        out["aux_my_rg"] = np.ascontiguousarray(
            np.asarray(inputs["L1_ffn_rg"][core], np.float32))
        out["aux_my_ru"] = np.ascontiguousarray(
            np.asarray(inputs["L1_ffn_ru"][core], np.float32))
        out["aux_my_rd"] = np.ascontiguousarray(
            np.asarray(inputs["L1_ffn_rd"][core], np.float32))
    return out


def shard_rows(x):
    out = []
    for c in range(NCORE):
        bidx, h = c // 2, c % 2
        out.append(np.ascontiguousarray(
            x[bidx, h * TOK:(h + 1) * TOK, :]))
    return out


def kernel(**inputs):
    x = np.asarray(inputs["L0_attn_src"], np.float32)
    shards = shard_rows(x)

    slots = SLOTS
    nc, b = build_program(slots, fused=True)

    in_maps = []
    for c in range(NCORE):
        aux = aux_inputs(c, inputs)
        m = {"xs": shards[c]}
        for name in b.inputs:
            if name in aux:
                m[name] = aux[name]
            elif name in inputs:
                m[name] = np.ascontiguousarray(
                    np.asarray(inputs[name], np.float32))
        in_maps.append(m)
    res = run_bass_kernel_spmd(nc, in_maps, core_ids=list(range(NCORE)))
    out = np.empty((4, 8192, D), np.float32)
    for c in range(NCORE):
        bidx, h = c // 2, c % 2
        out[bidx, h * TOK:(h + 1) * TOK, :] = res.results[c]["ys"]
    return out


# version 02
# speedup vs baseline: 1.0465x; 1.0465x over previous
import numpy as np
import ml_dtypes
import concourse.bass as bass
import concourse.mybir as mybir
from concourse.bass_utils import run_bass_kernel_spmd

F32 = mybir.dt.float32
BF16 = mybir.dt.bfloat16
U32 = mybir.dt.uint32
AF = mybir.ActivationFunctionType
ALU = mybir.AluOpType
AX = mybir.AxisListType

NCORE = 8
D = 1024
NT = 32
TOK = NT * 128
NH = 16
HD = 64
DFF = 4096
LN_EPS = 1e-5
ATTN_EPS = 1e-6
RMS_EPS = float(np.finfo(np.float32).eps)
E0 = 8
I0 = 512
SCALE0 = 2.5
E1 = 8
I1R = 4096
I1S = 8192
CAP = 5120
W1W = 256

NEGINF = np.float32(-1e9)
BIGCAP = np.float32(1e30)

PASS_FILTER = None
DEBUG_DUMPS = []
STEP_LIMIT = {}
SLOTS = ["l0_attn", "l0_ffn", "l1_attn", "l1_ffn"]


class Pass:

    def __init__(self, bld, name):
        self.b = bld
        self.name = name
        self.steps = []
        self.gpsimd_prelude = None

    def _add(self, engine, fn, kind, deps, group=None):
        k = len(self.steps)
        if deps is None:
            deps = [k - 1] if k > 0 else []
        deps = [d for d in deps if d is not None and d >= 0]
        self.steps.append(dict(engine=engine, fn=fn, kind=kind, deps=deps,
                               group=group))
        return k

    def eng(self, engine, fn, deps=None):
        return self._add(engine, fn, "eng", deps)

    def dma(self, engine, fn, deps=None, group=0):
        return self._add(engine, fn, "dma", deps, group)

    def coll(self, fn, deps=None):
        return self._add("gpsimd", fn, "coll", deps)

    def emit(self):
        if PASS_FILTER is not None and self.name not in PASS_FILTER:
            return
        if self.name in STEP_LIMIT:
            lim = STEP_LIMIT[self.name]
            self.steps = self.steps[:lim]
            for s in self.steps:
                s["deps"] = [d for d in s["deps"] if d < lim]
        nc = self.b.nc
        from contextlib import ExitStack
        with ExitStack() as sx:
            engines = sorted({s["engine"] for s in self.steps
                              if s["kind"] != "dma"})
            groups = sorted({s["group"] for s in self.steps
                             if s["kind"] == "dma"})
            ch = {en: sx.enter_context(
                nc.semaphore(f"{self.name}_c_{en}")) for en in engines}
            dg = {g: sx.enter_context(
                nc.semaphore(f"{self.name}_g_{g}")) for g in groups}
            blk = sx.enter_context(nc.Block())

            eord = {}
            gord = {}
            ecnt = {en: 0 for en in engines}
            gcnt = {g: 0 for g in groups}
            for k, s in enumerate(self.steps):
                if s["kind"] == "dma":
                    gord[k] = gcnt[s["group"]]
                    gcnt[s["group"]] += 1
                else:
                    eord[k] = ecnt[s["engine"]]
                    ecnt[s["engine"]] += 1

            by_eng = {}
            for k, s in enumerate(self.steps):
                by_eng.setdefault(s["engine"], []).append(k)

            def make_body(items, en):
                def body(eng):
                    if en == "gpsimd" and self.gpsimd_prelude is not None:
                        self.gpsimd_prelude(eng)
                    for k in items:
                        s = self.steps[k]
                        waits = {}
                        for d in s["deps"]:
                            sd = self.steps[d]
                            if sd["kind"] == "dma":
                                key = ("g", sd["group"])
                                val = 16 * (gord[d] + 1)
                            else:
                                if sd["engine"] == en and s["kind"] != "dma":
                                    continue
                                if sd["engine"] == en and s["kind"] == "dma":
                                    continue
                                key = ("e", sd["engine"])
                                val = eord[d] + 1
                            if waits.get(key, 0) < val:
                                waits[key] = val
                        for (kindk, ident), val in waits.items():
                            sem = dg[ident] if kindk == "g" else ch[ident]
                            eng.wait_ge(sem, val)
                        if s["kind"] == "dma":
                            ins = s["fn"](eng)
                            ins.then_inc(dg[s["group"]], 16)
                        elif s["kind"] == "coll":
                            ins = s["fn"](eng)
                            ins.then_inc(ch[en], 1)
                        else:
                            ins = s["fn"](eng)
                            assert ins is not None, \
                                f"step {k} in {self.name} returned None"
                            eng.drain().then_inc(ch[en], 1)
                return body

            for en, items in by_eng.items():
                getattr(blk, en)(make_body(items, en))

            def tail(eng):
                for g in groups:
                    eng.wait_ge(dg[g], 16 * gcnt[g])
                for en in engines:
                    if en == "sync":
                        continue
                    eng.wait_ge(ch[en], ecnt[en])
            blk.sync(tail)


class Builder:
    def __init__(self, nc):
        self.nc = nc
        self.inputs = {}
        self.consts = {}
        self.drams = {}

    def inp(self, name, shape, dtype=F32):
        if name not in self.inputs:
            self.inputs[name] = self.nc.dram_tensor(
                name, list(shape), dtype, kind="ExternalInput").ap()
        return self.inputs[name]

    def out(self, name, shape, dtype=F32):
        return self.nc.dram_tensor(name, list(shape), dtype,
                                   kind="ExternalOutput").ap()

    def dram(self, name, shape, dtype=F32):
        ap = self.nc.dram_tensor(name, list(shape), dtype,
                                 kind="Internal").ap()
        self.drams[name] = ap
        return ap

    def const(self, name, arr):
        if name not in self.consts:
            self.consts[name] = self.nc.inline_tensor(
                np.ascontiguousarray(arr), name=name).ap()
        return self.consts[name]


class Common:
    pass


def setup_common(b, ctx):
    nc = b.nc
    C = Common()
    C.ident_bf = ctx.enter_context(nc.sbuf_tensor("c_identbf", [128, 128], BF16))
    C.ident_f = ctx.enter_context(nc.sbuf_tensor("c_identf", [128, 128], F32))
    C.ones_col = ctx.enter_context(nc.sbuf_tensor("c_onescol", [1, 128], BF16))
    C.ones_t = ctx.enter_context(nc.sbuf_tensor("c_onest", [128, 1], BF16))
    C.ones_colf = ctx.enter_context(nc.sbuf_tensor("c_onescolf", [1, 128], F32))
    C.eps_ln = ctx.enter_context(nc.sbuf_tensor("c_epsln", [128, 1], F32))
    C.eps_rms = ctx.enter_context(nc.sbuf_tensor("c_epsrms", [128, 1], F32))
    idb = b.const("identbf", np.eye(128, dtype=ml_dtypes.bfloat16))
    idf = b.const("identf", np.eye(128, dtype=np.float32))

    p = Pass(b, "init")
    p.dma("sync", lambda e: e.dma_start(C.ident_bf[:], idb[:]))
    p.dma("sync", lambda e: e.dma_start(C.ident_f[:], idf[:]))
    p.eng("vector", lambda e: e.memset(C.ones_col[:], 1.0))
    p.eng("vector", lambda e: e.memset(C.ones_t[:], 1.0))
    p.eng("vector", lambda e: e.memset(C.ones_colf[:], 1.0))
    p.eng("vector", lambda e: e.memset(C.eps_ln[:], float(LN_EPS)))
    p.eng("vector", lambda e: e.memset(C.eps_rms[:], float(RMS_EPS)))
    return C, p



def tr8(p, C, src_bf, dst_bf, ps_tr, nchunk=8):
    def pe(e):
        ins = None
        for k in range(nchunk):
            ident = C.ident_bf if src_bf.dtype == BF16 else C.ident_f
            ins = e.transpose(ps_tr[:, k * 128:(k + 1) * 128],
                              src_bf[:, k * 128:(k + 1) * 128],
                              ident[:, :])
        return ins
    p.eng("tensor", pe)
    p.eng("vector", lambda e: e.tensor_copy(dst_bf[:, :nchunk * 128],
                                            ps_tr[:, :nchunk * 128]))


def ln_stats(p, x_tile, width, st, eps_ap):
    inv = 1.0 / width

    p.eng("scalar", lambda e: e.activation(st["scr"][:, :width],
                                           x_tile[:, :width], AF.Square))
    p.eng("vector", lambda e: e.reduce_sum(st["ssq"][:], st["scr"][:, :width],
                                           AX.X))
    p.eng("vector", lambda e: e.reduce_sum(st["ssum"][:], x_tile[:, :width], AX.X))
    p.eng("vector", lambda e: e.tensor_scalar(st["mu"][:], st["ssum"][:], inv,
                                              None, ALU.mult))
    p.eng("vector", lambda e: e.tensor_tensor(st["var"][:], st["mu"][:],
                                              st["mu"][:], ALU.mult))
    p.eng("vector", lambda e: e.tensor_scalar(st["var"][:], st["ssq"][:], inv,
                                              st["var"][:], ALU.mult,
                                              ALU.subtract))
    def rstd_fn(e):
        e.activation(st["var"][:], st["var"][:], AF.Ln, bias=eps_ap[:])
        return e.activation(st["rstd"][:], st["var"][:], AF.Exp, scale=-0.5)
    p.eng("scalar", rstd_fn)


def ln_apply_plain(p, x_tile, out_tile, st, width=D):
    p.eng("vector", lambda e: e.tensor_scalar(
        out_tile[:, :width], x_tile[:, :width], st["mu"][:], st["rstd"][:],
        ALU.subtract, ALU.mult))


def ln_apply_gb(p, x_tile, out_tile, st, g_bc, b_bc, tmp, width=D):
    p.eng("vector", lambda e: e.tensor_scalar(
        tmp[:, :width], x_tile[:, :width], st["mu"][:], st["rstd"][:],
        ALU.subtract, ALU.mult))
    def fin(e):
        e.tensor_tensor(tmp[:, :width], tmp[:, :width], g_bc[:, :width], ALU.mult)
        return e.tensor_tensor(out_tile[:, :width], tmp[:, :width],
                               b_bc[:, :width], ALU.add)
    p.eng("vector", fin)


def load_weight_bf16(p, src_ap, dst_sb, stage, rows, cols, cast_eng="scalar",
                     piece=1024):
    nchunk = rows // 128
    piece = min(piece, cols)
    for k in range(nchunk):
        for c0 in range(0, cols, piece):
            w = min(piece, cols - c0)
            p.dma("sync", lambda e, k=k, c0=c0, w=w: e.dma_start(
                stage[:, :w], src_ap[k * 128:(k + 1) * 128, c0:c0 + w]))
            if cast_eng == "scalar":
                p.eng("scalar", lambda e, k=k, c0=c0, w=w: e.copy(
                    dst_sb[:, k * cols + c0:k * cols + c0 + w], stage[:, :w]))
            else:
                p.eng("vector", lambda e, k=k, c0=c0, w=w: e.tensor_copy(
                    dst_sb[:, k * cols + c0:k * cols + c0 + w], stage[:, :w]))


def load_row_bf16(p, src_ap, dst_row, stage_row, n, piece=1024):
    piece = min(piece, n)
    for c0 in range(0, n, piece):
        w = min(piece, n - c0)
        p.dma("sync", lambda e, c0=c0, w=w: e.dma_start(
            stage_row[:1, :w], src_ap[None, c0:c0 + w]))
        p.eng("vector", lambda e, c0=c0, w=w: e.tensor_copy(
            dst_row[:1, c0:c0 + w], stage_row[:1, :w]))


def bcast_row(p, C, row_f32, out_bc, ps, n):
    ngr = (n + 511) // 512
    for g in range(ngr):
        w = min(512, n - g * 512)
        p.eng("tensor", lambda e, g=g, w=w: e.matmul(
            ps[:, :w], C.ones_colf[:1, :], row_f32[:1, g * 512:g * 512 + w],
            start=True, stop=True))
        p.eng("vector", lambda e, g=g, w=w: e.tensor_copy(
            out_bc[:, g * 512:g * 512 + w], ps[:, :w]))


def matmul_tm(p, C, xT, w_sb, w_cols, ps, fg0, nfg, kchunks, bias_row=None,
              start_clear=True):
    def pe(e):
        ins = None
        for d in range(kchunks):
            for i in range(nfg):
                fg = fg0 + i
                ins = e.matmul(
                    ps[:, i * 512:(i + 1) * 512],
                    xT[:, d * 128:(d + 1) * 128],
                    w_sb[:, d * w_cols + fg * 512: d * w_cols + fg * 512 + 512],
                    start=(d == 0 and start_clear),
                    stop=(d == kchunks - 1 and bias_row is None))
        if bias_row is not None:
            for i in range(nfg):
                fg = fg0 + i
                ins = e.matmul(ps[:, i * 512:(i + 1) * 512],
                               C.ones_col[:1, :],
                               bias_row[:1, fg * 512:fg * 512 + 512],
                               start=False, stop=True)
        return ins
    p.eng("tensor", pe)


def build_l0_attn(b, C, sctx, xs, ys):
    nc = b.nc
    qkv_w = b.inp("L0_attn_qkv_w", [D, 3 * D])
    qkv_b = b.inp("L0_attn_qkv_b", [3 * D])
    out_w = b.inp("L0_attn_out_w", [D, D])
    out_b = b.inp("L0_attn_out_b", [D])
    w1 = b.inp("L0_attn_w1", [D, DFF])
    b1 = b.inp("L0_attn_b1", [DFF])
    w2 = b.inp("L0_attn_w2", [DFF, D])
    b2 = b.inp("L0_attn_b2", [D])
    g1 = b.inp("L0_attn_g1", [D])
    be1 = b.inp("L0_attn_be1", [D])
    g2 = b.inp("L0_attn_g2", [D])
    be2 = b.inp("L0_attn_be2", [D])
    bsel = b.inp("aux_bsel", [128, 4])

    qf_sp = b.dram("l0a_qf", [TOK, D], BF16)
    x1_sp = b.dram("l0a_x1", [TOK, D], F32)
    xn2_sp = b.dram("l0a_xn2", [TOK, D], BF16)
    cc_in = b.dram("l0a_ccin", [4 * 65, D], F32)
    cc_out = b.dram("l0a_ccout", [4 * 65, D], F32)

    passes = []

    kv_mat = sctx.enter_context(nc.sbuf_tensor("l0a_kvmat", [128, 512], BF16))
    z_mat = sctx.enter_context(nc.sbuf_tensor("l0a_zmat", [128, 8], BF16))

    from contextlib import ExitStack
    with ExitStack() as actx:
        sb = lambda n, s, d=F32: actx.enter_context(nc.sbuf_tensor(n, s, d))
        ps = lambda n, s, d=F32: actx.enter_context(nc.psum_tensor(n, s, d))
        p = Pass(b, "l0a1")

        Wq = sb("a1_w", [128, 8 * 3072], BF16)
        stage = sb("a1_stage", [128, 1024])
        qkvb_row = sb("a1_qb", [1, 3072], BF16)
        rowstage = sb("a1_rs", [1, 1024])
        g1bc = sb("a1_g1bc", [128, D])
        be1bc = sb("a1_be1bc", [128, D])
        bselp = sb("a1_bsel", [128, 4])
        xin = sb("a1_xin", [128, D])
        xn_bf = sb("a1_xnbf", [128, D], BF16)
        xnT = sb("a1_xnT", [128, D], BF16)
        qmin = sb("a1_qmin", [128, 512])
        qexp = sb("a1_qexp", [128, 512])
        qf_sb = sb("a1_qf", [128, D], BF16)
        kf_sb = sb("a1_kf", [128, D], BF16)
        v_sb = sb("a1_v", [128, D], BF16)
        kv_f = sb("a1_kvf", [128, 512])
        z_f = sb("a1_zf", [128, 8])
        zrow = sb("a1_zrow", [1, D])
        kvs = sb("a1_kvs", [128, 512])
        zs = sb("a1_zs", [1, D])
        st = {k: sb("a1_st_" + k, [128, 1]) for k in
              ["ssq", "ssum", "mu", "var", "rstd"]}
        st["scr"] = sb("a1_scr", [128, D])
        tmp = sb("a1_tmp", [128, D])

        ps_tr = ps("a1_ptr", [128, 1024], BF16)
        ps_mm = ps("a1_pmm", [128, 2 * 512])
        ps_kv = ps("a1_pkv", [128, 512])
        ps_za = ps("a1_pza", [1, 512])
        ps_zb = ps("a1_pzb", [1, 512])
        ps_bc = ps("a1_pbc", [128, 512])

        load_weight_bf16(p, qkv_w, Wq, stage, D, 3072)
        load_row_bf16(p, qkv_b, qkvb_row, rowstage, 3072)
        grow = sb("a1_grow", [1, D])
        brow = sb("a1_brow", [1, D])
        p.dma("sync", lambda e: e.dma_start(grow[:1, :], g1[None, :]))
        p.dma("sync", lambda e: e.dma_start(brow[:1, :], be1[None, :]))
        p.dma("sync", lambda e: e.dma_start(bselp[:], bsel[:, :]))
        bcast_row(p, C, grow, g1bc, ps_bc, D)
        bcast_row(p, C, brow, be1bc, ps_bc, D)

        for j in range(NT):
            p.dma("sync", lambda e, j=j: e.dma_start(
                xin[:], xs[j * 128:(j + 1) * 128, :]))
            ln_stats(p, xin, D, st, C.eps_ln)
            ln_apply_gb(p, xin, xn_bf, st, g1bc, be1bc, tmp)
            tr8(p, C, xn_bf, xnT, ps_tr)
            for half in range(3):
                matmul_tm(p, C, xnT, Wq, 3072, ps_mm, 2 * half, 2, 8,
                          bias_row=qkvb_row)
                if half < 2:
                    dst = qf_sb if half == 0 else kf_sb
                    for gg in range(2):
                        col = gg * 512
                        p.eng("vector", lambda e, gg=gg: e.tensor_scalar(
                            qmin[:], ps_mm[:, gg * 512:(gg + 1) * 512],
                            0.0, None, ALU.min))
                        p.eng("scalar", lambda e: e.activation(
                            qexp[:], qmin[:], AF.Exp))
                        p.eng("vector", lambda e, gg=gg, dst=dst, col=col:
                              e.scalar_tensor_tensor(
                                  dst[:, col:col + 512],
                                  ps_mm[:, gg * 512:(gg + 1) * 512],
                                  0.0, qexp[:], ALU.max, ALU.add))
                else:
                    p.eng("vector", lambda e: e.tensor_copy(
                        v_sb[:], ps_mm[:]))
            def kvmm(e, j=j):
                ins = None
                for h in range(NH):
                    ins = e.matmul(
                        ps_kv[64 * (h % 2):64 * (h % 2) + 64,
                              (h // 2) * 64:(h // 2) * 64 + 64],
                        kf_sb[:, h * 64:(h + 1) * 64],
                        v_sb[:, h * 64:(h + 1) * 64],
                        start=(j == 0), stop=(j == NT - 1))
                ins = e.matmul(ps_za[:1, :], C.ones_t[:, :1],
                               kf_sb[:, 0:512], start=(j == 0),
                               stop=(j == NT - 1))
                ins = e.matmul(ps_zb[:1, :], C.ones_t[:, :1],
                               kf_sb[:, 512:1024], start=(j == 0),
                               stop=(j == NT - 1))
                return ins
            p.eng("tensor", kvmm)
            p.dma("gpsimd", lambda e, j=j: e.dma_start(
                qf_sp[j * 128:(j + 1) * 128, :], qf_sb[:]))

        p.eng("vector", lambda e: e.tensor_copy(kv_f[:], ps_kv[:]))
        def zev(e):
            e.tensor_copy(zrow[:1, 0:512], ps_za[:1, :])
            return e.tensor_copy(zrow[:1, 512:1024], ps_zb[:1, :])
        p.eng("vector", zev)
        for bb in range(4):
            p.eng("vector", lambda e, bb=bb: e.tensor_scalar(
                kvs[:], kv_f[:], bselp[:, bb:bb + 1], None, ALU.mult))
            p.eng("vector", lambda e, bb=bb: e.tensor_scalar(
                zs[:1, :], zrow[:1, :], bselp[0:1, bb:bb + 1], None, ALU.mult))
            for par in range(2):
                src = kvs[64 * par:64 * par + 64, :].rearrange(
                    "d (u e) -> d u e", e=64)
                dst = cc_in[bb * 65:bb * 65 + 64, :].rearrange(
                    "d (u two e) -> two d u e", two=2, e=64)[par]
                p.dma("gpsimd", lambda e, src=src, dst=dst: e.dma_start(dst, src))
            p.dma("gpsimd", lambda e, bb=bb: e.dma_start(
                cc_in[bb * 65 + 64:bb * 65 + 65, :], zs[:1, :]))
        p.coll(lambda e: e.collective_compute(
            "AllReduce", ALU.add, [list(range(NCORE))],
            [cc_in[:, :]], [cc_out[:, :]]))
        kvz_acc = sb("a1_kvza", [65, D])
        kvz_t = sb("a1_kvzt", [65, D])
        p.eng("vector", lambda e: e.memset(kvz_acc[:], 0.0))
        for bb in range(4):
            p.dma("sync", lambda e, bb=bb: e.dma_start(
                kvz_t[:], cc_out[bb * 65:(bb + 1) * 65, :]))
            p.eng("vector", lambda e, bb=bb: e.scalar_tensor_tensor(
                kvz_acc[:], kvz_t[:], bselp[0:65, bb:bb + 1], kvz_acc[:],
                ALU.mult, ALU.add))
        for par in range(2):
            src = kvz_acc[0:64, :].rearrange("d (u two e) -> d u two e",
                                             two=2, e=64)[:, :, par, :]
            p.eng("vector", lambda e, par=par, src=src: e.tensor_copy(
                kv_mat[64 * par:64 * par + 64, :].rearrange(
                    "d (u e) -> d u e", e=64), src))
        ps_z = ps("a1_pz", [128, 8])
        def ztr(e):
            ins = None
            for u in range(8):
                ins = e.transpose(ps_z[:, u:u + 1],
                                  kvz_acc[64:65, u * 128:(u + 1) * 128],
                                  C.ident_f[64:65, 64:65])
            return ins
        p.eng("tensor", ztr)
        p.eng("vector", lambda e: e.tensor_copy(z_mat[:], ps_z[:]))
        p.emit()
        passes.append("l0a1")

    with ExitStack() as actx:
        sb = lambda n, s, d=F32: actx.enter_context(nc.sbuf_tensor(n, s, d))
        ps = lambda n, s, d=F32: actx.enter_context(nc.psum_tensor(n, s, d))
        p = Pass(b, "l0a2")

        Wo = sb("a2_w", [128, 8 * D], BF16)
        stage = sb("a2_stage", [128, D])
        ob_row = sb("a2_ob", [1, D], BF16)
        rowstage = sb("a2_rs", [1, D])
        g2bc = sb("a2_g2bc", [128, D])
        be2bc = sb("a2_be2bc", [128, D])
        xin = sb("a2_xin", [128, D])
        qf_bf = sb("a2_qf", [128, D], BF16)
        qfT = sb("a2_qfT", [128, D], BF16)
        y_bf = sb("a2_y", [128, D], BF16)
        yT = sb("a2_yT", [128, D], BF16)
        x1 = sb("a2_x1", [128, D])
        xn2_bf = sb("a2_xn2", [128, D], BF16)
        recip = sb("a2_recip", [128, 16])
        den_t = sb("a2_dent", [128, 16])
        st = {k: sb("a2_st_" + k, [128, 1]) for k in
              ["ssq", "ssum", "mu", "var", "rstd"]}
        st["scr"] = sb("a2_scr", [128, D])
        tmp = sb("a2_tmp", [128, D])

        ps_tr = ps("a2_ptr", [128, 1024], BF16)
        ps_yn = ps("a2_pyn", [128, 1024])
        ps_yd = ps("a2_pyd", [128, 16])
        ps_o = ps("a2_po", [128, 2 * 512])
        ps_bc = ps("a2_pbc", [128, 512])

        load_weight_bf16(p, out_w, Wo, stage, D, D)
        load_row_bf16(p, out_b, ob_row, rowstage, D)
        grow = sb("a2_grow", [1, D])
        brow = sb("a2_brow", [1, D])
        p.dma("sync", lambda e: e.dma_start(grow[:1, :], g2[None, :]))
        p.dma("sync", lambda e: e.dma_start(brow[:1, :], be2[None, :]))
        bcast_row(p, C, grow, g2bc, ps_bc, D)
        bcast_row(p, C, brow, be2bc, ps_bc, D)

        for j in range(NT):
            p.dma("sync", lambda e, j=j: e.dma_start(
                qf_bf[:], qf_sp[j * 128:(j + 1) * 128, :]))
            p.dma("sync", lambda e, j=j: e.dma_start(
                xin[:], xs[j * 128:(j + 1) * 128, :]))
            tr8(p, C, qf_bf, qfT, ps_tr)
            def ymm(par):
                def fn(e):
                    ins = None
                    for h in range(par, NH, 2):
                        lhs = qfT[64 * par:64 * par + 64,
                                  (h // 2) * 128:(h // 2) * 128 + 128]
                        rhs = kv_mat[64 * par:64 * par + 64,
                                     (h // 2) * 64:(h // 2) * 64 + 64]
                        ins = e.matmul(ps_yn[:, h * 64:(h + 1) * 64], lhs,
                                       rhs, start=True, stop=True)
                        rz = z_mat[64 * par:64 * par + 64,
                                   (h // 2):(h // 2) + 1]
                        ins = e.matmul(ps_yd[:, h:h + 1], lhs, rz,
                                       start=True, stop=True)
                    return ins
                return fn
            p.eng("tensor", ymm(0))
            p.eng("tensor", ymm(1))
            p.eng("vector", lambda e: e.tensor_scalar(
                den_t[:], ps_yd[:], float(ATTN_EPS), None, ALU.max))
            p.eng("vector", lambda e: e.reciprocal(recip[:], den_t[:]))
            def ydiv(e):
                ins = None
                for h in range(NH):
                    ins = e.tensor_scalar(
                        y_bf[:, h * 64:(h + 1) * 64],
                        ps_yn[:, h * 64:(h + 1) * 64],
                        recip[:, h:h + 1], None, ALU.mult)
                return ins
            p.eng("vector", ydiv)
            tr8(p, C, y_bf, yT, ps_tr)
            matmul_tm(p, C, yT, Wo, D, ps_o, 0, 2, 8, bias_row=ob_row)
            def resid(e):
                e.tensor_tensor(x1[:, 0:512], ps_o[:, 0:512], xin[:, 0:512],
                                ALU.add)
                return e.tensor_tensor(x1[:, 512:1024], ps_o[:, 512:1024],
                                       xin[:, 512:1024], ALU.add)
            p.eng("vector", resid)
            ln_stats(p, x1, D, st, C.eps_ln)
            ln_apply_gb(p, x1, xn2_bf, st, g2bc, be2bc, tmp)
            p.dma("gpsimd", lambda e, j=j: e.dma_start(
                x1_sp[j * 128:(j + 1) * 128, :], x1[:]))
            p.dma("gpsimd", lambda e, j=j: e.dma_start(
                xn2_sp[j * 128:(j + 1) * 128, :], xn2_bf[:]))
        p.emit()
        passes.append("l0a2")

    with ExitStack() as actx:
        sb = lambda n, s, d=F32: actx.enter_context(nc.sbuf_tensor(n, s, d))
        ps = lambda n, s, d=F32: actx.enter_context(nc.psum_tensor(n, s, d))
        p = Pass(b, "l0a3")

        W1s = sb("a3_w1", [128, 8 * DFF], BF16)
        W2s = sb("a3_w2", [128, 32 * D], BF16)
        stage = sb("a3_stage", [128, 1024])
        b1_row = sb("a3_b1", [1, DFF], BF16)
        b2_row = sb("a3_b2", [1, D], BF16)
        rowstage = sb("a3_rs", [1, 1024])
        xin = sb("a3_xin", [128, D])
        x1t = sb("a3_x1", [128, D])
        xn2 = sb("a3_xn2", [128, D], BF16)
        xn2T = sb("a3_xn2T", [128, D], BF16)
        h_bf = sb("a3_h", [128, DFF], BF16)
        hT = sb("a3_hT", [128, DFF], BF16)
        arg = sb("a3_arg", [128, D])
        outt = sb("a3_out", [128, D])
        st = {k: sb("a3_st_" + k, [128, 1]) for k in
              ["ssq", "ssum", "mu", "var", "rstd"]}
        st["scr"] = sb("a3_scr", [128, D])

        ps_tr = ps("a3_ptr", [128, 1024], BF16)
        ps_h = ps("a3_ph", [128, 2 * 512])
        ps_y = ps("a3_py", [128, 2 * 512])

        load_weight_bf16(p, w1, W1s, stage, D, DFF)
        load_weight_bf16(p, w2, W2s, stage, DFF, D)
        load_row_bf16(p, b1, b1_row, rowstage, DFF)
        load_row_bf16(p, b2, b2_row, rowstage, D)

        for j in range(NT):
            p.dma("sync", lambda e, j=j: e.dma_start(
                xn2[:], xn2_sp[j * 128:(j + 1) * 128, :]))
            p.dma("sync", lambda e, j=j: e.dma_start(
                x1t[:], x1_sp[j * 128:(j + 1) * 128, :]))
            p.dma("sync", lambda e, j=j: e.dma_start(
                xin[:], xs[j * 128:(j + 1) * 128, :]))
            tr8(p, C, xn2, xn2T, ps_tr)
            for half in range(4):
                matmul_tm(p, C, xn2T, W1s, DFF, ps_h, 2 * half, 2, 8,
                          bias_row=b1_row)
                p.eng("scalar", lambda e, half=half: e.activation(
                    h_bf[:, half * 1024:(half + 1) * 1024], ps_h[:],
                    AF.Gelu))
            for q in range(4):
                tr8(p, C, h_bf[:, q * 1024:(q + 1) * 1024],
                    hT[:, q * 1024:(q + 1) * 1024], ps_tr)
            matmul_tm(p, C, hT, W2s, D, ps_y, 0, 2, 32, bias_row=b2_row)
            def res(e):
                e.tensor_tensor(arg[:, 0:512], ps_y[:, 0:512], x1t[:, 0:512],
                                ALU.add)
                e.tensor_tensor(arg[:, 512:1024], ps_y[:, 512:1024],
                                x1t[:, 512:1024], ALU.add)
                return e.tensor_tensor(arg[:], arg[:], xin[:], ALU.add)
            p.eng("vector", res)
            ln_stats(p, arg, D, st, C.eps_ln)
            ln_apply_plain(p, arg, outt, st)
            p.dma("gpsimd", lambda e, j=j: e.dma_start(
                ys[j * 128:(j + 1) * 128, :], outt[:]))
        p.emit()
        passes.append("l0a3")

    return passes


def build_l0_ffn(b, C, sctx, xs, ys):
    from contextlib import ExitStack
    nc = b.nc
    gate_w = b.inp("L0_ffn_gate_w", [E0, D])
    cb_in = b.inp("L0_ffn_correction_bias", [E0])
    Wg = b.inp("L0_ffn_Wg", [E0, D, I0])
    Wu = b.inp("L0_ffn_Wu", [E0, D, I0])
    Wd = b.inp("L0_ffn_Wd", [E0, I0, D])
    Wg_s = b.inp("L0_ffn_Wg_s", [D, I0])
    Wu_s = b.inp("L0_ffn_Wu_s", [D, I0])
    Wd_s = b.inp("L0_ffn_Wd_s", [I0, D])

    xT_sp = b.dram("l0f_xT", [TOK, D], BF16)
    racc = b.dram("l0f_racc", [TOK, D], F32)

    cw_em = sctx.enter_context(nc.sbuf_tensor("l0f_cw", [128, E0 * NT], F32))

    with ExitStack() as actx:
        sb = lambda n, s, d=F32: actx.enter_context(nc.sbuf_tensor(n, s, d))
        ps = lambda n, s, d=F32: actx.enter_context(nc.psum_tensor(n, s, d))
        p = Pass(b, "l0f0")

        gwT = sb("f0_gwT", [128, 8 * E0])
        gstage = sb("f0_gstage", [8, D])
        xTf = sb("f0_xTf", [128, D])
        cbrow = sb("f0_cbrow", [1, E0])
        cb_bc = sb("f0_cbbc", [128, E0])
        xin = sb("f0_xin", [128, D])
        x_bf = sb("f0_xbf", [128, D], BF16)
        xT = sb("f0_xT", [128, D], BF16)
        lgs_em = sb("f0_lgs", [128, E0 * NT])
        sc_em = sb("f0_sc", [128, E0 * NT])
        gs_em = sb("f0_gs", [128, 4 * NT])
        cnt4 = sb("f0_cnt4", [128, 4 * NT])
        tmpb = sb("f0_tmpb", [128, NT])
        smask = sb("f0_smask", [128, E0 * NT])
        sfc = sb("f0_sfc", [128, E0 * NT])
        cnt8 = sb("f0_cnt8", [128, E0 * NT])
        sel4 = sb("f0_sel4", [128, E0 * NT])
        w4 = sb("f0_w4", [128, E0 * NT])
        wsum = sb("f0_wsum", [128, NT])
        wfac = sb("f0_wfac", [128, NT])

        ps_tr = ps("f0_ptr", [128, 1024], BF16)
        ps_trf = ps("f0_ptrf", [128, 1024])
        ps_l = ps("f0_pl", [128, E0])
        ps_g8 = ps("f0_pg8", [128, 8 * E0])
        ps_bc = ps("f0_pbc", [128, 512])

        p.dma("sync", lambda e: e.dma_start(gstage[:, :], gate_w[:, :]))
        def gtr(e):
            ins = None
            for d in range(8):
                ins = e.transpose(ps_g8[:, d * E0:(d + 1) * E0],
                                  gstage[:, d * 128:(d + 1) * 128],
                                  C.ident_f[0:8, 0:8])
            return ins
        p.eng("tensor", gtr)
        p.eng("vector", lambda e: e.tensor_copy(gwT[:, :], ps_g8[:, :]))
        p.dma("sync", lambda e: e.dma_start(cbrow[:1, :], cb_in[None, :]))
        p.eng("tensor", lambda e: e.matmul(ps_bc[:, :E0], C.ones_colf[:1, :],
                                           cbrow[:1, :], start=True, stop=True))
        p.eng("vector", lambda e: e.tensor_copy(cb_bc[:, :], ps_bc[:, :E0]))

        for j in range(NT):
            p.dma("sync", lambda e, j=j: e.dma_start(
                xin[:], xs[j * 128:(j + 1) * 128, :]))
            p.eng("vector", lambda e: e.tensor_copy(x_bf[:], xin[:]))
            tr8(p, C, x_bf, xT, ps_tr)
            p.dma("gpsimd", lambda e, j=j: e.dma_start(
                xT_sp[j * 128:(j + 1) * 128, :], xT[:]))
            tr8(p, C, xin, xTf, ps_trf)
            def lmm(e):
                ins = None
                for d in range(8):
                    ins = e.matmul(ps_l[:, :], xTf[:, d * 128:(d + 1) * 128],
                                   gwT[:, d * E0:(d + 1) * E0],
                                   start=(d == 0), stop=(d == 7))
                return ins
            p.eng("tensor", lmm)
            p.eng("vector", lambda e, j=j: e.tensor_copy(
                lgs_em[:, j::NT], ps_l[:, :]))

        blk = lambda t, e0: t[:, e0 * NT:(e0 + 1) * NT]
        p.eng("scalar", lambda e: e.activation(sc_em[:, :], lgs_em[:, :],
                                               AF.Sigmoid))
        def addcb(e):
            ins = None
            for e0 in range(E0):
                ins = e.tensor_scalar(blk(sc_em, e0), blk(sc_em, e0),
                                      cb_bc[:, e0:e0 + 1], None, ALU.add)
            return ins
        p.eng("vector", addcb)
        def gsum(e):
            ins = None
            for g in range(4):
                ins = e.tensor_tensor(blk(gs_em, g), blk(sc_em, 2 * g),
                                      blk(sc_em, 2 * g + 1), ALU.add)
            return ins
        p.eng("vector", gsum)
        p.eng("vector", lambda e: e.memset(cnt4[:, :], 0.0))
        def grank(e):
            ins = None
            for g in range(4):
                for gp in range(4):
                    if gp == g:
                        continue
                    op = ALU.is_ge if gp < g else ALU.is_gt
                    e.tensor_tensor(tmpb[:, :], blk(gs_em, gp), blk(gs_em, g),
                                    op)
                    ins = e.tensor_tensor(blk(cnt4, g), blk(cnt4, g),
                                          tmpb[:, :], ALU.add)
            return ins
        p.eng("vector", grank)
        def gmask(e):
            e.tensor_single_scalar(cnt4[:, :], cnt4[:, :], 2.0, ALU.is_lt)
            ins = None
            for e0 in range(E0):
                ins = e.tensor_copy(blk(smask, e0), blk(cnt4, e0 // 2))
            return ins
        p.eng("vector", gmask)
        p.eng("vector", lambda e: e.tensor_tensor(sfc[:, :], smask[:, :],
                                                  sc_em[:, :], ALU.mult))
        p.eng("vector", lambda e: e.memset(cnt8[:, :], 0.0))
        for e0 in range(E0):
            def erank(e, e0=e0):
                ins = None
                for ep in range(E0):
                    if ep == e0:
                        continue
                    op = ALU.is_ge if ep < e0 else ALU.is_gt
                    e.tensor_tensor(tmpb[:, :], blk(sfc, ep), blk(sfc, e0), op)
                    ins = e.tensor_tensor(blk(cnt8, e0), blk(cnt8, e0),
                                          tmpb[:, :], ALU.add)
                return ins
            p.eng("vector", erank)
        def finw(e):
            e.tensor_single_scalar(sel4[:, :], cnt8[:, :], 4.0, ALU.is_lt)
            return sel4
        p.eng("vector", lambda e: e.tensor_single_scalar(
            sel4[:, :], cnt8[:, :], 4.0, ALU.is_lt))
        def subcb(e):
            ins = None
            for e0 in range(E0):
                ins = e.tensor_scalar(blk(sc_em, e0), blk(sc_em, e0),
                                      cb_bc[:, e0:e0 + 1], None, ALU.subtract)
            return ins
        p.eng("vector", subcb)
        p.eng("vector", lambda e: e.tensor_tensor(w4[:, :], sel4[:, :],
                                                  sc_em[:, :], ALU.mult))
        def wsumf(e):
            e.tensor_tensor(wsum[:, :], blk(w4, 0), blk(w4, 1), ALU.add)
            for e0 in range(2, E0):
                e.tensor_tensor(wsum[:, :], wsum[:, :], blk(w4, e0), ALU.add)
            return e.tensor_single_scalar(wsum[:, :], wsum[:, :], 1e-20,
                                          ALU.add)
        p.eng("vector", wsumf)
        p.eng("vector", lambda e: e.reciprocal(wfac[:, :], wsum[:, :]))
        p.eng("vector", lambda e: e.tensor_single_scalar(
            wfac[:, :], wfac[:, :], float(SCALE0), ALU.mult))
        def cwf(e):
            ins = None
            for e0 in range(E0):
                ins = e.tensor_tensor(blk(cw_em, e0), blk(w4, e0),
                                      wfac[:, :], ALU.mult)
            return ins
        p.eng("vector", cwf)
        p.emit()

    with ExitStack() as actx:
        sb = lambda n, s, d=F32: actx.enter_context(nc.sbuf_tensor(n, s, d))
        ps = lambda n, s, d=F32: actx.enter_context(nc.psum_tensor(n, s, d))
        p = Pass(b, "l0f1")

        Wgs = sb("f1_wg", [128, 8 * I0], BF16)
        Wus = sb("f1_wu", [128, 8 * I0], BF16)
        Wds = sb("f1_wd", [128, 4 * D], BF16)
        stage = sb("f1_stage", [128, 1024])
        xT = sb("f1_xT", [128, D], BF16)
        gs_bf = sb("f1_gs", [128, I0], BF16)
        h_bf = sb("f1_h", [128, I0], BF16)
        hT = sb("f1_hT", [128, I0], BF16)
        ysb = sb("f1_ysb", [128, D])
        ps_g = ps("f1_pg", [128, I0])
        ps_u = ps("f1_pu", [128, I0])
        ps_tr = ps("f1_ptr", [128, 512], BF16)
        ps_y = ps("f1_py", [128, 2 * 512])

        for ex in range(E0 + 1):
            shared = ex == E0
            wg_src = Wg_s if shared else Wg[ex]
            wu_src = Wu_s if shared else Wu[ex]
            wd_src = Wd_s if shared else Wd[ex]
            load_weight_bf16(p, wg_src, Wgs, stage, D, I0)
            load_weight_bf16(p, wu_src, Wus, stage, D, I0)
            load_weight_bf16(p, wd_src, Wds, stage, I0, D)
            for j in range(NT):
                p.dma("sync", lambda e, j=j: e.dma_start(
                    xT[:], xT_sp[j * 128:(j + 1) * 128, :]))
                def gumm(e):
                    ins = None
                    for d in range(8):
                        xTd = xT[:, d * 128:(d + 1) * 128]
                        ins = e.matmul(ps_g[:, :], xTd,
                                       Wgs[:, d * I0:(d + 1) * I0],
                                       start=(d == 0), stop=(d == 7))
                        ins = e.matmul(ps_u[:, :], xTd,
                                       Wus[:, d * I0:(d + 1) * I0],
                                       start=(d == 0), stop=(d == 7))
                    return ins
                p.eng("tensor", gumm)
                p.eng("scalar", lambda e: e.activation(gs_bf[:], ps_g[:],
                                                       AF.Silu))
                if shared:
                    p.eng("vector", lambda e: e.tensor_tensor(
                        h_bf[:], ps_u[:], gs_bf[:], ALU.mult))
                else:
                    p.eng("vector", lambda e, ex=ex, j=j:
                          e.scalar_tensor_tensor(
                              h_bf[:], ps_u[:], cw_em[:, ex * NT + j:
                                                      ex * NT + j + 1],
                              gs_bf[:], ALU.mult, ALU.mult))
                tr8(p, C, h_bf, hT, ps_tr, nchunk=4)
                def ymm(e):
                    ins = None
                    for fg in range(2):
                        for i in range(4):
                            ins = e.matmul(
                                ps_y[:, fg * 512:(fg + 1) * 512],
                                hT[:, i * 128:(i + 1) * 128],
                                Wds[:, i * D + fg * 512:i * D + fg * 512 + 512],
                                start=(i == 0), stop=(i == 3))
                    return ins
                p.eng("tensor", ymm)
                p.eng("vector", lambda e: e.tensor_copy(ysb[:], ps_y[:]))
                if ex == 0:
                    p.dma("gpsimd", lambda e, j=j: e.dma_start(
                        racc[j * 128:(j + 1) * 128, :], ysb[:]))
                else:
                    p.dma("gpsimd", lambda e, j=j: e.dma_start(
                        racc[j * 128:(j + 1) * 128, :], ysb[:],
                        accum_op=ALU.add))
        p.emit()

    with ExitStack() as actx:
        sb = lambda n, s, d=F32: actx.enter_context(nc.sbuf_tensor(n, s, d))
        p = Pass(b, "l0f2")
        xin = sb("f2_xin", [128, D])
        rin = sb("f2_rin", [128, D])
        arg = sb("f2_arg", [128, D])
        outt = sb("f2_out", [128, D])
        st = {k: sb("f2_st_" + k, [128, 1]) for k in
              ["ssq", "ssum", "mu", "var", "rstd"]}
        st["scr"] = sb("f2_scr", [128, D])
        for j in range(NT):
            p.dma("sync", lambda e, j=j: e.dma_start(
                xin[:], xs[j * 128:(j + 1) * 128, :]))
            p.dma("sync", lambda e, j=j: e.dma_start(
                rin[:], racc[j * 128:(j + 1) * 128, :]))
            p.eng("vector", lambda e: e.tensor_tensor(arg[:], xin[:], rin[:],
                                                      ALU.add))
            ln_stats(p, arg, D, st, C.eps_ln)
            ln_apply_plain(p, arg, outt, st)
            p.dma("gpsimd", lambda e, j=j: e.dma_start(
                ys[j * 128:(j + 1) * 128, :], outt[:]))
        p.emit()
    return ["l0f0", "l0f1", "l0f2"]


def build_l1_attn(b, C, sctx, xs, ys):
    from contextlib import ExitStack
    nc = b.nc
    ln1_g = b.inp("L1_attn_ln1_g", [D])
    ln1_b = b.inp("L1_attn_ln1_b", [D])
    ln2_g = b.inp("L1_attn_ln2_g", [D])
    ln2_b = b.inp("L1_attn_ln2_b", [D])
    W1 = b.inp("L1_attn_W1", [D, DFF])
    b1 = b.inp("L1_attn_b1", [DFF])
    W2 = b.inp("L1_attn_W2", [DFF, D])
    b2 = b.inp("L1_attn_b2", [D])
    lsel = b.inp("aux_lsel", [128, NCORE])
    rsel = b.inp("aux_rsel", [128, NCORE])
    maskin = b.inp("aux_maskcap", [6 * 128, 768])

    hal_in = b.dram("l1a_halin", [512, D], BF16)
    b.dram("l1a_attn", [TOK, D], F32)
    hal_out = b.dram("l1a_halout", [NCORE * 512, D], BF16)
    x1_sp = b.dram("l1a_x1", [TOK, D], F32)

    with ExitStack() as actx:
        sb = lambda n, s, d=F32: actx.enter_context(nc.sbuf_tensor(n, s, d))
        ps = lambda n, s, d=F32: actx.enter_context(nc.psum_tensor(n, s, d))
        p = Pass(b, "l1a1")

        mask_sb = sb("c1_mask", [128, 6 * 768])
        selp = sb("c1_selp", [128, 2 * NCORE])
        xin = sb("c1_xin", [128, D])
        halo_x = [sb(f"c1_hx{i}", [128, D], BF16) for i in range(4)]
        halo_xT = [sb(f"c1_hxT{i}", [128, D], BF16) for i in range(4)]
        ring_x = [sb(f"c1_rx{i}", [128, D], BF16) for i in range(8)]
        ring_xT = [sb(f"c1_rxT{i}", [128, D], BF16) for i in range(8)]
        accf = sb("c1_accf", [128, D])
        tmpbf = sb("c1_tmpbf", [128, D], BF16)
        s_sb = sb("c1_ssb", [128, 768])
        negmax = sb("c1_negmax", [128, 1])
        den = sb("c1_den", [128, 1])
        recip = sb("c1_recip", [128, 1])
        p_bf = sb("c1_pbf", [128, 768], BF16)
        pT = sb("c1_pT", [128, 768], BF16)
        attn_t = sb("c1_attn", [128, D])
        t1 = sb("c1_t1", [128, D])
        x1 = sb("c1_x1", [128, D])
        g1bc = sb("c1_g1bc", [128, D])
        b1bc = sb("c1_b1bc", [128, D])
        grow = sb("c1_grow", [1, D])
        tmp = sb("c1_tmp", [128, D])
        st = {k: sb("c1_st_" + k, [128, 1]) for k in
              ["ssq", "ssum", "mu", "var", "rstd"]}
        st["scr"] = sb("c1_scr", [128, D])

        ps_tr = ps("c1_ptr", [128, 1024], BF16)
        ps_s = ps("c1_ps", [128, 768])
        ps_o = ps("c1_po", [128, 64])
        ps_bc = ps("c1_pbc", [128, 512])

        for i in range(6):
            p.dma("sync", lambda e, i=i: e.dma_start(
                mask_sb[:, i * 768:(i + 1) * 768],
                maskin[i * 128:(i + 1) * 128, :]))
        p.dma("sync", lambda e: e.dma_start(selp[:, 0:NCORE], lsel[:, :]))
        p.dma("sync", lambda e: e.dma_start(selp[:, NCORE:], rsel[:, :]))
        p.dma("sync", lambda e: e.dma_start(grow[:1, :], ln1_g[None, :]))
        bcast_row(p, C, grow, g1bc, ps_bc, D)
        p.dma("sync", lambda e: e.dma_start(grow[:1, :], ln1_b[None, :]))
        bcast_row(p, C, grow, b1bc, ps_bc, D)

        for i, j in enumerate([0, 1, 30, 31]):
            p.dma("sync", lambda e, j=j: e.dma_start(
                xin[:], xs[j * 128:(j + 1) * 128, :]))
            p.eng("vector", lambda e: e.tensor_copy(tmpbf[:], xin[:]))
            p.dma("gpsimd", lambda e, i=i: e.dma_start(
                hal_in[i * 128:(i + 1) * 128, :], tmpbf[:]))
        p.coll(lambda e: e.collective_compute(
            "AllGather", ALU.bypass, [list(range(NCORE))],
            [hal_in[:, :]], [hal_out[:, :]]))
        for side in range(2):
            for r in range(2):
                p.eng("vector", lambda e: e.memset(accf[:], 0.0))
                for cc in range(NCORE):
                    src = hal_out[cc * 512 + (1 - side) * 256 + r * 128:
                                  cc * 512 + (1 - side) * 256 + r * 128 + 128, :]
                    p.dma("sync", lambda e, src=src: e.dma_start(
                        tmpbf[:], src))
                    p.eng("vector", lambda e, cc=cc, side=side:
                          e.scalar_tensor_tensor(
                              accf[:], tmpbf[:],
                              selp[:, side * NCORE + cc:side * NCORE + cc + 1],
                              accf[:], ALU.mult, ALU.add))
                idx = side * 2 + r
                p.eng("vector", lambda e, idx=idx: e.tensor_copy(
                    halo_x[idx][:], accf[:]))
            for r in range(2):
                idx = side * 2 + r
                tr8(p, C, halo_x[idx], halo_xT[idx], ps_tr)

        def wtile(w):
            if w < 0:
                return halo_x[w + 2], halo_xT[w + 2]
            if w >= NT:
                return halo_x[w - NT + 2], halo_xT[w - NT + 2]
            return ring_x[w % 8], ring_xT[w % 8]

        def materialize(j):
            p.dma("sync", lambda e, j=j: e.dma_start(
                xin[:], xs[j * 128:(j + 1) * 128, :]))
            p.eng("vector", lambda e, j=j: e.tensor_copy(
                ring_x[j % 8][:], xin[:]))
            tr8(p, C, ring_x[j % 8], ring_xT[j % 8], ps_tr)

        for j in range(4):
            materialize(j)

        for j in range(NT):
            if j + 4 < NT:
                materialize(j + 4)
            ws = j - 2 - (j % 2)
            if j <= 1:
                mslot = j
            elif j >= 30:
                mslot = j - 26
            else:
                mslot = 2 + (j % 2)
            mask_ap = mask_sb[:, mslot * 768:(mslot + 1) * 768]
            for h in range(NH):
                par = h % 2
                hi = h // 2
                def smm(e, h=h, par=par, hi=hi, ws=ws, j=j):
                    ins = None
                    for wi in range(6):
                        _, xT_w = wtile(ws + wi)
                        ins = e.matmul(
                            ps_s[:, wi * 128:(wi + 1) * 128],
                            wtile(j)[1][64 * par:64 * par + 64,
                                        hi * 128:hi * 128 + 128],
                            xT_w[64 * par:64 * par + 64,
                                 hi * 128:hi * 128 + 128],
                            start=True, stop=True)
                    return ins
                p.eng("tensor", smm)
                p.eng("vector", lambda e, mask_ap=mask_ap: e.tensor_tensor(
                    s_sb[:], ps_s[:], mask_ap, ALU.min))
                p.eng("vector", lambda e: e.reduce_max(
                    negmax[:], s_sb[:], AX.X, negate=True))
                p.eng("scalar", lambda e: e.activation(
                    p_bf[:], s_sb[:], AF.Exp, bias=negmax[:],
                    accum_out=den[:]))
                p.eng("vector", lambda e: e.reciprocal(recip[:], den[:]))
                def ptr(e):
                    ins = None
                    for wi in range(6):
                        ins = e.transpose(
                            ps_tr[:, wi * 128:(wi + 1) * 128],
                            p_bf[:, wi * 128:(wi + 1) * 128],
                            C.ident_bf[:, :])
                    return ins
                p.eng("tensor", ptr)
                p.eng("vector", lambda e: e.tensor_copy(
                    pT[:], ps_tr[:, 0:768]))
                def omm(e, h=h, ws=ws):
                    ins = None
                    for wi in range(6):
                        x_w, _ = wtile(ws + wi)
                        ins = e.matmul(
                            ps_o[:, :], pT[:, wi * 128:(wi + 1) * 128],
                            x_w[:, h * 64:(h + 1) * 64],
                            start=(wi == 0), stop=(wi == 5))
                    return ins
                p.eng("tensor", omm)
                p.eng("vector", lambda e, h=h: e.tensor_scalar(
                    attn_t[:, h * 64:(h + 1) * 64], ps_o[:, :], recip[:],
                    None, ALU.mult))
            if DEBUG_DUMPS:
                p.dma("gpsimd", lambda e, j=j: e.dma_start(
                    b.drams["l1a_attn"][j * 128:(j + 1) * 128, :], attn_t[:]))
            p.dma("sync", lambda e, j=j: e.dma_start(
                xin[:], xs[j * 128:(j + 1) * 128, :]))
            p.eng("vector", lambda e: e.tensor_tensor(t1[:], attn_t[:],
                                                      xin[:], ALU.add))
            ln_stats(p, t1, D, st, C.eps_ln)
            ln_apply_gb(p, t1, x1, st, g1bc, b1bc, tmp)
            p.dma("gpsimd", lambda e, j=j: e.dma_start(
                x1_sp[j * 128:(j + 1) * 128, :], x1[:]))
        p.emit()

    with ExitStack() as actx:
        sb = lambda n, s, d=F32: actx.enter_context(nc.sbuf_tensor(n, s, d))
        ps = lambda n, s, d=F32: actx.enter_context(nc.psum_tensor(n, s, d))
        p = Pass(b, "l1a2")

        W1s = sb("c2_w1", [128, 8 * DFF], BF16)
        W2s = sb("c2_w2", [128, 32 * D], BF16)
        stage = sb("c2_stage", [128, 1024])
        b1_row = sb("c2_b1", [1, DFF], BF16)
        b2_row = sb("c2_b2", [1, D], BF16)
        rowstage = sb("c2_rs", [1, 1024])
        g2bc = sb("c2_g2bc", [128, D])
        b2bc = sb("c2_b2bc", [128, D])
        grow = rowstage
        xin = sb("c2_xin", [128, D])
        x1t = sb("c2_x1", [128, D])
        x1bf = sb("c2_x1bf", [128, D], BF16)
        x1T = sb("c2_x1T", [128, D], BF16)
        h_bf = sb("c2_h", [128, DFF], BF16)
        hT = sb("c2_hT", [128, DFF], BF16)
        t2 = sb("c2_t2", [128, D])
        so = sb("c2_so", [128, D])
        arg = sb("c2_arg", [128, D])
        outt = sb("c2_out", [128, D])
        st = {k: sb("c2_st_" + k, [128, 1]) for k in
              ["ssq", "ssum", "mu", "var", "rstd"]}
        st["scr"] = sb("c2_scr", [128, D])
        tmp = st["scr"]

        ps_tr = ps("c2_ptr", [128, 1024], BF16)
        ps_h = ps("c2_ph", [128, 2 * 512])
        ps_y = ps("c2_py", [128, 2 * 512])
        ps_bc = ps("c2_pbc", [128, 512])

        load_weight_bf16(p, W1, W1s, stage, D, DFF)
        load_weight_bf16(p, W2, W2s, stage, DFF, D)
        load_row_bf16(p, b1, b1_row, rowstage, DFF)
        load_row_bf16(p, b2, b2_row, rowstage, D)
        p.dma("sync", lambda e: e.dma_start(grow[:1, :], ln2_g[None, :]))
        bcast_row(p, C, grow, g2bc, ps_bc, D)
        p.dma("sync", lambda e: e.dma_start(grow[:1, :], ln2_b[None, :]))
        bcast_row(p, C, grow, b2bc, ps_bc, D)

        for j in range(NT):
            p.dma("sync", lambda e, j=j: e.dma_start(
                x1t[:], x1_sp[j * 128:(j + 1) * 128, :]))
            p.eng("vector", lambda e: e.tensor_copy(x1bf[:], x1t[:]))
            tr8(p, C, x1bf, x1T, ps_tr)
            for half in range(4):
                matmul_tm(p, C, x1T, W1s, DFF, ps_h, 2 * half, 2, 8,
                          bias_row=b1_row)
                p.eng("scalar", lambda e, half=half: e.activation(
                    h_bf[:, half * 1024:(half + 1) * 1024], ps_h[:],
                    AF.Relu))
            for q in range(4):
                tr8(p, C, h_bf[:, q * 1024:(q + 1) * 1024],
                    hT[:, q * 1024:(q + 1) * 1024], ps_tr)
            matmul_tm(p, C, hT, W2s, D, ps_y, 0, 2, 32, bias_row=b2_row)
            def res(e):
                e.tensor_tensor(t2[:, 0:512], ps_y[:, 0:512], x1t[:, 0:512],
                                ALU.add)
                return e.tensor_tensor(t2[:, 512:1024], ps_y[:, 512:1024],
                                       x1t[:, 512:1024], ALU.add)
            p.eng("vector", res)
            ln_stats(p, t2, D, st, C.eps_ln)
            ln_apply_gb(p, t2, so, st, g2bc, b2bc, tmp)
            p.dma("sync", lambda e, j=j: e.dma_start(
                xin[:], xs[j * 128:(j + 1) * 128, :]))
            p.eng("vector", lambda e: e.tensor_tensor(arg[:], so[:], xin[:],
                                                      ALU.add))
            ln_stats(p, arg, D, st, C.eps_ln)
            ln_apply_plain(p, arg, outt, st)
            p.dma("gpsimd", lambda e, j=j: e.dma_start(
                ys[j * 128:(j + 1) * 128, :], outt[:]))
        p.emit()
    return ["l1a1", "l1a2"]


def build_l1_ffn(b, C, sctx, xs, ys):
    from contextlib import ExitStack
    nc = b.nc
    router_w = b.inp("L1_ffn_router_w", [E1, D])
    my_rg = b.inp("aux_my_rg", [D, I1R])
    my_ru = b.inp("aux_my_ru", [D, I1R])
    my_rd = b.inp("aux_my_rd", [I1R, D])
    sg = b.inp("L1_ffn_sg", [D, I1S])
    su = b.inp("L1_ffn_su", [D, I1S])
    sd = b.inp("L1_ffn_sd", [I1S, D])
    pm_in = b.inp("aux_pm", [E1, 1])

    xT_sp = b.dram("l1f_xT", [TOK, D], BF16)
    xbf_sp = b.dram("l1f_xbf", [TOK, D], BF16)
    g_sp = b.dram("l1f_g", [TOK, I1S], BF16)
    h_sp = b.dram("l1f_h", [TOK, I1S], BF16)
    hR_sp = b.dram("l1f_hR", [CAP, I1R], BF16)
    comp_d = b.dram("l1f_comp", [E1 * CAP, D], BF16)
    mycomp = b.dram("l1f_mycomp", [CAP, D], BF16)
    contrib = b.dram("l1f_contrib", [CAP, D], BF16)
    ag_contrib = b.dram("l1f_agc", [E1 * CAP, D], BF16)
    cnt_d = b.dram("l1f_cnt", [E1, 1], F32)
    cnt_ag = b.dram("l1f_cntag", [NCORE * E1, 1], F32)

    weff_em = sctx.enter_context(nc.sbuf_tensor("l1f_weff", [128, E1 * NT], F32))
    idx_em = sctx.enter_context(nc.sbuf_tensor("l1f_idx", [128, E1 * NT], U32))
    idx2_em = sctx.enter_context(nc.sbuf_tensor("l1f_idx2", [128, E1 * NT], U32))

    EM = lambda t, e0: t[:, e0 * NT:(e0 + 1) * NT]

    with ExitStack() as actx:
        sb = lambda n, s, d=F32: actx.enter_context(nc.sbuf_tensor(n, s, d))
        ps = lambda n, s, d=F32: actx.enter_context(nc.psum_tensor(n, s, d))
        p = Pass(b, "l1f0")
        rwT = sb("r0_rwT", [128, 8 * E1])
        rstage = sb("r0_rstage", [E1, D])
        xin = sb("r0_xin", [128, D])
        x_bf = sb("r0_xbf", [128, D], BF16)
        xT = sb("r0_xT", [128, D], BF16)
        xnf = sb("r0_xnf", [128, D])
        xnT = sb("r0_xnT", [128, D])
        lgs_tm = sb("r0_lgs", [128, NT * E1])
        st = {k: sb("r0_st_" + k, [128, 1]) for k in ["ssq", "rstd"]}
        scr = sb("r0_scr", [128, D])
        ps_tr = ps("r0_ptr", [128, 1024], BF16)
        ps_trf = ps("r0_ptrf", [128, 1024])
        ps_r = ps("r0_pr", [128, E1])
        ps_g8 = ps("r0_pg8", [128, 8 * E1])

        p.dma("sync", lambda e: e.dma_start(rstage[:, :], router_w[:, :]))
        def rtr(e):
            ins = None
            for d in range(8):
                ins = e.transpose(ps_g8[:, d * E1:(d + 1) * E1],
                                  rstage[:, d * 128:(d + 1) * 128],
                                  C.ident_f[0:E1, 0:E1])
            return ins
        p.eng("tensor", rtr)
        p.eng("vector", lambda e: e.tensor_copy(rwT[:, :], ps_g8[:, :]))

        for j in range(NT):
            p.dma("sync", lambda e, j=j: e.dma_start(
                xin[:], xs[j * 128:(j + 1) * 128, :]))
            p.eng("vector", lambda e: e.tensor_copy(x_bf[:], xin[:]))
            tr8(p, C, x_bf, xT, ps_tr)
            p.dma("gpsimd", lambda e, j=j: e.dma_start(
                xbf_sp[j * 128:(j + 1) * 128, :], x_bf[:]))
            p.dma("gpsimd", lambda e, j=j: e.dma_start(
                xT_sp[j * 128:(j + 1) * 128, :], xT[:]))
            p.eng("scalar", lambda e: e.activation(scr[:], xin[:], AF.Square))
            p.eng("vector", lambda e: e.reduce_sum(st["ssq"][:], scr[:], AX.X))
            def rstd_fn(e):
                e.activation(st["rstd"][:], st["ssq"][:], AF.Ln,
                             bias=C.eps_rms[:], scale=1.0 / D)
                return e.activation(st["rstd"][:], st["rstd"][:], AF.Exp,
                                    scale=-0.5)
            p.eng("scalar", rstd_fn)
            p.eng("vector", lambda e: e.tensor_scalar(
                xnf[:], xin[:], st["rstd"][:], None, ALU.mult))
            tr8(p, C, xnf, xnT, ps_trf)
            def rmm(e):
                ins = None
                for d in range(8):
                    ins = e.matmul(ps_r[:, :], xnT[:, d * 128:(d + 1) * 128],
                                   rwT[:, d * E1:(d + 1) * E1],
                                   start=(d == 0), stop=(d == 7))
                return ins
            p.eng("tensor", rmm)
            p.eng("vector", lambda e, j=j: e.tensor_copy(
                lgs_tm[:, j * E1:(j + 1) * E1], ps_r[:, :]))

        maxs = sb("r0_maxs", [128, NT])
        esub = sb("r0_esub", [128, NT * E1])
        ep_tm = sb("r0_eptm", [128, NT * E1])
        ep_em = sb("r0_epem", [128, E1 * NT])
        cnt_em = sb("r0_cnt", [128, E1 * NT])
        sel_em = sb("r0_sel", [128, E1 * NT])
        wsel = sb("r0_wsel", [128, E1 * NT])
        wsum = sb("r0_wsum", [128, NT])
        recipw = sb("r0_recipw", [128, NT])
        rw_em = sb("r0_rw", [128, E1 * NT])
        tmpb = sb("r0_tmpb", [128, NT])
        colsums = sb("r0_cols", [128, E1])
        cnt8 = sb("r0_cnt8", [E1, 1])
        offs_row = sb("r0_offs", [1, E1])
        offs_bc = sb("r0_offsbc", [128, E1])
        cntl = sb("r0_cntl", [E1, E1])
        pm_sb = sb("r0_pm", [E1, 1])
        selT = sb("r0_selT", [32, 128])
        zeros32 = sb("r0_z32", [32, 128])
        rsT = sb("r0_rsT", [32, 128])
        totrow = sb("r0_totrow", [1, 32])
        inclrow = sb("r0_incl", [1, 32])
        exclcol = sb("r0_excl", [32, 1])
        rank_b = sb("r0_rankb", [128, NT])
        grank_b = sb("r0_grankb", [128, NT])
        valid_b = sb("r0_validb", [128, NT])
        sv_b = sb("r0_svb", [128, NT])
        inv_b = sb("r0_invb", [128, NT])
        idxf_b = sb("r0_idxfb", [128, NT])
        ps_sm = ps("r0_psm", [128, 512])
        ps_t32 = ps_sm[0:32, 0:128]
        ps_t128 = ps_sm[:, 0:32]
        ps_r1 = ps_sm[0:1, 128:160]
        ps_c1 = ps_sm[0:32, 160:161]
        ps_off = ps_sm[0:1, 168:168 + E1]
        ps_obc = ps_sm[:, 176:176 + E1]

        TM = lambda t, e0: t[:, e0::E1]
        p.eng("vector", lambda e: e.reduce_max(
            maxs[:], lgs_tm[:, :].rearrange("p (j e) -> p j e", e=E1), AX.X))
        def esubf(e):
            ins = None
            for e0 in range(E1):
                ins = e.tensor_tensor(TM(esub, e0), TM(lgs_tm, e0), maxs[:],
                                      ALU.subtract)
            return ins
        p.eng("vector", esubf)
        p.eng("scalar", lambda e: e.activation(ep_tm[:, :], esub[:, :], AF.Exp))
        p.eng("vector", lambda e: e.memset(cnt_em[:, :], 0.0))
        for e0 in range(E1):
            def erank(e, e0=e0):
                ins = None
                for ep in range(E1):
                    if ep == e0:
                        continue
                    op = ALU.is_ge if ep < e0 else ALU.is_gt
                    e.tensor_tensor(tmpb[:, :], TM(ep_tm, ep), TM(ep_tm, e0),
                                    op)
                    ins = e.tensor_tensor(EM(cnt_em, e0), EM(cnt_em, e0),
                                          tmpb[:, :], ALU.add)
                return ins
            p.eng("vector", erank)
        p.eng("vector", lambda e: e.tensor_single_scalar(
            sel_em[:, :], cnt_em[:, :], 2.0, ALU.is_lt))
        def epem(e):
            ins = None
            for e0 in range(E1):
                ins = e.tensor_copy(EM(ep_em, e0), TM(ep_tm, e0))
            return ins
        p.eng("vector", epem)
        p.eng("vector", lambda e: e.tensor_tensor(wsel[:, :], sel_em[:, :],
                                                  ep_em[:, :], ALU.mult))
        def wsumf(e):
            e.tensor_tensor(wsum[:, :], EM(wsel, 0), EM(wsel, 1), ALU.add)
            ins = None
            for e0 in range(2, E1):
                ins = e.tensor_tensor(wsum[:, :], wsum[:, :], EM(wsel, e0),
                                      ALU.add)
            return ins
        p.eng("vector", wsumf)
        p.eng("vector", lambda e: e.reciprocal(recipw[:, :], wsum[:, :]))
        def rwf(e):
            ins = None
            for e0 in range(E1):
                ins = e.tensor_tensor(EM(rw_em, e0), EM(wsel, e0),
                                      recipw[:, :], ALU.mult)
            return ins
        p.eng("vector", rwf)
        def colsf(e):
            ins = None
            for e0 in range(E1):
                ins = e.reduce_sum(colsums[:, e0:e0 + 1], EM(sel_em, e0),
                                   AX.X)
            return ins
        p.eng("vector", colsf)
        p.eng("tensor", lambda e: e.transpose(
            ps_t32[0:E1, :], colsums[:, :], C.ident_f[:, :]))
        p.eng("vector", lambda e: e.reduce_sum(cnt8[:, :], ps_t32[0:E1, :],
                                               AX.X))
        p.dma("gpsimd", lambda e: e.dma_start(cnt_d[:, :], cnt8[:, :]))
        p.coll(lambda e: e.collective_compute(
            "AllGather", ALU.bypass, [list(range(NCORE))],
            [cnt_d[:, :]], [cnt_ag[:, :]]))
        p.dma("sync", lambda e: e.dma_start(
            cntl[:, :], cnt_ag.rearrange("(c e) one -> c (e one)", e=E1)))
        p.dma("sync", lambda e: e.dma_start(pm_sb[:, :], pm_in[:, :]))
        p.eng("tensor", lambda e: e.matmul(ps_off[:, :], pm_sb[:, :],
                                           cntl[:, :], start=True, stop=True))
        p.eng("vector", lambda e: e.tensor_copy(offs_row[:, :], ps_off[:, :]))
        p.eng("tensor", lambda e: e.matmul(ps_obc[:, :], C.ones_colf[:1, :],
                                           offs_row[:1, :], start=True,
                                           stop=True))
        p.eng("vector", lambda e: e.tensor_copy(offs_bc[:, :], ps_obc[:, :]))
        p.eng("vector", lambda e: e.memset(zeros32[:, :], 0.0))
        for e0 in range(E1):
            p.eng("tensor", lambda e, e0=e0: e.transpose(
                ps_t32[:, :], EM(sel_em, e0), C.ident_f[:, :]))
            p.eng("vector", lambda e: e.tensor_copy(selT[:, :], ps_t32[:, :]))
            p.eng("vector", lambda e: e.tensor_tensor_scan(
                rsT[:, :], zeros32[:, :], selT[:, :], 0.0, ALU.add, ALU.add))
            p.eng("tensor", lambda e: e.transpose(
                ps_r1[:, :], rsT[:, 127:128], C.ident_f[0:32, 0:32]))
            p.eng("vector", lambda e: e.tensor_copy(totrow[:, :], ps_r1[:, :]))
            p.eng("vector", lambda e: e.tensor_tensor_scan(
                inclrow[:, :], zeros32[0:1, 0:32], totrow[:, :], 0.0,
                ALU.add, ALU.add))
            p.eng("vector", lambda e: e.tensor_tensor(
                inclrow[:, :], inclrow[:, :], totrow[:, :], ALU.subtract))
            p.eng("tensor", lambda e: e.transpose(
                ps_c1[:, :], inclrow[:, :], C.ident_f[0:1, 0:1]))
            p.eng("vector", lambda e: e.tensor_copy(exclcol[:, :],
                                                    ps_c1[:, :]))
            p.eng("vector", lambda e: e.tensor_scalar(
                rsT[:, :], rsT[:, :], exclcol[:, :], None, ALU.add))
            p.eng("tensor", lambda e: e.transpose(
                ps_t128[:, :], rsT[:, :], C.ident_f[0:32, 0:32]))
            p.eng("vector", lambda e: e.tensor_copy(rank_b[:, :],
                                                    ps_t128[:, :]))
            p.eng("vector", lambda e, e0=e0: e.tensor_tensor(
                grank_b[:, :], rank_b[:, :], EM(sel_em, e0), ALU.subtract))
            p.eng("vector", lambda e, e0=e0: e.tensor_scalar(
                grank_b[:, :], grank_b[:, :], offs_bc[:, e0:e0 + 1], None,
                ALU.add))
            p.eng("vector", lambda e: e.tensor_single_scalar(
                valid_b[:, :], grank_b[:, :], float(CAP), ALU.is_lt))
            p.eng("vector", lambda e, e0=e0: e.tensor_tensor(
                sv_b[:, :], EM(sel_em, e0), valid_b[:, :], ALU.mult))
            p.eng("vector", lambda e, e0=e0: e.tensor_tensor(
                EM(weff_em, e0), EM(rw_em, e0), sv_b[:, :], ALU.mult))
            p.eng("vector", lambda e: e.tensor_tensor(
                idxf_b[:, :], grank_b[:, :], sv_b[:, :], ALU.mult))
            p.eng("vector", lambda e: e.tensor_scalar(
                inv_b[:, :], sv_b[:, :], -1e9, 1e9, ALU.mult, ALU.add))
            p.eng("vector", lambda e: e.tensor_tensor(
                idxf_b[:, :], idxf_b[:, :], inv_b[:, :], ALU.add))
            p.eng("vector", lambda e, e0=e0: e.tensor_copy(
                EM(idx_em, e0), idxf_b[:, :]))
            p.eng("vector", lambda e, e0=e0: e.tensor_scalar(
                idxf_b[:, :], idxf_b[:, :], float(e0 * CAP), None, ALU.add))
            p.eng("vector", lambda e, e0=e0: e.tensor_copy(
                EM(idx2_em, e0), idxf_b[:, :]))
        p.emit()

    with ExitStack() as actx:
        sb = lambda n, s, d=F32: actx.enter_context(nc.sbuf_tensor(n, s, d))
        ps = lambda n, s, d=F32: actx.enter_context(nc.psum_tensor(n, s, d))
        p = Pass(b, "l1f2a")
        bcreg = {}
        def prelude(g):
            r = g.alloc_register("bcreg2a")
            g.reg_mov(r, E1 * CAP - 1)
            bcreg["r"] = r
        p.gpsimd_prelude = prelude
        zz = sb("r2_zz", [128, 4096], BF16)
        xbt = sb("r2_xbt", [128, D], BF16)
        Rg = sb("r2_rg", [128, 8 * I1R], BF16)
        Ru = sb("r2_ru", [128, 8 * I1R], BF16)
        stage = sb("r2_stage", [128, 1024])
        xgT = sb("r2_xgT", [128, D], BF16)
        gs_bf = sb("r2_gs", [128, 512], BF16)
        hT_t = sb("r2_hT", [128, I1R], BF16)
        ps_tr = ps("r2_ptr", [128, 1024], BF16)
        ps_g = ps("r2_pg", [128, 512])
        ps_u = ps("r2_pu", [128, 512])

        p.eng("vector", lambda e: e.memset(zz[:, :], 0.0))
        for n in range(0, E1 * CAP // 128, 4):
            p.dma("gpsimd", lambda e, n=n: e.dma_start(
                comp_d.rearrange("(m p) d -> p m d", p=128)[:, n:n + 4, :],
                zz[:, :].rearrange("p (m d) -> p m d", d=D)))
        for j in range(NT):
            p.dma("sync", lambda e, j=j: e.dma_start(
                xbt[:], xbf_sp[j * 128:(j + 1) * 128, :]))
            for e0 in range(E1):
                p.dma("gpsimd", lambda e, j=j, e0=e0: e.indirect_dma_start(
                    comp_d[:, :],
                    bass.IndirectOffsetOnAxis(
                        idx2_em[:, e0 * NT + j:e0 * NT + j + 1], 0),
                    xbt[:, :], None,
                    bounds_check=bcreg["r"], oob_is_err=False))
        p.coll(lambda e: e.collective_compute(
            "ReduceScatter", ALU.add, [list(range(NCORE))],
            [comp_d[:, :]], [mycomp[:, :]]))
        load_weight_bf16(p, my_rg, Rg, stage, D, I1R)
        load_weight_bf16(p, my_ru, Ru, stage, D, I1R)
        for t in range(CAP // 128):
            p.dma("sync", lambda e, t=t: e.dma_start(
                xbt[:], mycomp[t * 128:(t + 1) * 128, :]))
            tr8(p, C, xbt, xgT, ps_tr)
            for fg in range(8):
                def gumm(e, fg=fg):
                    ins = None
                    for d in range(8):
                        xTd = xgT[:, d * 128:(d + 1) * 128]
                        ins = e.matmul(ps_g[:, :], xTd,
                                       Rg[:, d * I1R + fg * 512:
                                          d * I1R + fg * 512 + 512],
                                       start=(d == 0), stop=(d == 7))
                        ins = e.matmul(ps_u[:, :], xTd,
                                       Ru[:, d * I1R + fg * 512:
                                          d * I1R + fg * 512 + 512],
                                       start=(d == 0), stop=(d == 7))
                    return ins
                p.eng("tensor", gumm)
                p.eng("scalar", lambda e: e.activation(gs_bf[:], ps_g[:],
                                                       AF.Silu))
                p.eng("vector", lambda e: e.tensor_tensor(
                    gs_bf[:, :], ps_u[:], gs_bf[:], ALU.mult))
                def htr(e, fg=fg):
                    ins = None
                    for k in range(4):
                        ins = e.transpose(
                            ps_tr[:, k * 128:(k + 1) * 128],
                            gs_bf[:, k * 128:(k + 1) * 128],
                            C.ident_bf[:, :])
                    return ins
                p.eng("tensor", htr)
                p.eng("vector", lambda e, fg=fg: e.tensor_copy(
                    hT_t[:, fg * 512:(fg + 1) * 512], ps_tr[:, 0:512]))
            p.dma("gpsimd", lambda e, t=t: e.dma_start(
                hR_sp[t * 128:(t + 1) * 128, :], hT_t[:]))
        p.emit()

    with ExitStack() as actx:
        sb = lambda n, s, d=F32: actx.enter_context(nc.sbuf_tensor(n, s, d))
        ps = lambda n, s, d=F32: actx.enter_context(nc.psum_tensor(n, s, d))
        p = Pass(b, "l1f2b")
        Rd = sb("r2b_rd", [128, 32 * D], BF16)
        stage = sb("r2b_stage", [128, 1024])
        hT_t = sb("r2b_hT", [128, I1R], BF16)
        ctile = sb("r2b_ct", [128, D], BF16)
        ps_y = ps("r2b_py", [128, 2 * 512])
        load_weight_bf16(p, my_rd, Rd, stage, I1R, D)
        for t in range(CAP // 128):
            p.dma("sync", lambda e, t=t: e.dma_start(
                hT_t[:], hR_sp[t * 128:(t + 1) * 128, :]))
            def ymm(e):
                ins = None
                for i in range(32):
                    for fg in range(2):
                        ins = e.matmul(
                            ps_y[:, fg * 512:(fg + 1) * 512],
                            hT_t[:, i * 128:(i + 1) * 128],
                            Rd[:, i * D + fg * 512:i * D + fg * 512 + 512],
                            start=(i == 0), stop=(i == 31))
                return ins
            p.eng("tensor", ymm)
            p.eng("vector", lambda e: e.tensor_copy(ctile[:], ps_y[:]))
            p.dma("gpsimd", lambda e, t=t: e.dma_start(
                contrib[t * 128:(t + 1) * 128, :], ctile[:]))
        p.coll(lambda e: e.collective_compute(
            "AllGather", ALU.bypass, [list(range(NCORE))],
            [contrib[:, :]], [ag_contrib[:, :]]))
        p.emit()

    with ExitStack() as actx:
        sb = lambda n, s, d=F32: actx.enter_context(nc.sbuf_tensor(n, s, d))
        ps = lambda n, s, d=F32: actx.enter_context(nc.psum_tensor(n, s, d))
        p = Pass(b, "l1f3")
        Sg = sb("r3_sg", [128, 8 * I1S], BF16)
        stage = sb("r3_stage", [128, 1024])
        xT = sb("r3_xT", [128, D], BF16)
        g_t = sb("r3_gt", [128, I1S], BF16)
        ps_g = ps("r3_pg", [128, 512])
        load_weight_bf16(p, sg, Sg, stage, D, I1S)
        for j in range(NT):
            p.dma("sync", lambda e, j=j: e.dma_start(
                xT[:], xT_sp[j * 128:(j + 1) * 128, :]))
            for fg in range(16):
                def gmm(e, fg=fg):
                    ins = None
                    for d in range(8):
                        ins = e.matmul(ps_g[:, :],
                                       xT[:, d * 128:(d + 1) * 128],
                                       Sg[:, d * I1S + fg * 512:
                                          d * I1S + fg * 512 + 512],
                                       start=(d == 0), stop=(d == 7))
                    return ins
                p.eng("tensor", gmm)
                p.eng("scalar", lambda e, fg=fg: e.activation(
                    g_t[:, fg * 512:(fg + 1) * 512], ps_g[:], AF.Silu))
            p.dma("gpsimd", lambda e, j=j: e.dma_start(
                g_sp[j * 128:(j + 1) * 128, :], g_t[:]))
        p.emit()

    with ExitStack() as actx:
        sb = lambda n, s, d=F32: actx.enter_context(nc.sbuf_tensor(n, s, d))
        ps = lambda n, s, d=F32: actx.enter_context(nc.psum_tensor(n, s, d))
        p = Pass(b, "l1f4")
        Su = sb("r4_su", [128, 8 * I1S], BF16)
        stage = sb("r4_stage", [128, 1024])
        xT = sb("r4_xT", [128, D], BF16)
        g_t = sb("r4_gt", [128, I1S], BF16)
        h_bf = sb("r4_h", [128, 512], BF16)
        hT_t = sb("r4_hT", [128, I1S], BF16)
        ps_u = ps("r4_pu", [128, 512])
        ps_tr = ps("r4_ptr", [128, 512], BF16)
        load_weight_bf16(p, su, Su, stage, D, I1S)
        for j in range(NT):
            p.dma("sync", lambda e, j=j: e.dma_start(
                xT[:], xT_sp[j * 128:(j + 1) * 128, :]))
            p.dma("sync", lambda e, j=j: e.dma_start(
                g_t[:], g_sp[j * 128:(j + 1) * 128, :]))
            for fg in range(16):
                def umm(e, fg=fg):
                    ins = None
                    for d in range(8):
                        ins = e.matmul(ps_u[:, :],
                                       xT[:, d * 128:(d + 1) * 128],
                                       Su[:, d * I1S + fg * 512:
                                          d * I1S + fg * 512 + 512],
                                       start=(d == 0), stop=(d == 7))
                    return ins
                p.eng("tensor", umm)
                p.eng("vector", lambda e, fg=fg: e.tensor_tensor(
                    h_bf[:, :], ps_u[:], g_t[:, fg * 512:(fg + 1) * 512],
                    ALU.mult))
                def htr(e):
                    ins = None
                    for k in range(4):
                        ins = e.transpose(ps_tr[:, k * 128:(k + 1) * 128],
                                          h_bf[:, k * 128:(k + 1) * 128],
                                          C.ident_bf[:, :])
                    return ins
                p.eng("tensor", htr)
                p.eng("vector", lambda e, fg=fg: e.tensor_copy(
                    hT_t[:, fg * 512:(fg + 1) * 512], ps_tr[:, 0:512]))
            p.dma("gpsimd", lambda e, j=j: e.dma_start(
                h_sp[j * 128:(j + 1) * 128, :], hT_t[:]))
        p.emit()

    with ExitStack() as actx:
        sb = lambda n, s, d=F32: actx.enter_context(nc.sbuf_tensor(n, s, d))
        ps = lambda n, s, d=F32: actx.enter_context(nc.psum_tensor(n, s, d))
        p = Pass(b, "l1f5")
        bcreg = {}
        def prelude(g):
            r = g.alloc_register("bcreg5")
            g.reg_mov(r, E1 * CAP - 1)
            bcreg["r"] = r
        p.gpsimd_prelude = prelude
        Sd = sb("r5_sd", [128, 64 * D], BF16)
        stage = sb("r5_stage", [128, 1024])
        hT_t = sb("r5_hT", [128, I1S], BF16)
        gath = sb("r5_gath", [128, D], BF16)
        acc = sb("r5_acc", [128, D])
        xin = sb("r5_xin", [128, D])
        arg = sb("r5_arg", [128, D])
        outt = sb("r5_out", [128, D])
        st = {k: sb("r5_st_" + k, [128, 1]) for k in
              ["ssq", "ssum", "mu", "var", "rstd"]}
        st["scr"] = sb("r5_scr", [128, D])
        ps_y = ps("r5_py", [128, 2 * 512])
        load_weight_bf16(p, sd, Sd, stage, I1S, D)
        p.eng("vector", lambda e: e.memset(gath[:, :], 0.0))
        for j in range(NT):
            p.dma("sync", lambda e, j=j: e.dma_start(
                hT_t[:], h_sp[j * 128:(j + 1) * 128, :]))
            def ymm(e):
                ins = None
                for i in range(64):
                    for fg in range(2):
                        ins = e.matmul(
                            ps_y[:, fg * 512:(fg + 1) * 512],
                            hT_t[:, i * 128:(i + 1) * 128],
                            Sd[:, i * D + fg * 512:i * D + fg * 512 + 512],
                            start=(i == 0), stop=(i == 63))
                return ins
            p.eng("tensor", ymm)
            p.eng("vector", lambda e: e.tensor_copy(acc[:], ps_y[:]))
            for e0 in range(E1):
                p.dma("gpsimd", lambda e, j=j, e0=e0: e.indirect_dma_start(
                    gath[:, :], None, ag_contrib[:, :],
                    bass.IndirectOffsetOnAxis(
                        idx2_em[:, e0 * NT + j:e0 * NT + j + 1], 0),
                    bounds_check=bcreg["r"], oob_is_err=False))
                p.eng("vector", lambda e, j=j, e0=e0: e.scalar_tensor_tensor(
                    acc[:, :], gath[:, :],
                    weff_em[:, e0 * NT + j:e0 * NT + j + 1], acc[:, :],
                    ALU.mult, ALU.add))
            p.dma("sync", lambda e, j=j: e.dma_start(
                xin[:], xs[j * 128:(j + 1) * 128, :]))
            p.eng("vector", lambda e: e.tensor_tensor(arg[:], acc[:], xin[:],
                                                      ALU.add))
            ln_stats(p, arg, D, st, C.eps_ln)
            ln_apply_plain(p, arg, outt, st)
            p.dma("gpsimd", lambda e, j=j: e.dma_start(
                ys[j * 128:(j + 1) * 128, :], outt[:]))
        p.emit()
    return ["l1f0", "l1f2a", "l1f2b", "l1f3", "l1f4", "l1f5"]


def build_program(slots, fused):
    from contextlib import ExitStack
    nc = bass.Bass("TRN2", target_bir_lowering=False)
    b = Builder(nc)
    xs_ext = b.inp("xs", [TOK, D], F32)
    ys_ext = b.out("ys", [TOK, D], F32)
    bufs = [xs_ext]
    for i in range(len(slots) - 1):
        bufs.append(b.dram(f"stream{i}", [TOK, D], F32))
    bufs.append(ys_ext)

    with ExitStack() as sctx:
        C, initp = setup_common(b, sctx)
        initp.emit()
        for i, s in enumerate(slots):
            src, dst = bufs[i], bufs[i + 1]
            if s == "l0_attn":
                build_l0_attn(b, C, sctx, src, dst)
            elif s == "l0_ffn":
                build_l0_ffn(b, C, sctx, src, dst)
            elif s == "l1_attn":
                build_l1_attn(b, C, sctx, src, dst)
            elif s == "l1_ffn":
                build_l1_ffn(b, C, sctx, src, dst)
            else:
                raise ValueError(s)
        if DEBUG_DUMPS:
            p = Pass(b, "dbgdump")
            for name in DEBUG_DUMPS:
                ap = b.drams[name]
                o = b.out("dump_" + name, ap.shape, ap.dtype)
                p.dma("gpsimd", lambda e, o=o, ap=ap: e.dma_start(o[:, :],
                                                                 ap[:, :]))
            p.steps.append(("sync", lambda e: e.nop(), "eng"))
            p.emit()
    return nc, b


def build_maskcap(core):
    out = np.empty((6, 128, 768), np.float32)
    col = np.arange(768)[None, :]
    pp = np.arange(128)[:, None]
    for s, j in enumerate([0, 1, 2, 3, 30, 31]):
        pj = j % 2
        rel = (col // 128 - 2 - pj) * 128 + (col % 128) - pp
        valid = np.abs(rel) <= W1W
        if j <= 1 and core % 2 == 0:
            valid = valid & (col >= 256)
        if j >= 30 and core % 2 == 1:
            valid = valid & (col < 512)
        out[s] = np.where(valid, 1e30, -1e9)
    return np.ascontiguousarray(out.reshape(6 * 128, 768).astype(np.float32))


def aux_inputs(core, inputs):
    bsel = np.zeros((128, 4), np.float32)
    bsel[:, core // 2] = 1.0
    lsel = np.zeros((128, NCORE), np.float32)
    rsel = np.zeros((128, NCORE), np.float32)
    if core % 2 == 1:
        lsel[:, core - 1] = 1.0
    else:
        rsel[:, core + 1] = 1.0
    pm = np.zeros((E1, 1), np.float32)
    pm[:core, 0] = 1.0
    out = {"aux_bsel": bsel, "aux_lsel": lsel, "aux_rsel": rsel,
           "aux_maskcap": build_maskcap(core), "aux_pm": pm}
    if "L1_ffn_rg" in inputs:
        out["aux_my_rg"] = np.ascontiguousarray(
            np.asarray(inputs["L1_ffn_rg"][core], np.float32))
        out["aux_my_ru"] = np.ascontiguousarray(
            np.asarray(inputs["L1_ffn_ru"][core], np.float32))
        out["aux_my_rd"] = np.ascontiguousarray(
            np.asarray(inputs["L1_ffn_rd"][core], np.float32))
    return out


def shard_rows(x):
    out = []
    for c in range(NCORE):
        bidx, h = c // 2, c % 2
        out.append(np.ascontiguousarray(
            x[bidx, h * TOK:(h + 1) * TOK, :]))
    return out


def kernel(**inputs):
    x = np.asarray(inputs["L0_attn_src"], np.float32)
    shards = shard_rows(x)

    slots = SLOTS
    nc, b = build_program(slots, fused=True)

    in_maps = []
    for c in range(NCORE):
        aux = aux_inputs(c, inputs)
        m = {"xs": shards[c]}
        for name in b.inputs:
            if name in aux:
                m[name] = aux[name]
            elif name in inputs:
                m[name] = np.ascontiguousarray(
                    np.asarray(inputs[name], np.float32))
        in_maps.append(m)
    res = run_bass_kernel_spmd(nc, in_maps, core_ids=list(range(NCORE)))
    out = np.empty((4, 8192, D), np.float32)
    for c in range(NCORE):
        bidx, h = c // 2, c % 2
        out[bidx, h * TOK:(h + 1) * TOK, :] = res.results[c]["ys"]
    return out


# version 03
# speedup vs baseline: 1.1638x; 1.1121x over previous
import numpy as np
import ml_dtypes
import concourse.bass as bass
import concourse.mybir as mybir
from concourse.bass_utils import run_bass_kernel_spmd

F32 = mybir.dt.float32
BF16 = mybir.dt.bfloat16
U32 = mybir.dt.uint32
AF = mybir.ActivationFunctionType
ALU = mybir.AluOpType
AX = mybir.AxisListType

NCORE = 8
D = 1024
NT = 32
TOK = NT * 128
NH = 16
HD = 64
DFF = 4096
LN_EPS = 1e-5
ATTN_EPS = 1e-6
RMS_EPS = float(np.finfo(np.float32).eps)
E0 = 8
I0 = 512
SCALE0 = 2.5
E1 = 8
I1R = 4096
I1S = 8192
CAP = 5120
W1W = 256

NEGINF = np.float32(-1e9)
BIGCAP = np.float32(1e30)

PASS_FILTER = None
DEBUG_DUMPS = []
STEP_LIMIT = {}
SLOTS = ["l0_attn", "l0_ffn", "l1_attn", "l1_ffn"]


class Pass:

    def __init__(self, bld, name):
        self.b = bld
        self.name = name
        self.steps = []
        self.gpsimd_prelude = None
        self._last_w = {}
        self._readers = {}

    def _auto_deps(self, k, reads, writes):
        deps = set()
        for r in reads:
            if r in self._last_w:
                deps.add(self._last_w[r])
        for w in writes:
            deps.update(self._readers.get(w, ()))
            if w in self._last_w:
                deps.add(self._last_w[w])
        for r in reads:
            self._readers.setdefault(r, set()).add(k)
        for w in writes:
            self._last_w[w] = k
            self._readers[w] = set()
        return sorted(deps)

    def _add(self, engine, fn, kind, deps, group=None, rw=None):
        k = len(self.steps)
        if rw is not None:
            deps = self._auto_deps(k, rw[0], rw[1])
        elif deps is None:
            deps = [k - 1] if k > 0 else []
        deps = [d for d in deps if d is not None and 0 <= d < k]
        self.steps.append(dict(engine=engine, fn=fn, kind=kind, deps=deps,
                               group=group))
        return k

    def eng(self, engine, fn, deps=None, rw=None):
        return self._add(engine, fn, "eng", deps, rw=rw)

    def dma(self, engine, fn, deps=None, group=0, rw=None):
        return self._add(engine, fn, "dma", deps, group, rw=rw)

    def coll(self, fn, deps=None, rw=None):
        return self._add("gpsimd", fn, "coll", deps, rw=rw)

    def emit(self):
        if PASS_FILTER is not None and self.name not in PASS_FILTER:
            return
        if self.name in STEP_LIMIT:
            lim = STEP_LIMIT[self.name]
            self.steps = self.steps[:lim]
            for s in self.steps:
                s["deps"] = [d for d in s["deps"] if d < lim]
        nc = self.b.nc
        from contextlib import ExitStack
        with ExitStack() as sx:
            engines = sorted({s["engine"] for s in self.steps
                              if s["kind"] != "dma"})
            groups = sorted({s["group"] for s in self.steps
                             if s["kind"] == "dma"})
            ch = {en: sx.enter_context(
                nc.semaphore(f"{self.name}_c_{en}")) for en in engines}
            dg = {g: sx.enter_context(
                nc.semaphore(f"{self.name}_g_{g}")) for g in groups}
            blk = sx.enter_context(nc.Block())

            eord = {}
            gord = {}
            ecnt = {en: 0 for en in engines}
            gcnt = {g: 0 for g in groups}
            for k, s in enumerate(self.steps):
                if s["kind"] == "dma":
                    gord[k] = gcnt[s["group"]]
                    gcnt[s["group"]] += 1
                else:
                    eord[k] = ecnt[s["engine"]]
                    ecnt[s["engine"]] += 1

            by_eng = {}
            for k, s in enumerate(self.steps):
                by_eng.setdefault(s["engine"], []).append(k)

            def make_body(items, en):
                def body(eng):
                    if en == "gpsimd" and self.gpsimd_prelude is not None:
                        self.gpsimd_prelude(eng)
                    for k in items:
                        s = self.steps[k]
                        waits = {}
                        for d in s["deps"]:
                            sd = self.steps[d]
                            if sd["kind"] == "dma":
                                key = ("g", sd["group"])
                                val = 16 * (gord[d] + 1)
                            else:
                                if sd["engine"] == en and s["kind"] != "dma":
                                    continue
                                if sd["engine"] == en and s["kind"] == "dma":
                                    continue
                                key = ("e", sd["engine"])
                                val = eord[d] + 1
                            if waits.get(key, 0) < val:
                                waits[key] = val
                        for (kindk, ident), val in waits.items():
                            sem = dg[ident] if kindk == "g" else ch[ident]
                            eng.wait_ge(sem, val)
                        if s["kind"] == "dma":
                            ins = s["fn"](eng)
                            ins.then_inc(dg[s["group"]], 16)
                        elif s["kind"] == "coll":
                            ins = s["fn"](eng)
                            ins.then_inc(ch[en], 1)
                        else:
                            ins = s["fn"](eng)
                            assert ins is not None, \
                                f"step {k} in {self.name} returned None"
                            eng.drain().then_inc(ch[en], 1)
                return body

            for en, items in by_eng.items():
                getattr(blk, en)(make_body(items, en))

            def tail(eng):
                for g in groups:
                    eng.wait_ge(dg[g], 16 * gcnt[g])
                for en in engines:
                    if en == "sync":
                        continue
                    eng.wait_ge(ch[en], ecnt[en])
            blk.sync(tail)


class Builder:
    def __init__(self, nc):
        self.nc = nc
        self.inputs = {}
        self.consts = {}
        self.drams = {}

    def inp(self, name, shape, dtype=F32):
        if name not in self.inputs:
            self.inputs[name] = self.nc.dram_tensor(
                name, list(shape), dtype, kind="ExternalInput").ap()
        return self.inputs[name]

    def out(self, name, shape, dtype=F32):
        return self.nc.dram_tensor(name, list(shape), dtype,
                                   kind="ExternalOutput").ap()

    def dram(self, name, shape, dtype=F32):
        ap = self.nc.dram_tensor(name, list(shape), dtype,
                                 kind="Internal").ap()
        self.drams[name] = ap
        return ap

    def const(self, name, arr):
        if name not in self.consts:
            self.consts[name] = self.nc.inline_tensor(
                np.ascontiguousarray(arr), name=name).ap()
        return self.consts[name]


class Common:
    pass


def setup_common(b, ctx):
    nc = b.nc
    C = Common()
    C.ident_bf = ctx.enter_context(nc.sbuf_tensor("c_identbf", [128, 128], BF16))
    C.ident_f = ctx.enter_context(nc.sbuf_tensor("c_identf", [128, 128], F32))
    C.ones_col = ctx.enter_context(nc.sbuf_tensor("c_onescol", [1, 128], BF16))
    C.ones_t = ctx.enter_context(nc.sbuf_tensor("c_onest", [128, 1], BF16))
    C.ones_colf = ctx.enter_context(nc.sbuf_tensor("c_onescolf", [1, 128], F32))
    C.eps_ln = ctx.enter_context(nc.sbuf_tensor("c_epsln", [128, 1], F32))
    C.eps_rms = ctx.enter_context(nc.sbuf_tensor("c_epsrms", [128, 1], F32))
    idb = b.const("identbf", np.eye(128, dtype=ml_dtypes.bfloat16))
    idf = b.const("identf", np.eye(128, dtype=np.float32))

    p = Pass(b, "init")
    p.dma("sync", lambda e: e.dma_start(C.ident_bf[:], idb[:]))
    p.dma("sync", lambda e: e.dma_start(C.ident_f[:], idf[:]))
    p.eng("vector", lambda e: e.memset(C.ones_col[:], 1.0))
    p.eng("vector", lambda e: e.memset(C.ones_t[:], 1.0))
    p.eng("vector", lambda e: e.memset(C.ones_colf[:], 1.0))
    p.eng("vector", lambda e: e.memset(C.eps_ln[:], float(LN_EPS)))
    p.eng("vector", lambda e: e.memset(C.eps_rms[:], float(RMS_EPS)))
    return C, p



def tr8(p, C, src_bf, dst_bf, ps_tr, nchunk=8):
    def pe(e):
        ins = None
        for k in range(nchunk):
            ident = C.ident_bf if src_bf.dtype == BF16 else C.ident_f
            ins = e.transpose(ps_tr[:, k * 128:(k + 1) * 128],
                              src_bf[:, k * 128:(k + 1) * 128],
                              ident[:, :])
        return ins
    p.eng("tensor", pe)
    p.eng("vector", lambda e: e.tensor_copy(dst_bf[:, :nchunk * 128],
                                            ps_tr[:, :nchunk * 128]))


def ln_stats(p, x_tile, width, st, eps_ap):
    inv = 1.0 / width

    p.eng("scalar", lambda e: e.activation(st["scr"][:, :width],
                                           x_tile[:, :width], AF.Square))
    p.eng("vector", lambda e: e.reduce_sum(st["ssq"][:], st["scr"][:, :width],
                                           AX.X))
    p.eng("vector", lambda e: e.reduce_sum(st["ssum"][:], x_tile[:, :width], AX.X))
    p.eng("vector", lambda e: e.tensor_scalar(st["mu"][:], st["ssum"][:], inv,
                                              None, ALU.mult))
    p.eng("vector", lambda e: e.tensor_tensor(st["var"][:], st["mu"][:],
                                              st["mu"][:], ALU.mult))
    p.eng("vector", lambda e: e.tensor_scalar(st["var"][:], st["ssq"][:], inv,
                                              st["var"][:], ALU.mult,
                                              ALU.subtract))
    def rstd_fn(e):
        e.activation(st["var"][:], st["var"][:], AF.Ln, bias=eps_ap[:])
        return e.activation(st["rstd"][:], st["var"][:], AF.Exp, scale=-0.5)
    p.eng("scalar", rstd_fn)


def ln_apply_plain(p, x_tile, out_tile, st, width=D):
    p.eng("vector", lambda e: e.tensor_scalar(
        out_tile[:, :width], x_tile[:, :width], st["mu"][:], st["rstd"][:],
        ALU.subtract, ALU.mult))


def ln_apply_gb(p, x_tile, out_tile, st, g_bc, b_bc, tmp, width=D):
    p.eng("vector", lambda e: e.tensor_scalar(
        tmp[:, :width], x_tile[:, :width], st["mu"][:], st["rstd"][:],
        ALU.subtract, ALU.mult))
    def fin(e):
        e.tensor_tensor(tmp[:, :width], tmp[:, :width], g_bc[:, :width], ALU.mult)
        return e.tensor_tensor(out_tile[:, :width], tmp[:, :width],
                               b_bc[:, :width], ALU.add)
    p.eng("vector", fin)


def load_weight_bf16(p, src_ap, dst_sb, stage, rows, cols, cast_eng="scalar",
                     piece=1024):
    nchunk = rows // 128
    piece = min(piece, cols)
    for k in range(nchunk):
        for c0 in range(0, cols, piece):
            w = min(piece, cols - c0)
            p.dma("sync", lambda e, k=k, c0=c0, w=w: e.dma_start(
                stage[:, :w], src_ap[k * 128:(k + 1) * 128, c0:c0 + w]))
            if cast_eng == "scalar":
                p.eng("scalar", lambda e, k=k, c0=c0, w=w: e.copy(
                    dst_sb[:, k * cols + c0:k * cols + c0 + w], stage[:, :w]))
            else:
                p.eng("vector", lambda e, k=k, c0=c0, w=w: e.tensor_copy(
                    dst_sb[:, k * cols + c0:k * cols + c0 + w], stage[:, :w]))


def load_row_bf16(p, src_ap, dst_row, stage_row, n, piece=1024):
    piece = min(piece, n)
    for c0 in range(0, n, piece):
        w = min(piece, n - c0)
        p.dma("sync", lambda e, c0=c0, w=w: e.dma_start(
            stage_row[:1, :w], src_ap[None, c0:c0 + w]))
        p.eng("vector", lambda e, c0=c0, w=w: e.tensor_copy(
            dst_row[:1, c0:c0 + w], stage_row[:1, :w]))


def bcast_row(p, C, row_f32, out_bc, ps, n):
    ngr = (n + 511) // 512
    for g in range(ngr):
        w = min(512, n - g * 512)
        p.eng("tensor", lambda e, g=g, w=w: e.matmul(
            ps[:, :w], C.ones_colf[:1, :], row_f32[:1, g * 512:g * 512 + w],
            start=True, stop=True))
        p.eng("vector", lambda e, g=g, w=w: e.tensor_copy(
            out_bc[:, g * 512:g * 512 + w], ps[:, :w]))


def matmul_tm(p, C, xT, w_sb, w_cols, ps, fg0, nfg, kchunks, bias_row=None,
              start_clear=True):
    def pe(e):
        ins = None
        for d in range(kchunks):
            for i in range(nfg):
                fg = fg0 + i
                ins = e.matmul(
                    ps[:, i * 512:(i + 1) * 512],
                    xT[:, d * 128:(d + 1) * 128],
                    w_sb[:, d * w_cols + fg * 512: d * w_cols + fg * 512 + 512],
                    start=(d == 0 and start_clear),
                    stop=(d == kchunks - 1 and bias_row is None))
        if bias_row is not None:
            for i in range(nfg):
                fg = fg0 + i
                ins = e.matmul(ps[:, i * 512:(i + 1) * 512],
                               C.ones_col[:1, :],
                               bias_row[:1, fg * 512:fg * 512 + 512],
                               start=False, stop=True)
        return ins
    p.eng("tensor", pe)


def build_l0_attn(b, C, sctx, xs, ys):
    nc = b.nc
    qkv_w = b.inp("L0_attn_qkv_w", [D, 3 * D])
    qkv_b = b.inp("L0_attn_qkv_b", [3 * D])
    out_w = b.inp("L0_attn_out_w", [D, D])
    out_b = b.inp("L0_attn_out_b", [D])
    w1 = b.inp("L0_attn_w1", [D, DFF])
    b1 = b.inp("L0_attn_b1", [DFF])
    w2 = b.inp("L0_attn_w2", [DFF, D])
    b2 = b.inp("L0_attn_b2", [D])
    g1 = b.inp("L0_attn_g1", [D])
    be1 = b.inp("L0_attn_be1", [D])
    g2 = b.inp("L0_attn_g2", [D])
    be2 = b.inp("L0_attn_be2", [D])
    bsel = b.inp("aux_bsel", [128, 4])

    qf_sp = b.dram("l0a_qf", [TOK, D], BF16)
    x1_sp = b.dram("l0a_x1", [TOK, D], F32)
    xn2_sp = b.dram("l0a_xn2", [TOK, D], BF16)
    cc_in = b.dram("l0a_ccin", [4 * 65, D], F32)
    cc_out = b.dram("l0a_ccout", [4 * 65, D], F32)

    passes = []

    kv_mat = sctx.enter_context(nc.sbuf_tensor("l0a_kvmat", [128, 512], BF16))
    z_mat = sctx.enter_context(nc.sbuf_tensor("l0a_zmat", [128, 8], BF16))

    from contextlib import ExitStack
    with ExitStack() as actx:
        sb = lambda n, s, d=F32: actx.enter_context(nc.sbuf_tensor(n, s, d))
        ps = lambda n, s, d=F32: actx.enter_context(nc.psum_tensor(n, s, d))
        p = Pass(b, "l0a1")

        Wq = sb("a1_w", [128, 8 * 3072], BF16)
        stage = sb("a1_stage", [128, 1024])
        qkvb_row = sb("a1_qb", [1, 3072], BF16)
        rowstage = sb("a1_rs", [1, 1024])
        g1bc = sb("a1_g1bc", [128, D])
        be1bc = sb("a1_be1bc", [128, D])
        bselp = sb("a1_bsel", [128, 4])
        xin = sb("a1_xin", [128, D])
        xn_bf = sb("a1_xnbf", [128, D], BF16)
        xnT = sb("a1_xnT", [128, D], BF16)
        qmin = sb("a1_qmin", [128, 512])
        qexp = sb("a1_qexp", [128, 512])
        qf_sb = sb("a1_qf", [128, D], BF16)
        kf_sb = sb("a1_kf", [128, D], BF16)
        v_sb = sb("a1_v", [128, D], BF16)
        kv_f = sb("a1_kvf", [128, 512])
        z_f = sb("a1_zf", [128, 8])
        zrow = sb("a1_zrow", [1, D])
        kvs = sb("a1_kvs", [128, 512])
        zs = sb("a1_zs", [1, D])
        st = {k: sb("a1_st_" + k, [128, 1]) for k in
              ["ssq", "ssum", "mu", "var", "rstd"]}
        st["scr"] = sb("a1_scr", [128, D])
        tmp = sb("a1_tmp", [128, D])

        ps_tr = ps("a1_ptr", [128, 1024], BF16)
        ps_mm = ps("a1_pmm", [128, 2 * 512])
        ps_kv = ps("a1_pkv", [128, 512])
        ps_za = ps("a1_pza", [1, 512])
        ps_zb = ps("a1_pzb", [1, 512])
        ps_bc = ps("a1_pbc", [128, 512])

        load_weight_bf16(p, qkv_w, Wq, stage, D, 3072)
        load_row_bf16(p, qkv_b, qkvb_row, rowstage, 3072)
        grow = sb("a1_grow", [1, D])
        brow = sb("a1_brow", [1, D])
        p.dma("sync", lambda e: e.dma_start(grow[:1, :], g1[None, :]))
        p.dma("sync", lambda e: e.dma_start(brow[:1, :], be1[None, :]))
        p.dma("sync", lambda e: e.dma_start(bselp[:], bsel[:, :]))
        bcast_row(p, C, grow, g1bc, ps_bc, D)
        bcast_row(p, C, brow, be1bc, ps_bc, D)

        for j in range(NT):
            p.dma("sync", lambda e, j=j: e.dma_start(
                xin[:], xs[j * 128:(j + 1) * 128, :]))
            ln_stats(p, xin, D, st, C.eps_ln)
            ln_apply_gb(p, xin, xn_bf, st, g1bc, be1bc, tmp)
            tr8(p, C, xn_bf, xnT, ps_tr)
            for half in range(3):
                matmul_tm(p, C, xnT, Wq, 3072, ps_mm, 2 * half, 2, 8,
                          bias_row=qkvb_row)
                if half < 2:
                    dst = qf_sb if half == 0 else kf_sb
                    for gg in range(2):
                        col = gg * 512
                        p.eng("vector", lambda e, gg=gg: e.tensor_scalar(
                            qmin[:], ps_mm[:, gg * 512:(gg + 1) * 512],
                            0.0, None, ALU.min))
                        p.eng("scalar", lambda e: e.activation(
                            qexp[:], qmin[:], AF.Exp))
                        p.eng("vector", lambda e, gg=gg, dst=dst, col=col:
                              e.scalar_tensor_tensor(
                                  dst[:, col:col + 512],
                                  ps_mm[:, gg * 512:(gg + 1) * 512],
                                  0.0, qexp[:], ALU.max, ALU.add))
                else:
                    p.eng("vector", lambda e: e.tensor_copy(
                        v_sb[:], ps_mm[:]))
            def kvmm(e, j=j):
                ins = None
                for h in range(NH):
                    ins = e.matmul(
                        ps_kv[64 * (h % 2):64 * (h % 2) + 64,
                              (h // 2) * 64:(h // 2) * 64 + 64],
                        kf_sb[:, h * 64:(h + 1) * 64],
                        v_sb[:, h * 64:(h + 1) * 64],
                        start=(j == 0), stop=(j == NT - 1))
                ins = e.matmul(ps_za[:1, :], C.ones_t[:, :1],
                               kf_sb[:, 0:512], start=(j == 0),
                               stop=(j == NT - 1))
                ins = e.matmul(ps_zb[:1, :], C.ones_t[:, :1],
                               kf_sb[:, 512:1024], start=(j == 0),
                               stop=(j == NT - 1))
                return ins
            p.eng("tensor", kvmm)
            p.dma("gpsimd", lambda e, j=j: e.dma_start(
                qf_sp[j * 128:(j + 1) * 128, :], qf_sb[:]))

        p.eng("vector", lambda e: e.tensor_copy(kv_f[:], ps_kv[:]))
        def zev(e):
            e.tensor_copy(zrow[:1, 0:512], ps_za[:1, :])
            return e.tensor_copy(zrow[:1, 512:1024], ps_zb[:1, :])
        p.eng("vector", zev)
        for bb in range(4):
            p.eng("vector", lambda e, bb=bb: e.tensor_scalar(
                kvs[:], kv_f[:], bselp[:, bb:bb + 1], None, ALU.mult))
            p.eng("vector", lambda e, bb=bb: e.tensor_scalar(
                zs[:1, :], zrow[:1, :], bselp[0:1, bb:bb + 1], None, ALU.mult))
            for par in range(2):
                src = kvs[64 * par:64 * par + 64, :].rearrange(
                    "d (u e) -> d u e", e=64)
                dst = cc_in[bb * 65:bb * 65 + 64, :].rearrange(
                    "d (u two e) -> two d u e", two=2, e=64)[par]
                p.dma("gpsimd", lambda e, src=src, dst=dst: e.dma_start(dst, src))
            p.dma("gpsimd", lambda e, bb=bb: e.dma_start(
                cc_in[bb * 65 + 64:bb * 65 + 65, :], zs[:1, :]))
        p.coll(lambda e: e.collective_compute(
            "AllReduce", ALU.add, [list(range(NCORE))],
            [cc_in[:, :]], [cc_out[:, :]]))
        kvz_acc = sb("a1_kvza", [65, D])
        kvz_t = sb("a1_kvzt", [65, D])
        p.eng("vector", lambda e: e.memset(kvz_acc[:], 0.0))
        for bb in range(4):
            p.dma("sync", lambda e, bb=bb: e.dma_start(
                kvz_t[:], cc_out[bb * 65:(bb + 1) * 65, :]))
            p.eng("vector", lambda e, bb=bb: e.scalar_tensor_tensor(
                kvz_acc[:], kvz_t[:], bselp[0:65, bb:bb + 1], kvz_acc[:],
                ALU.mult, ALU.add))
        for par in range(2):
            src = kvz_acc[0:64, :].rearrange("d (u two e) -> d u two e",
                                             two=2, e=64)[:, :, par, :]
            p.eng("vector", lambda e, par=par, src=src: e.tensor_copy(
                kv_mat[64 * par:64 * par + 64, :].rearrange(
                    "d (u e) -> d u e", e=64), src))
        ps_z = ps("a1_pz", [128, 8])
        def ztr(e):
            ins = None
            for u in range(8):
                ins = e.transpose(ps_z[:, u:u + 1],
                                  kvz_acc[64:65, u * 128:(u + 1) * 128],
                                  C.ident_f[64:65, 64:65])
            return ins
        p.eng("tensor", ztr)
        p.eng("vector", lambda e: e.tensor_copy(z_mat[:], ps_z[:]))
        p.emit()
        passes.append("l0a1")

    with ExitStack() as actx:
        sb = lambda n, s, d=F32: actx.enter_context(nc.sbuf_tensor(n, s, d))
        ps = lambda n, s, d=F32: actx.enter_context(nc.psum_tensor(n, s, d))
        p = Pass(b, "l0a2")

        Wo = sb("a2_w", [128, 8 * D], BF16)
        stage = sb("a2_stage", [128, D])
        ob_row = sb("a2_ob", [1, D], BF16)
        rowstage = sb("a2_rs", [1, D])
        g2bc = sb("a2_g2bc", [128, D])
        be2bc = sb("a2_be2bc", [128, D])
        xin = sb("a2_xin", [128, D])
        qf_bf = sb("a2_qf", [128, D], BF16)
        qfT = sb("a2_qfT", [128, D], BF16)
        y_bf = sb("a2_y", [128, D], BF16)
        yT = sb("a2_yT", [128, D], BF16)
        x1 = sb("a2_x1", [128, D])
        xn2_bf = sb("a2_xn2", [128, D], BF16)
        recip = sb("a2_recip", [128, 16])
        den_t = sb("a2_dent", [128, 16])
        st = {k: sb("a2_st_" + k, [128, 1]) for k in
              ["ssq", "ssum", "mu", "var", "rstd"]}
        st["scr"] = sb("a2_scr", [128, D])
        tmp = sb("a2_tmp", [128, D])

        ps_tr = ps("a2_ptr", [128, 1024], BF16)
        ps_yn = ps("a2_pyn", [128, 1024])
        ps_yd = ps("a2_pyd", [128, 16])
        ps_o = ps("a2_po", [128, 2 * 512])
        ps_bc = ps("a2_pbc", [128, 512])

        load_weight_bf16(p, out_w, Wo, stage, D, D)
        load_row_bf16(p, out_b, ob_row, rowstage, D)
        grow = sb("a2_grow", [1, D])
        brow = sb("a2_brow", [1, D])
        p.dma("sync", lambda e: e.dma_start(grow[:1, :], g2[None, :]))
        p.dma("sync", lambda e: e.dma_start(brow[:1, :], be2[None, :]))
        bcast_row(p, C, grow, g2bc, ps_bc, D)
        bcast_row(p, C, brow, be2bc, ps_bc, D)

        for j in range(NT):
            p.dma("sync", lambda e, j=j: e.dma_start(
                qf_bf[:], qf_sp[j * 128:(j + 1) * 128, :]))
            p.dma("sync", lambda e, j=j: e.dma_start(
                xin[:], xs[j * 128:(j + 1) * 128, :]))
            tr8(p, C, qf_bf, qfT, ps_tr)
            def ymm(par):
                def fn(e):
                    ins = None
                    for h in range(par, NH, 2):
                        lhs = qfT[64 * par:64 * par + 64,
                                  (h // 2) * 128:(h // 2) * 128 + 128]
                        rhs = kv_mat[64 * par:64 * par + 64,
                                     (h // 2) * 64:(h // 2) * 64 + 64]
                        ins = e.matmul(ps_yn[:, h * 64:(h + 1) * 64], lhs,
                                       rhs, start=True, stop=True)
                        rz = z_mat[64 * par:64 * par + 64,
                                   (h // 2):(h // 2) + 1]
                        ins = e.matmul(ps_yd[:, h:h + 1], lhs, rz,
                                       start=True, stop=True)
                    return ins
                return fn
            p.eng("tensor", ymm(0))
            p.eng("tensor", ymm(1))
            p.eng("vector", lambda e: e.tensor_scalar(
                den_t[:], ps_yd[:], float(ATTN_EPS), None, ALU.max))
            p.eng("vector", lambda e: e.reciprocal(recip[:], den_t[:]))
            def ydiv(e):
                ins = None
                for h in range(NH):
                    ins = e.tensor_scalar(
                        y_bf[:, h * 64:(h + 1) * 64],
                        ps_yn[:, h * 64:(h + 1) * 64],
                        recip[:, h:h + 1], None, ALU.mult)
                return ins
            p.eng("vector", ydiv)
            tr8(p, C, y_bf, yT, ps_tr)
            matmul_tm(p, C, yT, Wo, D, ps_o, 0, 2, 8, bias_row=ob_row)
            def resid(e):
                e.tensor_tensor(x1[:, 0:512], ps_o[:, 0:512], xin[:, 0:512],
                                ALU.add)
                return e.tensor_tensor(x1[:, 512:1024], ps_o[:, 512:1024],
                                       xin[:, 512:1024], ALU.add)
            p.eng("vector", resid)
            ln_stats(p, x1, D, st, C.eps_ln)
            ln_apply_gb(p, x1, xn2_bf, st, g2bc, be2bc, tmp)
            p.dma("gpsimd", lambda e, j=j: e.dma_start(
                x1_sp[j * 128:(j + 1) * 128, :], x1[:]))
            p.dma("gpsimd", lambda e, j=j: e.dma_start(
                xn2_sp[j * 128:(j + 1) * 128, :], xn2_bf[:]))
        p.emit()
        passes.append("l0a2")

    with ExitStack() as actx:
        sb = lambda n, s, d=F32: actx.enter_context(nc.sbuf_tensor(n, s, d))
        ps = lambda n, s, d=F32: actx.enter_context(nc.psum_tensor(n, s, d))
        p = Pass(b, "l0a3")

        W1s = sb("a3_w1", [128, 8 * DFF], BF16)
        W2s = sb("a3_w2", [128, 32 * D], BF16)
        stage = sb("a3_stage", [128, 1024])
        b1_row = sb("a3_b1", [1, DFF], BF16)
        b2_row = sb("a3_b2", [1, D], BF16)
        rowstage = sb("a3_rs", [1, 1024])
        xin = sb("a3_xin", [128, D])
        x1t = sb("a3_x1", [128, D])
        xn2 = sb("a3_xn2", [128, D], BF16)
        xn2T = sb("a3_xn2T", [128, D], BF16)
        h_bf = sb("a3_h", [128, DFF], BF16)
        hT = sb("a3_hT", [128, DFF], BF16)
        arg = sb("a3_arg", [128, D])
        outt = sb("a3_out", [128, D])
        st = {k: sb("a3_st_" + k, [128, 1]) for k in
              ["ssq", "ssum", "mu", "var", "rstd"]}
        st["scr"] = sb("a3_scr", [128, D])

        ps_tr = ps("a3_ptr", [128, 1024], BF16)
        ps_h = ps("a3_ph", [128, 2 * 512])
        ps_y = ps("a3_py", [128, 2 * 512])

        load_weight_bf16(p, w1, W1s, stage, D, DFF)
        load_weight_bf16(p, w2, W2s, stage, DFF, D)
        load_row_bf16(p, b1, b1_row, rowstage, DFF)
        load_row_bf16(p, b2, b2_row, rowstage, D)

        for j in range(NT):
            p.dma("sync", lambda e, j=j: e.dma_start(
                xn2[:], xn2_sp[j * 128:(j + 1) * 128, :]))
            p.dma("sync", lambda e, j=j: e.dma_start(
                x1t[:], x1_sp[j * 128:(j + 1) * 128, :]))
            p.dma("sync", lambda e, j=j: e.dma_start(
                xin[:], xs[j * 128:(j + 1) * 128, :]))
            tr8(p, C, xn2, xn2T, ps_tr)
            for half in range(4):
                matmul_tm(p, C, xn2T, W1s, DFF, ps_h, 2 * half, 2, 8,
                          bias_row=b1_row)
                p.eng("scalar", lambda e, half=half: e.activation(
                    h_bf[:, half * 1024:(half + 1) * 1024], ps_h[:],
                    AF.Gelu))
            for q in range(4):
                tr8(p, C, h_bf[:, q * 1024:(q + 1) * 1024],
                    hT[:, q * 1024:(q + 1) * 1024], ps_tr)
            matmul_tm(p, C, hT, W2s, D, ps_y, 0, 2, 32, bias_row=b2_row)
            def res(e):
                e.tensor_tensor(arg[:, 0:512], ps_y[:, 0:512], x1t[:, 0:512],
                                ALU.add)
                e.tensor_tensor(arg[:, 512:1024], ps_y[:, 512:1024],
                                x1t[:, 512:1024], ALU.add)
                return e.tensor_tensor(arg[:], arg[:], xin[:], ALU.add)
            p.eng("vector", res)
            ln_stats(p, arg, D, st, C.eps_ln)
            ln_apply_plain(p, arg, outt, st)
            p.dma("gpsimd", lambda e, j=j: e.dma_start(
                ys[j * 128:(j + 1) * 128, :], outt[:]))
        p.emit()
        passes.append("l0a3")

    return passes


def build_l0_ffn(b, C, sctx, xs, ys):
    from contextlib import ExitStack
    nc = b.nc
    gate_w = b.inp("L0_ffn_gate_w", [E0, D])
    cb_in = b.inp("L0_ffn_correction_bias", [E0])
    Wg = b.inp("L0_ffn_Wg", [E0, D, I0])
    Wu = b.inp("L0_ffn_Wu", [E0, D, I0])
    Wd = b.inp("L0_ffn_Wd", [E0, I0, D])
    Wg_s = b.inp("L0_ffn_Wg_s", [D, I0])
    Wu_s = b.inp("L0_ffn_Wu_s", [D, I0])
    Wd_s = b.inp("L0_ffn_Wd_s", [I0, D])

    xT_sp = b.dram("l0f_xT", [TOK, D], BF16)
    racc = b.dram("l0f_racc", [TOK, D], F32)

    cw_em = sctx.enter_context(nc.sbuf_tensor("l0f_cw", [128, E0 * NT], F32))

    with ExitStack() as actx:
        sb = lambda n, s, d=F32: actx.enter_context(nc.sbuf_tensor(n, s, d))
        ps = lambda n, s, d=F32: actx.enter_context(nc.psum_tensor(n, s, d))
        p = Pass(b, "l0f0")

        gwT = sb("f0_gwT", [128, 8 * E0])
        gstage = sb("f0_gstage", [8, D])
        xTf = sb("f0_xTf", [128, D])
        cbrow = sb("f0_cbrow", [1, E0])
        cb_bc = sb("f0_cbbc", [128, E0])
        xin = sb("f0_xin", [128, D])
        x_bf = sb("f0_xbf", [128, D], BF16)
        xT = sb("f0_xT", [128, D], BF16)
        lgs_em = sb("f0_lgs", [128, E0 * NT])
        sc_em = sb("f0_sc", [128, E0 * NT])
        gs_em = sb("f0_gs", [128, 4 * NT])
        cnt4 = sb("f0_cnt4", [128, 4 * NT])
        tmpb = sb("f0_tmpb", [128, NT])
        smask = sb("f0_smask", [128, E0 * NT])
        sfc = sb("f0_sfc", [128, E0 * NT])
        cnt8 = sb("f0_cnt8", [128, E0 * NT])
        sel4 = sb("f0_sel4", [128, E0 * NT])
        w4 = sb("f0_w4", [128, E0 * NT])
        wsum = sb("f0_wsum", [128, NT])
        wfac = sb("f0_wfac", [128, NT])

        ps_tr = ps("f0_ptr", [128, 1024], BF16)
        ps_trf = ps("f0_ptrf", [128, 1024])
        ps_l = ps("f0_pl", [128, E0])
        ps_g8 = ps("f0_pg8", [128, 8 * E0])
        ps_bc = ps("f0_pbc", [128, 512])

        p.dma("sync", lambda e: e.dma_start(gstage[:, :], gate_w[:, :]))
        def gtr(e):
            ins = None
            for d in range(8):
                ins = e.transpose(ps_g8[:, d * E0:(d + 1) * E0],
                                  gstage[:, d * 128:(d + 1) * 128],
                                  C.ident_f[0:8, 0:8])
            return ins
        p.eng("tensor", gtr)
        p.eng("vector", lambda e: e.tensor_copy(gwT[:, :], ps_g8[:, :]))
        p.dma("sync", lambda e: e.dma_start(cbrow[:1, :], cb_in[None, :]))
        p.eng("tensor", lambda e: e.matmul(ps_bc[:, :E0], C.ones_colf[:1, :],
                                           cbrow[:1, :], start=True, stop=True))
        p.eng("vector", lambda e: e.tensor_copy(cb_bc[:, :], ps_bc[:, :E0]))

        for j in range(NT):
            p.dma("sync", lambda e, j=j: e.dma_start(
                xin[:], xs[j * 128:(j + 1) * 128, :]))
            p.eng("vector", lambda e: e.tensor_copy(x_bf[:], xin[:]))
            tr8(p, C, x_bf, xT, ps_tr)
            p.dma("gpsimd", lambda e, j=j: e.dma_start(
                xT_sp[j * 128:(j + 1) * 128, :], xT[:]))
            tr8(p, C, xin, xTf, ps_trf)
            def lmm(e):
                ins = None
                for d in range(8):
                    ins = e.matmul(ps_l[:, :], xTf[:, d * 128:(d + 1) * 128],
                                   gwT[:, d * E0:(d + 1) * E0],
                                   start=(d == 0), stop=(d == 7))
                return ins
            p.eng("tensor", lmm)
            p.eng("vector", lambda e, j=j: e.tensor_copy(
                lgs_em[:, j::NT], ps_l[:, :]))

        blk = lambda t, e0: t[:, e0 * NT:(e0 + 1) * NT]
        p.eng("scalar", lambda e: e.activation(sc_em[:, :], lgs_em[:, :],
                                               AF.Sigmoid))
        def addcb(e):
            ins = None
            for e0 in range(E0):
                ins = e.tensor_scalar(blk(sc_em, e0), blk(sc_em, e0),
                                      cb_bc[:, e0:e0 + 1], None, ALU.add)
            return ins
        p.eng("vector", addcb)
        def gsum(e):
            ins = None
            for g in range(4):
                ins = e.tensor_tensor(blk(gs_em, g), blk(sc_em, 2 * g),
                                      blk(sc_em, 2 * g + 1), ALU.add)
            return ins
        p.eng("vector", gsum)
        p.eng("vector", lambda e: e.memset(cnt4[:, :], 0.0))
        def grank(e):
            ins = None
            for g in range(4):
                for gp in range(4):
                    if gp == g:
                        continue
                    op = ALU.is_ge if gp < g else ALU.is_gt
                    e.tensor_tensor(tmpb[:, :], blk(gs_em, gp), blk(gs_em, g),
                                    op)
                    ins = e.tensor_tensor(blk(cnt4, g), blk(cnt4, g),
                                          tmpb[:, :], ALU.add)
            return ins
        p.eng("vector", grank)
        def gmask(e):
            e.tensor_single_scalar(cnt4[:, :], cnt4[:, :], 2.0, ALU.is_lt)
            ins = None
            for e0 in range(E0):
                ins = e.tensor_copy(blk(smask, e0), blk(cnt4, e0 // 2))
            return ins
        p.eng("vector", gmask)
        p.eng("vector", lambda e: e.tensor_tensor(sfc[:, :], smask[:, :],
                                                  sc_em[:, :], ALU.mult))
        p.eng("vector", lambda e: e.memset(cnt8[:, :], 0.0))
        for e0 in range(E0):
            def erank(e, e0=e0):
                ins = None
                for ep in range(E0):
                    if ep == e0:
                        continue
                    op = ALU.is_ge if ep < e0 else ALU.is_gt
                    e.tensor_tensor(tmpb[:, :], blk(sfc, ep), blk(sfc, e0), op)
                    ins = e.tensor_tensor(blk(cnt8, e0), blk(cnt8, e0),
                                          tmpb[:, :], ALU.add)
                return ins
            p.eng("vector", erank)
        def finw(e):
            e.tensor_single_scalar(sel4[:, :], cnt8[:, :], 4.0, ALU.is_lt)
            return sel4
        p.eng("vector", lambda e: e.tensor_single_scalar(
            sel4[:, :], cnt8[:, :], 4.0, ALU.is_lt))
        def subcb(e):
            ins = None
            for e0 in range(E0):
                ins = e.tensor_scalar(blk(sc_em, e0), blk(sc_em, e0),
                                      cb_bc[:, e0:e0 + 1], None, ALU.subtract)
            return ins
        p.eng("vector", subcb)
        p.eng("vector", lambda e: e.tensor_tensor(w4[:, :], sel4[:, :],
                                                  sc_em[:, :], ALU.mult))
        def wsumf(e):
            e.tensor_tensor(wsum[:, :], blk(w4, 0), blk(w4, 1), ALU.add)
            for e0 in range(2, E0):
                e.tensor_tensor(wsum[:, :], wsum[:, :], blk(w4, e0), ALU.add)
            return e.tensor_single_scalar(wsum[:, :], wsum[:, :], 1e-20,
                                          ALU.add)
        p.eng("vector", wsumf)
        p.eng("vector", lambda e: e.reciprocal(wfac[:, :], wsum[:, :]))
        p.eng("vector", lambda e: e.tensor_single_scalar(
            wfac[:, :], wfac[:, :], float(SCALE0), ALU.mult))
        def cwf(e):
            ins = None
            for e0 in range(E0):
                ins = e.tensor_tensor(blk(cw_em, e0), blk(w4, e0),
                                      wfac[:, :], ALU.mult)
            return ins
        p.eng("vector", cwf)
        p.emit()

    with ExitStack() as actx:
        sb = lambda n, s, d=F32: actx.enter_context(nc.sbuf_tensor(n, s, d))
        ps = lambda n, s, d=F32: actx.enter_context(nc.psum_tensor(n, s, d))
        p = Pass(b, "l0f1")

        Wgs = [sb(f"f1_wg{i}", [128, 8 * I0], BF16) for i in range(2)]
        Wus = [sb(f"f1_wu{i}", [128, 8 * I0], BF16) for i in range(2)]
        Wds = [sb(f"f1_wd{i}", [128, 4 * D], BF16) for i in range(2)]
        stage = [sb(f"f1_stage{i}", [128, 1024]) for i in range(2)]
        xT = [sb(f"f1_xT{i}", [128, D], BF16) for i in range(2)]
        gs_bf = [sb(f"f1_gs{i}", [128, I0], BF16) for i in range(2)]
        h_bf = [sb(f"f1_h{i}", [128, I0], BF16) for i in range(2)]
        hT = [sb(f"f1_hT{i}", [128, I0], BF16) for i in range(2)]
        ysb = [sb(f"f1_ysb{i}", [128, D]) for i in range(2)]
        ps_g = [ps(f"f1_pg{i}", [128, I0]) for i in range(2)]
        ps_u = [ps(f"f1_pu{i}", [128, I0]) for i in range(2)]
        ps_tr = ps("f1_ptr", [128, 512], BF16)
        ps_y = ps("f1_py", [128, 2 * 512])

        def loadw(ex):
            wp = ex % 2
            shared = ex == E0
            srcs = [(Wg_s if shared else Wg[ex], Wgs[wp], D, I0),
                    (Wu_s if shared else Wu[ex], Wus[wp], D, I0),
                    (Wd_s if shared else Wd[ex], Wds[wp], I0, D)]
            lp = 0
            for (src, dst, rows, cols) in srcs:
                for k in range(rows // 128):
                    p.dma("sync", lambda e, src=src, k=k, cols=cols, lp=lp:
                          e.dma_start(stage[lp][:, :cols],
                                      src[k * 128:(k + 1) * 128, :]),
                          group=4 + lp, rw=([], [f"stg{lp}"]))
                    p.eng("scalar", lambda e, dst=dst, k=k, cols=cols, lp=lp:
                          e.copy(dst[:, k * cols:(k + 1) * cols],
                                 stage[lp][:, :cols]),
                          rw=([f"stg{lp}"], [f"wbuf{wp}"]))
                    lp = 1 - lp

        for ex in range(E0 + 1):
            wp = ex % 2
            loadw(ex)
            for j in range(NT):
                q = j % 2
                p.dma("sync", lambda e, j=j, q=q: e.dma_start(
                    xT[q][:], xT_sp[j * 128:(j + 1) * 128, :]),
                    group=q, rw=(["xTsp"], [f"xT{q}"]))
                def gumm(e, q=q, wp=wp):
                    ins = None
                    for d in range(8):
                        xTd = xT[q][:, d * 128:(d + 1) * 128]
                        ins = e.matmul(ps_g[q][:, :], xTd,
                                       Wgs[wp][:, d * I0:(d + 1) * I0],
                                       start=(d == 0), stop=(d == 7))
                        ins = e.matmul(ps_u[q][:, :], xTd,
                                       Wus[wp][:, d * I0:(d + 1) * I0],
                                       start=(d == 0), stop=(d == 7))
                    return ins
                p.eng("tensor", gumm,
                      rw=([f"xT{q}", f"wbuf{wp}"], [f"pg{q}", f"pu{q}"]))
                p.eng("scalar", lambda e, q=q: e.activation(
                    gs_bf[q][:], ps_g[q][:], AF.Silu),
                    rw=([f"pg{q}"], [f"gs{q}"]))
                if ex == E0:
                    p.eng("vector", lambda e, q=q: e.tensor_tensor(
                        h_bf[q][:], ps_u[q][:], gs_bf[q][:], ALU.mult),
                        rw=([f"pu{q}", f"gs{q}"], [f"h{q}"]))
                else:
                    p.eng("vector", lambda e, ex=ex, j=j, q=q:
                          e.scalar_tensor_tensor(
                              h_bf[q][:], ps_u[q][:],
                              cw_em[:, ex * NT + j:ex * NT + j + 1],
                              gs_bf[q][:], ALU.mult, ALU.mult),
                          rw=([f"pu{q}", f"gs{q}"], [f"h{q}"]))
                def htr(e, q=q):
                    ins = None
                    for k in range(4):
                        ins = e.transpose(ps_tr[:, k * 128:(k + 1) * 128],
                                          h_bf[q][:, k * 128:(k + 1) * 128],
                                          C.ident_bf[:, :])
                    return ins
                p.eng("tensor", htr, rw=([f"h{q}"], ["ptr"]))
                p.eng("vector", lambda e, q=q: e.tensor_copy(
                    hT[q][:], ps_tr[:, 0:512]), rw=(["ptr"], [f"hT{q}"]))
                def ymm(e, q=q, wp=wp):
                    ins = None
                    for i in range(4):
                        for fg in range(2):
                            ins = e.matmul(
                                ps_y[:, fg * 512:(fg + 1) * 512],
                                hT[q][:, i * 128:(i + 1) * 128],
                                Wds[wp][:, i * D + fg * 512:
                                         i * D + fg * 512 + 512],
                                start=(i == 0), stop=(i == 3))
                    return ins
                p.eng("tensor", ymm, rw=([f"hT{q}", f"wbuf{wp}"], ["py"]))
                p.eng("vector", lambda e, q=q: e.tensor_copy(
                    ysb[q][:], ps_y[:]), rw=(["py"], [f"ysb{q}"]))
                if ex == 0:
                    fnst = (lambda e, j=j, q=q: e.dma_start(
                        racc[j * 128:(j + 1) * 128, :], ysb[q][:]))
                else:
                    fnst = (lambda e, j=j, q=q: e.dma_start(
                        racc[j * 128:(j + 1) * 128, :], ysb[q][:],
                        accum_op=ALU.add))
                p.dma("gpsimd", fnst, group=2 + q,
                      rw=([f"ysb{q}", f"racc{j}", f"stq{2 + q}"],
                          [f"racc{j}", f"stq{2 + q}"]))
        p.emit()

    with ExitStack() as actx:
        sb = lambda n, s, d=F32: actx.enter_context(nc.sbuf_tensor(n, s, d))
        p = Pass(b, "l0f2")
        xin = sb("f2_xin", [128, D])
        rin = sb("f2_rin", [128, D])
        arg = sb("f2_arg", [128, D])
        outt = sb("f2_out", [128, D])
        st = {k: sb("f2_st_" + k, [128, 1]) for k in
              ["ssq", "ssum", "mu", "var", "rstd"]}
        st["scr"] = sb("f2_scr", [128, D])
        for j in range(NT):
            p.dma("sync", lambda e, j=j: e.dma_start(
                xin[:], xs[j * 128:(j + 1) * 128, :]))
            p.dma("sync", lambda e, j=j: e.dma_start(
                rin[:], racc[j * 128:(j + 1) * 128, :]))
            p.eng("vector", lambda e: e.tensor_tensor(arg[:], xin[:], rin[:],
                                                      ALU.add))
            ln_stats(p, arg, D, st, C.eps_ln)
            ln_apply_plain(p, arg, outt, st)
            p.dma("gpsimd", lambda e, j=j: e.dma_start(
                ys[j * 128:(j + 1) * 128, :], outt[:]))
        p.emit()
    return ["l0f0", "l0f1", "l0f2"]


def build_l1_attn(b, C, sctx, xs, ys):
    from contextlib import ExitStack
    nc = b.nc
    ln1_g = b.inp("L1_attn_ln1_g", [D])
    ln1_b = b.inp("L1_attn_ln1_b", [D])
    ln2_g = b.inp("L1_attn_ln2_g", [D])
    ln2_b = b.inp("L1_attn_ln2_b", [D])
    W1 = b.inp("L1_attn_W1", [D, DFF])
    b1 = b.inp("L1_attn_b1", [DFF])
    W2 = b.inp("L1_attn_W2", [DFF, D])
    b2 = b.inp("L1_attn_b2", [D])
    lsel = b.inp("aux_lsel", [128, NCORE])
    rsel = b.inp("aux_rsel", [128, NCORE])
    maskin = b.inp("aux_maskcap", [6 * 128, 768])

    hal_in = b.dram("l1a_halin", [512, D], BF16)
    b.dram("l1a_attn", [TOK, D], F32)
    hal_out = b.dram("l1a_halout", [NCORE * 512, D], BF16)
    x1_sp = b.dram("l1a_x1", [TOK, D], F32)

    with ExitStack() as actx:
        sb = lambda n, s, d=F32: actx.enter_context(nc.sbuf_tensor(n, s, d))
        ps = lambda n, s, d=F32: actx.enter_context(nc.psum_tensor(n, s, d))
        p = Pass(b, "l1a1")

        mask_sb = sb("c1_mask", [128, 6 * 768])
        selp = sb("c1_selp", [128, 2 * NCORE])
        xin = sb("c1_xin", [128, D])
        halo_x = [sb(f"c1_hx{i}", [128, D], BF16) for i in range(4)]
        halo_xT = [sb(f"c1_hxT{i}", [128, D], BF16) for i in range(4)]
        ring_x = [sb(f"c1_rx{i}", [128, D], BF16) for i in range(8)]
        ring_xT = [sb(f"c1_rxT{i}", [128, D], BF16) for i in range(8)]
        accf = sb("c1_accf", [128, D])
        tmpbf = sb("c1_tmpbf", [128, D], BF16)
        s_sb = sb("c1_ssb", [128, 768])
        negmax = sb("c1_negmax", [128, 1])
        den = sb("c1_den", [128, 1])
        recip = sb("c1_recip", [128, 1])
        p_bf = sb("c1_pbf", [128, 768], BF16)
        pT = sb("c1_pT", [128, 768], BF16)
        attn_t = sb("c1_attn", [128, D])
        t1 = sb("c1_t1", [128, D])
        x1 = sb("c1_x1", [128, D])
        g1bc = sb("c1_g1bc", [128, D])
        b1bc = sb("c1_b1bc", [128, D])
        grow = sb("c1_grow", [1, D])
        tmp = sb("c1_tmp", [128, D])
        st = {k: sb("c1_st_" + k, [128, 1]) for k in
              ["ssq", "ssum", "mu", "var", "rstd"]}
        st["scr"] = sb("c1_scr", [128, D])

        ps_tr = ps("c1_ptr", [128, 1024], BF16)
        ps_s = ps("c1_ps", [128, 768])
        ps_o = ps("c1_po", [128, 64])
        ps_bc = ps("c1_pbc", [128, 512])

        for i in range(6):
            p.dma("sync", lambda e, i=i: e.dma_start(
                mask_sb[:, i * 768:(i + 1) * 768],
                maskin[i * 128:(i + 1) * 128, :]))
        p.dma("sync", lambda e: e.dma_start(selp[:, 0:NCORE], lsel[:, :]))
        p.dma("sync", lambda e: e.dma_start(selp[:, NCORE:], rsel[:, :]))
        p.dma("sync", lambda e: e.dma_start(grow[:1, :], ln1_g[None, :]))
        bcast_row(p, C, grow, g1bc, ps_bc, D)
        p.dma("sync", lambda e: e.dma_start(grow[:1, :], ln1_b[None, :]))
        bcast_row(p, C, grow, b1bc, ps_bc, D)

        for i, j in enumerate([0, 1, 30, 31]):
            p.dma("sync", lambda e, j=j: e.dma_start(
                xin[:], xs[j * 128:(j + 1) * 128, :]))
            p.eng("vector", lambda e: e.tensor_copy(tmpbf[:], xin[:]))
            p.dma("gpsimd", lambda e, i=i: e.dma_start(
                hal_in[i * 128:(i + 1) * 128, :], tmpbf[:]))
        p.coll(lambda e: e.collective_compute(
            "AllGather", ALU.bypass, [list(range(NCORE))],
            [hal_in[:, :]], [hal_out[:, :]]))
        for side in range(2):
            for r in range(2):
                p.eng("vector", lambda e: e.memset(accf[:], 0.0))
                for cc in range(NCORE):
                    src = hal_out[cc * 512 + (1 - side) * 256 + r * 128:
                                  cc * 512 + (1 - side) * 256 + r * 128 + 128, :]
                    p.dma("sync", lambda e, src=src: e.dma_start(
                        tmpbf[:], src))
                    p.eng("vector", lambda e, cc=cc, side=side:
                          e.scalar_tensor_tensor(
                              accf[:], tmpbf[:],
                              selp[:, side * NCORE + cc:side * NCORE + cc + 1],
                              accf[:], ALU.mult, ALU.add))
                idx = side * 2 + r
                p.eng("vector", lambda e, idx=idx: e.tensor_copy(
                    halo_x[idx][:], accf[:]))
            for r in range(2):
                idx = side * 2 + r
                tr8(p, C, halo_x[idx], halo_xT[idx], ps_tr)

        def wtile(w):
            if w < 0:
                return halo_x[w + 2], halo_xT[w + 2]
            if w >= NT:
                return halo_x[w - NT + 2], halo_xT[w - NT + 2]
            return ring_x[w % 8], ring_xT[w % 8]

        def materialize(j):
            p.dma("sync", lambda e, j=j: e.dma_start(
                xin[:], xs[j * 128:(j + 1) * 128, :]))
            p.eng("vector", lambda e, j=j: e.tensor_copy(
                ring_x[j % 8][:], xin[:]))
            tr8(p, C, ring_x[j % 8], ring_xT[j % 8], ps_tr)

        for j in range(4):
            materialize(j)

        for j in range(NT):
            if j + 4 < NT:
                materialize(j + 4)
            ws = j - 2 - (j % 2)
            if j <= 1:
                mslot = j
            elif j >= 30:
                mslot = j - 26
            else:
                mslot = 2 + (j % 2)
            mask_ap = mask_sb[:, mslot * 768:(mslot + 1) * 768]
            for h in range(NH):
                par = h % 2
                hi = h // 2
                def smm(e, h=h, par=par, hi=hi, ws=ws, j=j):
                    ins = None
                    for wi in range(6):
                        _, xT_w = wtile(ws + wi)
                        ins = e.matmul(
                            ps_s[:, wi * 128:(wi + 1) * 128],
                            wtile(j)[1][64 * par:64 * par + 64,
                                        hi * 128:hi * 128 + 128],
                            xT_w[64 * par:64 * par + 64,
                                 hi * 128:hi * 128 + 128],
                            start=True, stop=True)
                    return ins
                p.eng("tensor", smm)
                p.eng("vector", lambda e, mask_ap=mask_ap: e.tensor_tensor(
                    s_sb[:], ps_s[:], mask_ap, ALU.min))
                p.eng("vector", lambda e: e.reduce_max(
                    negmax[:], s_sb[:], AX.X, negate=True))
                p.eng("scalar", lambda e: e.activation(
                    p_bf[:], s_sb[:], AF.Exp, bias=negmax[:],
                    accum_out=den[:]))
                p.eng("vector", lambda e: e.reciprocal(recip[:], den[:]))
                def ptr(e):
                    ins = None
                    for wi in range(6):
                        ins = e.transpose(
                            ps_tr[:, wi * 128:(wi + 1) * 128],
                            p_bf[:, wi * 128:(wi + 1) * 128],
                            C.ident_bf[:, :])
                    return ins
                p.eng("tensor", ptr)
                p.eng("vector", lambda e: e.tensor_copy(
                    pT[:], ps_tr[:, 0:768]))
                def omm(e, h=h, ws=ws):
                    ins = None
                    for wi in range(6):
                        x_w, _ = wtile(ws + wi)
                        ins = e.matmul(
                            ps_o[:, :], pT[:, wi * 128:(wi + 1) * 128],
                            x_w[:, h * 64:(h + 1) * 64],
                            start=(wi == 0), stop=(wi == 5))
                    return ins
                p.eng("tensor", omm)
                p.eng("vector", lambda e, h=h: e.tensor_scalar(
                    attn_t[:, h * 64:(h + 1) * 64], ps_o[:, :], recip[:],
                    None, ALU.mult))
            if DEBUG_DUMPS:
                p.dma("gpsimd", lambda e, j=j: e.dma_start(
                    b.drams["l1a_attn"][j * 128:(j + 1) * 128, :], attn_t[:]))
            p.dma("sync", lambda e, j=j: e.dma_start(
                xin[:], xs[j * 128:(j + 1) * 128, :]))
            p.eng("vector", lambda e: e.tensor_tensor(t1[:], attn_t[:],
                                                      xin[:], ALU.add))
            ln_stats(p, t1, D, st, C.eps_ln)
            ln_apply_gb(p, t1, x1, st, g1bc, b1bc, tmp)
            p.dma("gpsimd", lambda e, j=j: e.dma_start(
                x1_sp[j * 128:(j + 1) * 128, :], x1[:]))
        p.emit()

    with ExitStack() as actx:
        sb = lambda n, s, d=F32: actx.enter_context(nc.sbuf_tensor(n, s, d))
        ps = lambda n, s, d=F32: actx.enter_context(nc.psum_tensor(n, s, d))
        p = Pass(b, "l1a2")

        W1s = sb("c2_w1", [128, 8 * DFF], BF16)
        W2s = sb("c2_w2", [128, 32 * D], BF16)
        stage = sb("c2_stage", [128, 1024])
        b1_row = sb("c2_b1", [1, DFF], BF16)
        b2_row = sb("c2_b2", [1, D], BF16)
        rowstage = sb("c2_rs", [1, 1024])
        g2bc = sb("c2_g2bc", [128, D])
        b2bc = sb("c2_b2bc", [128, D])
        grow = rowstage
        xin = sb("c2_xin", [128, D])
        x1t = sb("c2_x1", [128, D])
        x1bf = sb("c2_x1bf", [128, D], BF16)
        x1T = sb("c2_x1T", [128, D], BF16)
        h_bf = sb("c2_h", [128, DFF], BF16)
        hT = sb("c2_hT", [128, DFF], BF16)
        t2 = sb("c2_t2", [128, D])
        so = sb("c2_so", [128, D])
        arg = sb("c2_arg", [128, D])
        outt = sb("c2_out", [128, D])
        st = {k: sb("c2_st_" + k, [128, 1]) for k in
              ["ssq", "ssum", "mu", "var", "rstd"]}
        st["scr"] = sb("c2_scr", [128, D])
        tmp = st["scr"]

        ps_tr = ps("c2_ptr", [128, 1024], BF16)
        ps_h = ps("c2_ph", [128, 2 * 512])
        ps_y = ps("c2_py", [128, 2 * 512])
        ps_bc = ps("c2_pbc", [128, 512])

        load_weight_bf16(p, W1, W1s, stage, D, DFF)
        load_weight_bf16(p, W2, W2s, stage, DFF, D)
        load_row_bf16(p, b1, b1_row, rowstage, DFF)
        load_row_bf16(p, b2, b2_row, rowstage, D)
        p.dma("sync", lambda e: e.dma_start(grow[:1, :], ln2_g[None, :]))
        bcast_row(p, C, grow, g2bc, ps_bc, D)
        p.dma("sync", lambda e: e.dma_start(grow[:1, :], ln2_b[None, :]))
        bcast_row(p, C, grow, b2bc, ps_bc, D)

        for j in range(NT):
            p.dma("sync", lambda e, j=j: e.dma_start(
                x1t[:], x1_sp[j * 128:(j + 1) * 128, :]))
            p.eng("vector", lambda e: e.tensor_copy(x1bf[:], x1t[:]))
            tr8(p, C, x1bf, x1T, ps_tr)
            for half in range(4):
                matmul_tm(p, C, x1T, W1s, DFF, ps_h, 2 * half, 2, 8,
                          bias_row=b1_row)
                p.eng("scalar", lambda e, half=half: e.activation(
                    h_bf[:, half * 1024:(half + 1) * 1024], ps_h[:],
                    AF.Relu))
            for q in range(4):
                tr8(p, C, h_bf[:, q * 1024:(q + 1) * 1024],
                    hT[:, q * 1024:(q + 1) * 1024], ps_tr)
            matmul_tm(p, C, hT, W2s, D, ps_y, 0, 2, 32, bias_row=b2_row)
            def res(e):
                e.tensor_tensor(t2[:, 0:512], ps_y[:, 0:512], x1t[:, 0:512],
                                ALU.add)
                return e.tensor_tensor(t2[:, 512:1024], ps_y[:, 512:1024],
                                       x1t[:, 512:1024], ALU.add)
            p.eng("vector", res)
            ln_stats(p, t2, D, st, C.eps_ln)
            ln_apply_gb(p, t2, so, st, g2bc, b2bc, tmp)
            p.dma("sync", lambda e, j=j: e.dma_start(
                xin[:], xs[j * 128:(j + 1) * 128, :]))
            p.eng("vector", lambda e: e.tensor_tensor(arg[:], so[:], xin[:],
                                                      ALU.add))
            ln_stats(p, arg, D, st, C.eps_ln)
            ln_apply_plain(p, arg, outt, st)
            p.dma("gpsimd", lambda e, j=j: e.dma_start(
                ys[j * 128:(j + 1) * 128, :], outt[:]))
        p.emit()
    return ["l1a1", "l1a2"]


def build_l1_ffn(b, C, sctx, xs, ys):
    from contextlib import ExitStack
    nc = b.nc
    router_w = b.inp("L1_ffn_router_w", [E1, D])
    my_rg = b.inp("aux_my_rg", [D, I1R])
    my_ru = b.inp("aux_my_ru", [D, I1R])
    my_rd = b.inp("aux_my_rd", [I1R, D])
    sg = b.inp("L1_ffn_sg", [D, I1S])
    su = b.inp("L1_ffn_su", [D, I1S])
    sd = b.inp("L1_ffn_sd", [I1S, D])
    pm_in = b.inp("aux_pm", [E1, 1])

    xT_sp = b.dram("l1f_xT", [TOK, D], BF16)
    xbf_sp = b.dram("l1f_xbf", [TOK, D], BF16)
    g_sp = b.dram("l1f_g", [TOK, I1S], BF16)
    h_sp = b.dram("l1f_h", [TOK, I1S], BF16)
    hR_sp = b.dram("l1f_hR", [CAP, I1R], BF16)
    comp_d = b.dram("l1f_comp", [E1 * CAP, D], BF16)
    mycomp = b.dram("l1f_mycomp", [CAP, D], BF16)
    contrib = b.dram("l1f_contrib", [CAP, D], BF16)
    ag_contrib = b.dram("l1f_agc", [E1 * CAP, D], BF16)
    cnt_d = b.dram("l1f_cnt", [E1, 1], F32)
    cnt_ag = b.dram("l1f_cntag", [NCORE * E1, 1], F32)

    weff_em = sctx.enter_context(nc.sbuf_tensor("l1f_weff", [128, E1 * NT], F32))
    idx_em = sctx.enter_context(nc.sbuf_tensor("l1f_idx", [128, E1 * NT], U32))
    idx2_em = sctx.enter_context(nc.sbuf_tensor("l1f_idx2", [128, E1 * NT], U32))

    EM = lambda t, e0: t[:, e0 * NT:(e0 + 1) * NT]

    with ExitStack() as actx:
        sb = lambda n, s, d=F32: actx.enter_context(nc.sbuf_tensor(n, s, d))
        ps = lambda n, s, d=F32: actx.enter_context(nc.psum_tensor(n, s, d))
        p = Pass(b, "l1f0")
        rwT = sb("r0_rwT", [128, 8 * E1])
        rstage = sb("r0_rstage", [E1, D])
        xin = sb("r0_xin", [128, D])
        x_bf = sb("r0_xbf", [128, D], BF16)
        xT = sb("r0_xT", [128, D], BF16)
        xnf = sb("r0_xnf", [128, D])
        xnT = sb("r0_xnT", [128, D])
        lgs_tm = sb("r0_lgs", [128, NT * E1])
        st = {k: sb("r0_st_" + k, [128, 1]) for k in ["ssq", "rstd"]}
        scr = sb("r0_scr", [128, D])
        ps_tr = ps("r0_ptr", [128, 1024], BF16)
        ps_trf = ps("r0_ptrf", [128, 1024])
        ps_r = ps("r0_pr", [128, E1])
        ps_g8 = ps("r0_pg8", [128, 8 * E1])

        p.dma("sync", lambda e: e.dma_start(rstage[:, :], router_w[:, :]))
        def rtr(e):
            ins = None
            for d in range(8):
                ins = e.transpose(ps_g8[:, d * E1:(d + 1) * E1],
                                  rstage[:, d * 128:(d + 1) * 128],
                                  C.ident_f[0:E1, 0:E1])
            return ins
        p.eng("tensor", rtr)
        p.eng("vector", lambda e: e.tensor_copy(rwT[:, :], ps_g8[:, :]))

        for j in range(NT):
            p.dma("sync", lambda e, j=j: e.dma_start(
                xin[:], xs[j * 128:(j + 1) * 128, :]))
            p.eng("vector", lambda e: e.tensor_copy(x_bf[:], xin[:]))
            tr8(p, C, x_bf, xT, ps_tr)
            p.dma("gpsimd", lambda e, j=j: e.dma_start(
                xbf_sp[j * 128:(j + 1) * 128, :], x_bf[:]))
            p.dma("gpsimd", lambda e, j=j: e.dma_start(
                xT_sp[j * 128:(j + 1) * 128, :], xT[:]))
            p.eng("scalar", lambda e: e.activation(scr[:], xin[:], AF.Square))
            p.eng("vector", lambda e: e.reduce_sum(st["ssq"][:], scr[:], AX.X))
            def rstd_fn(e):
                e.activation(st["rstd"][:], st["ssq"][:], AF.Ln,
                             bias=C.eps_rms[:], scale=1.0 / D)
                return e.activation(st["rstd"][:], st["rstd"][:], AF.Exp,
                                    scale=-0.5)
            p.eng("scalar", rstd_fn)
            p.eng("vector", lambda e: e.tensor_scalar(
                xnf[:], xin[:], st["rstd"][:], None, ALU.mult))
            tr8(p, C, xnf, xnT, ps_trf)
            def rmm(e):
                ins = None
                for d in range(8):
                    ins = e.matmul(ps_r[:, :], xnT[:, d * 128:(d + 1) * 128],
                                   rwT[:, d * E1:(d + 1) * E1],
                                   start=(d == 0), stop=(d == 7))
                return ins
            p.eng("tensor", rmm)
            p.eng("vector", lambda e, j=j: e.tensor_copy(
                lgs_tm[:, j * E1:(j + 1) * E1], ps_r[:, :]))

        maxs = sb("r0_maxs", [128, NT])
        esub = sb("r0_esub", [128, NT * E1])
        ep_tm = sb("r0_eptm", [128, NT * E1])
        ep_em = sb("r0_epem", [128, E1 * NT])
        cnt_em = sb("r0_cnt", [128, E1 * NT])
        sel_em = sb("r0_sel", [128, E1 * NT])
        wsel = sb("r0_wsel", [128, E1 * NT])
        wsum = sb("r0_wsum", [128, NT])
        recipw = sb("r0_recipw", [128, NT])
        rw_em = sb("r0_rw", [128, E1 * NT])
        tmpb = sb("r0_tmpb", [128, NT])
        colsums = sb("r0_cols", [128, E1])
        cnt8 = sb("r0_cnt8", [E1, 1])
        offs_row = sb("r0_offs", [1, E1])
        offs_bc = sb("r0_offsbc", [128, E1])
        cntl = sb("r0_cntl", [E1, E1])
        pm_sb = sb("r0_pm", [E1, 1])
        selT = sb("r0_selT", [32, 128])
        zeros32 = sb("r0_z32", [32, 128])
        rsT = sb("r0_rsT", [32, 128])
        totrow = sb("r0_totrow", [1, 32])
        inclrow = sb("r0_incl", [1, 32])
        exclcol = sb("r0_excl", [32, 1])
        rank_b = sb("r0_rankb", [128, NT])
        grank_b = sb("r0_grankb", [128, NT])
        valid_b = sb("r0_validb", [128, NT])
        sv_b = sb("r0_svb", [128, NT])
        inv_b = sb("r0_invb", [128, NT])
        idxf_b = sb("r0_idxfb", [128, NT])
        ps_sm = ps("r0_psm", [128, 512])
        ps_t32 = ps_sm[0:32, 0:128]
        ps_t128 = ps_sm[:, 0:32]
        ps_r1 = ps_sm[0:1, 128:160]
        ps_c1 = ps_sm[0:32, 160:161]
        ps_off = ps_sm[0:1, 168:168 + E1]
        ps_obc = ps_sm[:, 176:176 + E1]

        TM = lambda t, e0: t[:, e0::E1]
        p.eng("vector", lambda e: e.reduce_max(
            maxs[:], lgs_tm[:, :].rearrange("p (j e) -> p j e", e=E1), AX.X))
        def esubf(e):
            ins = None
            for e0 in range(E1):
                ins = e.tensor_tensor(TM(esub, e0), TM(lgs_tm, e0), maxs[:],
                                      ALU.subtract)
            return ins
        p.eng("vector", esubf)
        p.eng("scalar", lambda e: e.activation(ep_tm[:, :], esub[:, :], AF.Exp))
        p.eng("vector", lambda e: e.memset(cnt_em[:, :], 0.0))
        for e0 in range(E1):
            def erank(e, e0=e0):
                ins = None
                for ep in range(E1):
                    if ep == e0:
                        continue
                    op = ALU.is_ge if ep < e0 else ALU.is_gt
                    e.tensor_tensor(tmpb[:, :], TM(ep_tm, ep), TM(ep_tm, e0),
                                    op)
                    ins = e.tensor_tensor(EM(cnt_em, e0), EM(cnt_em, e0),
                                          tmpb[:, :], ALU.add)
                return ins
            p.eng("vector", erank)
        p.eng("vector", lambda e: e.tensor_single_scalar(
            sel_em[:, :], cnt_em[:, :], 2.0, ALU.is_lt))
        def epem(e):
            ins = None
            for e0 in range(E1):
                ins = e.tensor_copy(EM(ep_em, e0), TM(ep_tm, e0))
            return ins
        p.eng("vector", epem)
        p.eng("vector", lambda e: e.tensor_tensor(wsel[:, :], sel_em[:, :],
                                                  ep_em[:, :], ALU.mult))
        def wsumf(e):
            e.tensor_tensor(wsum[:, :], EM(wsel, 0), EM(wsel, 1), ALU.add)
            ins = None
            for e0 in range(2, E1):
                ins = e.tensor_tensor(wsum[:, :], wsum[:, :], EM(wsel, e0),
                                      ALU.add)
            return ins
        p.eng("vector", wsumf)
        p.eng("vector", lambda e: e.reciprocal(recipw[:, :], wsum[:, :]))
        def rwf(e):
            ins = None
            for e0 in range(E1):
                ins = e.tensor_tensor(EM(rw_em, e0), EM(wsel, e0),
                                      recipw[:, :], ALU.mult)
            return ins
        p.eng("vector", rwf)
        def colsf(e):
            ins = None
            for e0 in range(E1):
                ins = e.reduce_sum(colsums[:, e0:e0 + 1], EM(sel_em, e0),
                                   AX.X)
            return ins
        p.eng("vector", colsf)
        p.eng("tensor", lambda e: e.transpose(
            ps_t32[0:E1, :], colsums[:, :], C.ident_f[:, :]))
        p.eng("vector", lambda e: e.reduce_sum(cnt8[:, :], ps_t32[0:E1, :],
                                               AX.X))
        p.dma("gpsimd", lambda e: e.dma_start(cnt_d[:, :], cnt8[:, :]))
        p.coll(lambda e: e.collective_compute(
            "AllGather", ALU.bypass, [list(range(NCORE))],
            [cnt_d[:, :]], [cnt_ag[:, :]]))
        p.dma("sync", lambda e: e.dma_start(
            cntl[:, :], cnt_ag.rearrange("(c e) one -> c (e one)", e=E1)))
        p.dma("sync", lambda e: e.dma_start(pm_sb[:, :], pm_in[:, :]))
        p.eng("tensor", lambda e: e.matmul(ps_off[:, :], pm_sb[:, :],
                                           cntl[:, :], start=True, stop=True))
        p.eng("vector", lambda e: e.tensor_copy(offs_row[:, :], ps_off[:, :]))
        p.eng("tensor", lambda e: e.matmul(ps_obc[:, :], C.ones_colf[:1, :],
                                           offs_row[:1, :], start=True,
                                           stop=True))
        p.eng("vector", lambda e: e.tensor_copy(offs_bc[:, :], ps_obc[:, :]))
        p.eng("vector", lambda e: e.memset(zeros32[:, :], 0.0))
        for e0 in range(E1):
            p.eng("tensor", lambda e, e0=e0: e.transpose(
                ps_t32[:, :], EM(sel_em, e0), C.ident_f[:, :]))
            p.eng("vector", lambda e: e.tensor_copy(selT[:, :], ps_t32[:, :]))
            p.eng("vector", lambda e: e.tensor_tensor_scan(
                rsT[:, :], zeros32[:, :], selT[:, :], 0.0, ALU.add, ALU.add))
            p.eng("tensor", lambda e: e.transpose(
                ps_r1[:, :], rsT[:, 127:128], C.ident_f[0:32, 0:32]))
            p.eng("vector", lambda e: e.tensor_copy(totrow[:, :], ps_r1[:, :]))
            p.eng("vector", lambda e: e.tensor_tensor_scan(
                inclrow[:, :], zeros32[0:1, 0:32], totrow[:, :], 0.0,
                ALU.add, ALU.add))
            p.eng("vector", lambda e: e.tensor_tensor(
                inclrow[:, :], inclrow[:, :], totrow[:, :], ALU.subtract))
            p.eng("tensor", lambda e: e.transpose(
                ps_c1[:, :], inclrow[:, :], C.ident_f[0:1, 0:1]))
            p.eng("vector", lambda e: e.tensor_copy(exclcol[:, :],
                                                    ps_c1[:, :]))
            p.eng("vector", lambda e: e.tensor_scalar(
                rsT[:, :], rsT[:, :], exclcol[:, :], None, ALU.add))
            p.eng("tensor", lambda e: e.transpose(
                ps_t128[:, :], rsT[:, :], C.ident_f[0:32, 0:32]))
            p.eng("vector", lambda e: e.tensor_copy(rank_b[:, :],
                                                    ps_t128[:, :]))
            p.eng("vector", lambda e, e0=e0: e.tensor_tensor(
                grank_b[:, :], rank_b[:, :], EM(sel_em, e0), ALU.subtract))
            p.eng("vector", lambda e, e0=e0: e.tensor_scalar(
                grank_b[:, :], grank_b[:, :], offs_bc[:, e0:e0 + 1], None,
                ALU.add))
            p.eng("vector", lambda e: e.tensor_single_scalar(
                valid_b[:, :], grank_b[:, :], float(CAP), ALU.is_lt))
            p.eng("vector", lambda e, e0=e0: e.tensor_tensor(
                sv_b[:, :], EM(sel_em, e0), valid_b[:, :], ALU.mult))
            p.eng("vector", lambda e, e0=e0: e.tensor_tensor(
                EM(weff_em, e0), EM(rw_em, e0), sv_b[:, :], ALU.mult))
            p.eng("vector", lambda e: e.tensor_tensor(
                idxf_b[:, :], grank_b[:, :], sv_b[:, :], ALU.mult))
            p.eng("vector", lambda e: e.tensor_scalar(
                inv_b[:, :], sv_b[:, :], -1e9, 1e9, ALU.mult, ALU.add))
            p.eng("vector", lambda e: e.tensor_tensor(
                idxf_b[:, :], idxf_b[:, :], inv_b[:, :], ALU.add))
            p.eng("vector", lambda e, e0=e0: e.tensor_copy(
                EM(idx_em, e0), idxf_b[:, :]))
            p.eng("vector", lambda e, e0=e0: e.tensor_scalar(
                idxf_b[:, :], idxf_b[:, :], float(e0 * CAP), None, ALU.add))
            p.eng("vector", lambda e, e0=e0: e.tensor_copy(
                EM(idx2_em, e0), idxf_b[:, :]))
        p.emit()

    with ExitStack() as actx:
        sb = lambda n, s, d=F32: actx.enter_context(nc.sbuf_tensor(n, s, d))
        ps = lambda n, s, d=F32: actx.enter_context(nc.psum_tensor(n, s, d))
        p = Pass(b, "l1f2a")
        bcreg = {}
        def prelude(g):
            r = g.alloc_register("bcreg2a")
            g.reg_mov(r, E1 * CAP - 1)
            bcreg["r"] = r
        p.gpsimd_prelude = prelude
        zz = sb("r2_zz", [128, 4096], BF16)
        xbt = sb("r2_xbt", [128, D], BF16)
        Rg = sb("r2_rg", [128, 8 * I1R], BF16)
        Ru = sb("r2_ru", [128, 8 * I1R], BF16)
        stage = sb("r2_stage", [128, 1024])
        xgT = sb("r2_xgT", [128, D], BF16)
        gs_bf = sb("r2_gs", [128, 512], BF16)
        hT_t = sb("r2_hT", [128, I1R], BF16)
        ps_tr = ps("r2_ptr", [128, 1024], BF16)
        ps_g = ps("r2_pg", [128, 512])
        ps_u = ps("r2_pu", [128, 512])

        p.eng("vector", lambda e: e.memset(zz[:, :], 0.0))
        for n in range(0, E1 * CAP // 128, 4):
            p.dma("gpsimd", lambda e, n=n: e.dma_start(
                comp_d.rearrange("(m p) d -> p m d", p=128)[:, n:n + 4, :],
                zz[:, :].rearrange("p (m d) -> p m d", d=D)))
        for j in range(NT):
            p.dma("sync", lambda e, j=j: e.dma_start(
                xbt[:], xbf_sp[j * 128:(j + 1) * 128, :]))
            for e0 in range(E1):
                p.dma("gpsimd", lambda e, j=j, e0=e0: e.indirect_dma_start(
                    comp_d[:, :],
                    bass.IndirectOffsetOnAxis(
                        idx2_em[:, e0 * NT + j:e0 * NT + j + 1], 0),
                    xbt[:, :], None,
                    bounds_check=bcreg["r"], oob_is_err=False))
        p.coll(lambda e: e.collective_compute(
            "ReduceScatter", ALU.add, [list(range(NCORE))],
            [comp_d[:, :]], [mycomp[:, :]]))
        load_weight_bf16(p, my_rg, Rg, stage, D, I1R)
        load_weight_bf16(p, my_ru, Ru, stage, D, I1R)
        for t in range(CAP // 128):
            p.dma("sync", lambda e, t=t: e.dma_start(
                xbt[:], mycomp[t * 128:(t + 1) * 128, :]))
            tr8(p, C, xbt, xgT, ps_tr)
            for fg in range(8):
                def gumm(e, fg=fg):
                    ins = None
                    for d in range(8):
                        xTd = xgT[:, d * 128:(d + 1) * 128]
                        ins = e.matmul(ps_g[:, :], xTd,
                                       Rg[:, d * I1R + fg * 512:
                                          d * I1R + fg * 512 + 512],
                                       start=(d == 0), stop=(d == 7))
                        ins = e.matmul(ps_u[:, :], xTd,
                                       Ru[:, d * I1R + fg * 512:
                                          d * I1R + fg * 512 + 512],
                                       start=(d == 0), stop=(d == 7))
                    return ins
                p.eng("tensor", gumm)
                p.eng("scalar", lambda e: e.activation(gs_bf[:], ps_g[:],
                                                       AF.Silu))
                p.eng("vector", lambda e: e.tensor_tensor(
                    gs_bf[:, :], ps_u[:], gs_bf[:], ALU.mult))
                def htr(e, fg=fg):
                    ins = None
                    for k in range(4):
                        ins = e.transpose(
                            ps_tr[:, k * 128:(k + 1) * 128],
                            gs_bf[:, k * 128:(k + 1) * 128],
                            C.ident_bf[:, :])
                    return ins
                p.eng("tensor", htr)
                p.eng("vector", lambda e, fg=fg: e.tensor_copy(
                    hT_t[:, fg * 512:(fg + 1) * 512], ps_tr[:, 0:512]))
            p.dma("gpsimd", lambda e, t=t: e.dma_start(
                hR_sp[t * 128:(t + 1) * 128, :], hT_t[:]))
        p.emit()

    with ExitStack() as actx:
        sb = lambda n, s, d=F32: actx.enter_context(nc.sbuf_tensor(n, s, d))
        ps = lambda n, s, d=F32: actx.enter_context(nc.psum_tensor(n, s, d))
        p = Pass(b, "l1f2b")
        Rd = sb("r2b_rd", [128, 32 * D], BF16)
        stage = sb("r2b_stage", [128, 1024])
        hT_t = sb("r2b_hT", [128, I1R], BF16)
        ctile = sb("r2b_ct", [128, D], BF16)
        ps_y = ps("r2b_py", [128, 2 * 512])
        load_weight_bf16(p, my_rd, Rd, stage, I1R, D)
        for t in range(CAP // 128):
            p.dma("sync", lambda e, t=t: e.dma_start(
                hT_t[:], hR_sp[t * 128:(t + 1) * 128, :]))
            def ymm(e):
                ins = None
                for i in range(32):
                    for fg in range(2):
                        ins = e.matmul(
                            ps_y[:, fg * 512:(fg + 1) * 512],
                            hT_t[:, i * 128:(i + 1) * 128],
                            Rd[:, i * D + fg * 512:i * D + fg * 512 + 512],
                            start=(i == 0), stop=(i == 31))
                return ins
            p.eng("tensor", ymm)
            p.eng("vector", lambda e: e.tensor_copy(ctile[:], ps_y[:]))
            p.dma("gpsimd", lambda e, t=t: e.dma_start(
                contrib[t * 128:(t + 1) * 128, :], ctile[:]))
        p.coll(lambda e: e.collective_compute(
            "AllGather", ALU.bypass, [list(range(NCORE))],
            [contrib[:, :]], [ag_contrib[:, :]]))
        p.emit()

    with ExitStack() as actx:
        sb = lambda n, s, d=F32: actx.enter_context(nc.sbuf_tensor(n, s, d))
        ps = lambda n, s, d=F32: actx.enter_context(nc.psum_tensor(n, s, d))
        p = Pass(b, "l1f3")
        Sg = sb("r3_sg", [128, 8 * I1S], BF16)
        stage = sb("r3_stage", [128, 1024])
        xT = sb("r3_xT", [128, D], BF16)
        g_t = sb("r3_gt", [128, I1S], BF16)
        ps_g = ps("r3_pg", [128, 512])
        load_weight_bf16(p, sg, Sg, stage, D, I1S)
        for j in range(NT):
            p.dma("sync", lambda e, j=j: e.dma_start(
                xT[:], xT_sp[j * 128:(j + 1) * 128, :]))
            for fg in range(16):
                def gmm(e, fg=fg):
                    ins = None
                    for d in range(8):
                        ins = e.matmul(ps_g[:, :],
                                       xT[:, d * 128:(d + 1) * 128],
                                       Sg[:, d * I1S + fg * 512:
                                          d * I1S + fg * 512 + 512],
                                       start=(d == 0), stop=(d == 7))
                    return ins
                p.eng("tensor", gmm)
                p.eng("scalar", lambda e, fg=fg: e.activation(
                    g_t[:, fg * 512:(fg + 1) * 512], ps_g[:], AF.Silu))
            p.dma("gpsimd", lambda e, j=j: e.dma_start(
                g_sp[j * 128:(j + 1) * 128, :], g_t[:]))
        p.emit()

    with ExitStack() as actx:
        sb = lambda n, s, d=F32: actx.enter_context(nc.sbuf_tensor(n, s, d))
        ps = lambda n, s, d=F32: actx.enter_context(nc.psum_tensor(n, s, d))
        p = Pass(b, "l1f4")
        Su = sb("r4_su", [128, 8 * I1S], BF16)
        stage = sb("r4_stage", [128, 1024])
        xT = sb("r4_xT", [128, D], BF16)
        g_t = sb("r4_gt", [128, I1S], BF16)
        h_bf = sb("r4_h", [128, 512], BF16)
        hT_t = sb("r4_hT", [128, I1S], BF16)
        ps_u = ps("r4_pu", [128, 512])
        ps_tr = ps("r4_ptr", [128, 512], BF16)
        load_weight_bf16(p, su, Su, stage, D, I1S)
        for j in range(NT):
            p.dma("sync", lambda e, j=j: e.dma_start(
                xT[:], xT_sp[j * 128:(j + 1) * 128, :]))
            p.dma("sync", lambda e, j=j: e.dma_start(
                g_t[:], g_sp[j * 128:(j + 1) * 128, :]))
            for fg in range(16):
                def umm(e, fg=fg):
                    ins = None
                    for d in range(8):
                        ins = e.matmul(ps_u[:, :],
                                       xT[:, d * 128:(d + 1) * 128],
                                       Su[:, d * I1S + fg * 512:
                                          d * I1S + fg * 512 + 512],
                                       start=(d == 0), stop=(d == 7))
                    return ins
                p.eng("tensor", umm)
                p.eng("vector", lambda e, fg=fg: e.tensor_tensor(
                    h_bf[:, :], ps_u[:], g_t[:, fg * 512:(fg + 1) * 512],
                    ALU.mult))
                def htr(e):
                    ins = None
                    for k in range(4):
                        ins = e.transpose(ps_tr[:, k * 128:(k + 1) * 128],
                                          h_bf[:, k * 128:(k + 1) * 128],
                                          C.ident_bf[:, :])
                    return ins
                p.eng("tensor", htr)
                p.eng("vector", lambda e, fg=fg: e.tensor_copy(
                    hT_t[:, fg * 512:(fg + 1) * 512], ps_tr[:, 0:512]))
            p.dma("gpsimd", lambda e, j=j: e.dma_start(
                h_sp[j * 128:(j + 1) * 128, :], hT_t[:]))
        p.emit()

    with ExitStack() as actx:
        sb = lambda n, s, d=F32: actx.enter_context(nc.sbuf_tensor(n, s, d))
        ps = lambda n, s, d=F32: actx.enter_context(nc.psum_tensor(n, s, d))
        p = Pass(b, "l1f5")
        bcreg = {}
        def prelude(g):
            r = g.alloc_register("bcreg5")
            g.reg_mov(r, E1 * CAP - 1)
            bcreg["r"] = r
        p.gpsimd_prelude = prelude
        Sd = sb("r5_sd", [128, 64 * D], BF16)
        stage = sb("r5_stage", [128, 1024])
        hT_t = sb("r5_hT", [128, I1S], BF16)
        gath = sb("r5_gath", [128, D], BF16)
        acc = sb("r5_acc", [128, D])
        xin = sb("r5_xin", [128, D])
        arg = sb("r5_arg", [128, D])
        outt = sb("r5_out", [128, D])
        st = {k: sb("r5_st_" + k, [128, 1]) for k in
              ["ssq", "ssum", "mu", "var", "rstd"]}
        st["scr"] = sb("r5_scr", [128, D])
        ps_y = ps("r5_py", [128, 2 * 512])
        load_weight_bf16(p, sd, Sd, stage, I1S, D)
        p.eng("vector", lambda e: e.memset(gath[:, :], 0.0))
        for j in range(NT):
            p.dma("sync", lambda e, j=j: e.dma_start(
                hT_t[:], h_sp[j * 128:(j + 1) * 128, :]))
            def ymm(e):
                ins = None
                for i in range(64):
                    for fg in range(2):
                        ins = e.matmul(
                            ps_y[:, fg * 512:(fg + 1) * 512],
                            hT_t[:, i * 128:(i + 1) * 128],
                            Sd[:, i * D + fg * 512:i * D + fg * 512 + 512],
                            start=(i == 0), stop=(i == 63))
                return ins
            p.eng("tensor", ymm)
            p.eng("vector", lambda e: e.tensor_copy(acc[:], ps_y[:]))
            for e0 in range(E1):
                p.dma("gpsimd", lambda e, j=j, e0=e0: e.indirect_dma_start(
                    gath[:, :], None, ag_contrib[:, :],
                    bass.IndirectOffsetOnAxis(
                        idx2_em[:, e0 * NT + j:e0 * NT + j + 1], 0),
                    bounds_check=bcreg["r"], oob_is_err=False))
                p.eng("vector", lambda e, j=j, e0=e0: e.scalar_tensor_tensor(
                    acc[:, :], gath[:, :],
                    weff_em[:, e0 * NT + j:e0 * NT + j + 1], acc[:, :],
                    ALU.mult, ALU.add))
            p.dma("sync", lambda e, j=j: e.dma_start(
                xin[:], xs[j * 128:(j + 1) * 128, :]))
            p.eng("vector", lambda e: e.tensor_tensor(arg[:], acc[:], xin[:],
                                                      ALU.add))
            ln_stats(p, arg, D, st, C.eps_ln)
            ln_apply_plain(p, arg, outt, st)
            p.dma("gpsimd", lambda e, j=j: e.dma_start(
                ys[j * 128:(j + 1) * 128, :], outt[:]))
        p.emit()
    return ["l1f0", "l1f2a", "l1f2b", "l1f3", "l1f4", "l1f5"]


def build_program(slots, fused):
    from contextlib import ExitStack
    nc = bass.Bass("TRN2", target_bir_lowering=False)
    b = Builder(nc)
    xs_ext = b.inp("xs", [TOK, D], F32)
    ys_ext = b.out("ys", [TOK, D], F32)
    bufs = [xs_ext]
    for i in range(len(slots) - 1):
        bufs.append(b.dram(f"stream{i}", [TOK, D], F32))
    bufs.append(ys_ext)

    with ExitStack() as sctx:
        C, initp = setup_common(b, sctx)
        initp.emit()
        for i, s in enumerate(slots):
            src, dst = bufs[i], bufs[i + 1]
            if s == "l0_attn":
                build_l0_attn(b, C, sctx, src, dst)
            elif s == "l0_ffn":
                build_l0_ffn(b, C, sctx, src, dst)
            elif s == "l1_attn":
                build_l1_attn(b, C, sctx, src, dst)
            elif s == "l1_ffn":
                build_l1_ffn(b, C, sctx, src, dst)
            else:
                raise ValueError(s)
        if DEBUG_DUMPS:
            p = Pass(b, "dbgdump")
            for name in DEBUG_DUMPS:
                ap = b.drams[name]
                o = b.out("dump_" + name, ap.shape, ap.dtype)
                p.dma("gpsimd", lambda e, o=o, ap=ap: e.dma_start(o[:, :],
                                                                 ap[:, :]))
            p.steps.append(("sync", lambda e: e.nop(), "eng"))
            p.emit()
    return nc, b


def build_maskcap(core):
    out = np.empty((6, 128, 768), np.float32)
    col = np.arange(768)[None, :]
    pp = np.arange(128)[:, None]
    for s, j in enumerate([0, 1, 2, 3, 30, 31]):
        pj = j % 2
        rel = (col // 128 - 2 - pj) * 128 + (col % 128) - pp
        valid = np.abs(rel) <= W1W
        if j <= 1 and core % 2 == 0:
            valid = valid & (col >= 256)
        if j >= 30 and core % 2 == 1:
            valid = valid & (col < 512)
        out[s] = np.where(valid, 1e30, -1e9)
    return np.ascontiguousarray(out.reshape(6 * 128, 768).astype(np.float32))


def aux_inputs(core, inputs):
    bsel = np.zeros((128, 4), np.float32)
    bsel[:, core // 2] = 1.0
    lsel = np.zeros((128, NCORE), np.float32)
    rsel = np.zeros((128, NCORE), np.float32)
    if core % 2 == 1:
        lsel[:, core - 1] = 1.0
    else:
        rsel[:, core + 1] = 1.0
    pm = np.zeros((E1, 1), np.float32)
    pm[:core, 0] = 1.0
    out = {"aux_bsel": bsel, "aux_lsel": lsel, "aux_rsel": rsel,
           "aux_maskcap": build_maskcap(core), "aux_pm": pm}
    if "L1_ffn_rg" in inputs:
        out["aux_my_rg"] = np.ascontiguousarray(
            np.asarray(inputs["L1_ffn_rg"][core], np.float32))
        out["aux_my_ru"] = np.ascontiguousarray(
            np.asarray(inputs["L1_ffn_ru"][core], np.float32))
        out["aux_my_rd"] = np.ascontiguousarray(
            np.asarray(inputs["L1_ffn_rd"][core], np.float32))
    return out


def shard_rows(x):
    out = []
    for c in range(NCORE):
        bidx, h = c // 2, c % 2
        out.append(np.ascontiguousarray(
            x[bidx, h * TOK:(h + 1) * TOK, :]))
    return out


def kernel(**inputs):
    x = np.asarray(inputs["L0_attn_src"], np.float32)
    shards = shard_rows(x)

    slots = SLOTS
    nc, b = build_program(slots, fused=True)

    in_maps = []
    for c in range(NCORE):
        aux = aux_inputs(c, inputs)
        m = {"xs": shards[c]}
        for name in b.inputs:
            if name in aux:
                m[name] = aux[name]
            elif name in inputs:
                m[name] = np.ascontiguousarray(
                    np.asarray(inputs[name], np.float32))
        in_maps.append(m)
    res = run_bass_kernel_spmd(nc, in_maps, core_ids=list(range(NCORE)))
    out = np.empty((4, 8192, D), np.float32)
    for c in range(NCORE):
        bidx, h = c // 2, c % 2
        out[bidx, h * TOK:(h + 1) * TOK, :] = res.results[c]["ys"]
    return out


# version 04
# speedup vs baseline: 1.3156x; 1.1304x over previous
import numpy as np
import ml_dtypes
import concourse.bass as bass
import concourse.mybir as mybir
from concourse.bass_utils import run_bass_kernel_spmd

F32 = mybir.dt.float32
BF16 = mybir.dt.bfloat16
U32 = mybir.dt.uint32
AF = mybir.ActivationFunctionType
ALU = mybir.AluOpType
AX = mybir.AxisListType

NCORE = 8
D = 1024
NT = 32
TOK = NT * 128
NH = 16
HD = 64
DFF = 4096
LN_EPS = 1e-5
ATTN_EPS = 1e-6
RMS_EPS = float(np.finfo(np.float32).eps)
E0 = 8
I0 = 512
SCALE0 = 2.5
E1 = 8
I1R = 4096
I1S = 8192
CAP = 5120
W1W = 256

NEGINF = np.float32(-1e9)
BIGCAP = np.float32(1e30)

PASS_FILTER = None
DEBUG_DUMPS = []
STEP_LIMIT = {}
SLOTS = ["l0_attn", "l0_ffn", "l1_attn", "l1_ffn"]


class Pass:

    def __init__(self, bld, name):
        self.b = bld
        self.name = name
        self.steps = []
        self.gpsimd_prelude = None
        self._last_w = {}
        self._readers = {}

    def _auto_deps(self, k, reads, writes):
        deps = set()
        for r in reads:
            if r in self._last_w:
                deps.add(self._last_w[r])
        for w in writes:
            deps.update(self._readers.get(w, ()))
            if w in self._last_w:
                deps.add(self._last_w[w])
        for r in reads:
            self._readers.setdefault(r, set()).add(k)
        for w in writes:
            self._last_w[w] = k
            self._readers[w] = set()
        return sorted(deps)

    def _add(self, engine, fn, kind, deps, group=None, rw=None):
        k = len(self.steps)
        if rw is not None:
            deps = self._auto_deps(k, rw[0], rw[1])
        elif deps is None:
            deps = [k - 1] if k > 0 else []
        deps = [d for d in deps if d is not None and 0 <= d < k]
        self.steps.append(dict(engine=engine, fn=fn, kind=kind, deps=deps,
                               group=group))
        return k

    def eng(self, engine, fn, deps=None, rw=None):
        return self._add(engine, fn, "eng", deps, rw=rw)

    def dma(self, engine, fn, deps=None, group=0, rw=None):
        return self._add(engine, fn, "dma", deps, group, rw=rw)

    def coll(self, fn, deps=None, rw=None):
        return self._add("gpsimd", fn, "coll", deps, rw=rw)

    def emit(self):
        if PASS_FILTER is not None and self.name not in PASS_FILTER:
            return
        if self.name in STEP_LIMIT:
            lim = STEP_LIMIT[self.name]
            self.steps = self.steps[:lim]
            for s in self.steps:
                s["deps"] = [d for d in s["deps"] if d < lim]
        nc = self.b.nc
        from contextlib import ExitStack
        with ExitStack() as sx:
            engines = sorted({s["engine"] for s in self.steps
                              if s["kind"] != "dma"})
            groups = sorted({s["group"] for s in self.steps
                             if s["kind"] == "dma"})
            ch = {en: sx.enter_context(
                nc.semaphore(f"{self.name}_c_{en}")) for en in engines}
            dg = {g: sx.enter_context(
                nc.semaphore(f"{self.name}_g_{g}")) for g in groups}
            blk = sx.enter_context(nc.Block())

            eord = {}
            gord = {}
            ecnt = {en: 0 for en in engines}
            gcnt = {g: 0 for g in groups}
            for k, s in enumerate(self.steps):
                if s["kind"] == "dma":
                    gord[k] = gcnt[s["group"]]
                    gcnt[s["group"]] += 1
                else:
                    eord[k] = ecnt[s["engine"]]
                    ecnt[s["engine"]] += 1

            by_eng = {}
            for k, s in enumerate(self.steps):
                by_eng.setdefault(s["engine"], []).append(k)

            def make_body(items, en):
                def body(eng):
                    if en == "gpsimd" and self.gpsimd_prelude is not None:
                        self.gpsimd_prelude(eng)
                    for k in items:
                        s = self.steps[k]
                        waits = {}
                        for d in s["deps"]:
                            sd = self.steps[d]
                            if sd["kind"] == "dma":
                                key = ("g", sd["group"])
                                val = 16 * (gord[d] + 1)
                            else:
                                if sd["engine"] == en and s["kind"] != "dma":
                                    continue
                                if sd["engine"] == en and s["kind"] == "dma":
                                    continue
                                key = ("e", sd["engine"])
                                val = eord[d] + 1
                            if waits.get(key, 0) < val:
                                waits[key] = val
                        for (kindk, ident), val in waits.items():
                            sem = dg[ident] if kindk == "g" else ch[ident]
                            eng.wait_ge(sem, val)
                        if s["kind"] == "dma":
                            ins = s["fn"](eng)
                            ins.then_inc(dg[s["group"]], 16)
                        elif s["kind"] == "coll":
                            ins = s["fn"](eng)
                            ins.then_inc(ch[en], 1)
                        else:
                            ins = s["fn"](eng)
                            assert ins is not None, \
                                f"step {k} in {self.name} returned None"
                            eng.drain().then_inc(ch[en], 1)
                return body

            for en, items in by_eng.items():
                getattr(blk, en)(make_body(items, en))

            def tail(eng):
                for g in groups:
                    eng.wait_ge(dg[g], 16 * gcnt[g])
                for en in engines:
                    if en == "sync":
                        continue
                    eng.wait_ge(ch[en], ecnt[en])
            blk.sync(tail)


class Builder:
    def __init__(self, nc):
        self.nc = nc
        self.inputs = {}
        self.consts = {}
        self.drams = {}

    def inp(self, name, shape, dtype=F32):
        if name not in self.inputs:
            self.inputs[name] = self.nc.dram_tensor(
                name, list(shape), dtype, kind="ExternalInput").ap()
        return self.inputs[name]

    def out(self, name, shape, dtype=F32):
        return self.nc.dram_tensor(name, list(shape), dtype,
                                   kind="ExternalOutput").ap()

    def dram(self, name, shape, dtype=F32):
        ap = self.nc.dram_tensor(name, list(shape), dtype,
                                 kind="Internal").ap()
        self.drams[name] = ap
        return ap

    def const(self, name, arr):
        if name not in self.consts:
            self.consts[name] = self.nc.inline_tensor(
                np.ascontiguousarray(arr), name=name).ap()
        return self.consts[name]


class Common:
    pass


def setup_common(b, ctx):
    nc = b.nc
    C = Common()
    C.ident_bf = ctx.enter_context(nc.sbuf_tensor("c_identbf", [128, 128], BF16))
    C.ident_f = ctx.enter_context(nc.sbuf_tensor("c_identf", [128, 128], F32))
    C.ones_col = ctx.enter_context(nc.sbuf_tensor("c_onescol", [1, 128], BF16))
    C.ones_t = ctx.enter_context(nc.sbuf_tensor("c_onest", [128, 1], BF16))
    C.ones_colf = ctx.enter_context(nc.sbuf_tensor("c_onescolf", [1, 128], F32))
    C.eps_ln = ctx.enter_context(nc.sbuf_tensor("c_epsln", [128, 1], F32))
    C.eps_rms = ctx.enter_context(nc.sbuf_tensor("c_epsrms", [128, 1], F32))
    idb = b.const("identbf", np.eye(128, dtype=ml_dtypes.bfloat16))
    idf = b.const("identf", np.eye(128, dtype=np.float32))

    p = Pass(b, "init")
    p.dma("sync", lambda e: e.dma_start(C.ident_bf[:], idb[:]))
    p.dma("sync", lambda e: e.dma_start(C.ident_f[:], idf[:]))
    p.eng("vector", lambda e: e.memset(C.ones_col[:], 1.0))
    p.eng("vector", lambda e: e.memset(C.ones_t[:], 1.0))
    p.eng("vector", lambda e: e.memset(C.ones_colf[:], 1.0))
    p.eng("vector", lambda e: e.memset(C.eps_ln[:], float(LN_EPS)))
    p.eng("vector", lambda e: e.memset(C.eps_rms[:], float(RMS_EPS)))
    return C, p



def tr8(p, C, src_bf, dst_bf, ps_tr, nchunk=8):
    def pe(e):
        ins = None
        for k in range(nchunk):
            ident = C.ident_bf if src_bf.dtype == BF16 else C.ident_f
            ins = e.transpose(ps_tr[:, k * 128:(k + 1) * 128],
                              src_bf[:, k * 128:(k + 1) * 128],
                              ident[:, :])
        return ins
    p.eng("tensor", pe)
    p.eng("vector", lambda e: e.tensor_copy(dst_bf[:, :nchunk * 128],
                                            ps_tr[:, :nchunk * 128]))


def ln_stats(p, x_tile, width, st, eps_ap):
    inv = 1.0 / width

    p.eng("scalar", lambda e: e.activation(st["scr"][:, :width],
                                           x_tile[:, :width], AF.Square))
    p.eng("vector", lambda e: e.reduce_sum(st["ssq"][:], st["scr"][:, :width],
                                           AX.X))
    p.eng("vector", lambda e: e.reduce_sum(st["ssum"][:], x_tile[:, :width], AX.X))
    p.eng("vector", lambda e: e.tensor_scalar(st["mu"][:], st["ssum"][:], inv,
                                              None, ALU.mult))
    p.eng("vector", lambda e: e.tensor_tensor(st["var"][:], st["mu"][:],
                                              st["mu"][:], ALU.mult))
    p.eng("vector", lambda e: e.tensor_scalar(st["var"][:], st["ssq"][:], inv,
                                              st["var"][:], ALU.mult,
                                              ALU.subtract))
    def rstd_fn(e):
        e.activation(st["var"][:], st["var"][:], AF.Ln, bias=eps_ap[:])
        return e.activation(st["rstd"][:], st["var"][:], AF.Exp, scale=-0.5)
    p.eng("scalar", rstd_fn)


def ln_apply_plain(p, x_tile, out_tile, st, width=D):
    p.eng("vector", lambda e: e.tensor_scalar(
        out_tile[:, :width], x_tile[:, :width], st["mu"][:], st["rstd"][:],
        ALU.subtract, ALU.mult))


def ln_apply_gb(p, x_tile, out_tile, st, g_bc, b_bc, tmp, width=D):
    p.eng("vector", lambda e: e.tensor_scalar(
        tmp[:, :width], x_tile[:, :width], st["mu"][:], st["rstd"][:],
        ALU.subtract, ALU.mult))
    def fin(e):
        e.tensor_tensor(tmp[:, :width], tmp[:, :width], g_bc[:, :width], ALU.mult)
        return e.tensor_tensor(out_tile[:, :width], tmp[:, :width],
                               b_bc[:, :width], ALU.add)
    p.eng("vector", fin)


def load_weight_bf16(p, src_ap, dst_sb, stage, rows, cols, cast_eng="scalar",
                     piece=1024):
    nchunk = rows // 128
    piece = min(piece, cols)
    for k in range(nchunk):
        for c0 in range(0, cols, piece):
            w = min(piece, cols - c0)
            p.dma("sync", lambda e, k=k, c0=c0, w=w: e.dma_start(
                stage[:, :w], src_ap[k * 128:(k + 1) * 128, c0:c0 + w]))
            if cast_eng == "scalar":
                p.eng("scalar", lambda e, k=k, c0=c0, w=w: e.copy(
                    dst_sb[:, k * cols + c0:k * cols + c0 + w], stage[:, :w]))
            else:
                p.eng("vector", lambda e, k=k, c0=c0, w=w: e.tensor_copy(
                    dst_sb[:, k * cols + c0:k * cols + c0 + w], stage[:, :w]))


def load_row_bf16(p, src_ap, dst_row, stage_row, n, piece=1024):
    piece = min(piece, n)
    for c0 in range(0, n, piece):
        w = min(piece, n - c0)
        p.dma("sync", lambda e, c0=c0, w=w: e.dma_start(
            stage_row[:1, :w], src_ap[None, c0:c0 + w]))
        p.eng("vector", lambda e, c0=c0, w=w: e.tensor_copy(
            dst_row[:1, c0:c0 + w], stage_row[:1, :w]))


def bcast_row(p, C, row_f32, out_bc, ps, n):
    ngr = (n + 511) // 512
    for g in range(ngr):
        w = min(512, n - g * 512)
        p.eng("tensor", lambda e, g=g, w=w: e.matmul(
            ps[:, :w], C.ones_colf[:1, :], row_f32[:1, g * 512:g * 512 + w],
            start=True, stop=True))
        p.eng("vector", lambda e, g=g, w=w: e.tensor_copy(
            out_bc[:, g * 512:g * 512 + w], ps[:, :w]))


def load_weight_pl(p, src_ap, dst_sb, stages, rows, cols, tag, wgroup,
                   piece=1024):
    nchunk = rows // 128
    piece = min(piece, cols)
    lp = 0
    for k in range(nchunk):
        for c0 in range(0, cols, piece):
            w = min(piece, cols - c0)
            p.dma("sync", lambda e, k=k, c0=c0, w=w, lp=lp: e.dma_start(
                stages[lp][:, :w], src_ap[k * 128:(k + 1) * 128, c0:c0 + w]),
                group=wgroup + lp, rw=([], [f"{tag}stg{lp}"]))
            p.eng("scalar", lambda e, k=k, c0=c0, w=w, lp=lp: e.copy(
                dst_sb[:, k * cols + c0:k * cols + c0 + w], stages[lp][:, :w]),
                rw=([f"{tag}stg{lp}"], [f"{tag}_k{k}_c{c0}"]))
            lp = 1 - lp


def wres(tag, kchunks, cols, col_lo, col_hi, piece=1024):
    out = []
    for k in range(kchunks):
        for c0 in range(0, cols, piece):
            if c0 < col_hi and c0 + piece > col_lo:
                out.append(f"{tag}_k{k}_c{c0}")
    return out


def ln_pl(p, C, x_tile, x_res, out_tile, out_res, st, tag, width=D,
          scr_res=None):
    inv = 1.0 / width
    if scr_res is None:
        scr_res = f"{tag}scr"
    p.eng("scalar", lambda e: e.activation(
        st["scr"][:, :width], x_tile[:, :width], AF.Square),
        rw=([x_res], [scr_res]))
    p.eng("vector", lambda e: e.reduce_sum(st["ssq"][:], st["scr"][:, :width],
                                           AX.X), rw=([scr_res],
                                                      [f"{tag}ssq"]))
    p.eng("vector", lambda e: e.reduce_sum(st["ssum"][:], x_tile[:, :width],
                                           AX.X), rw=([x_res], [f"{tag}ssum"]))
    p.eng("vector", lambda e: e.tensor_scalar(st["mu"][:], st["ssum"][:], inv,
                                              None, ALU.mult),
          rw=([f"{tag}ssum"], [f"{tag}mu"]))
    p.eng("vector", lambda e: e.tensor_tensor(st["var"][:], st["mu"][:],
                                              st["mu"][:], ALU.mult),
          rw=([f"{tag}mu"], [f"{tag}var"]))
    p.eng("vector", lambda e: e.tensor_scalar(st["var"][:], st["ssq"][:], inv,
                                              st["var"][:], ALU.mult,
                                              ALU.subtract),
          rw=([f"{tag}ssq", f"{tag}var"], [f"{tag}var"]))
    def rstd_fn(e):
        e.activation(st["var"][:], st["var"][:], AF.Ln, bias=C.eps_ln[:])
        return e.activation(st["rstd"][:], st["var"][:], AF.Exp, scale=-0.5)
    p.eng("scalar", rstd_fn, rw=([f"{tag}var"], [f"{tag}rstd", f"{tag}var"]))
    p.eng("vector", lambda e: e.tensor_scalar(
        out_tile[:, :width], x_tile[:, :width], st["mu"][:], st["rstd"][:],
        ALU.subtract, ALU.mult),
        rw=([x_res, f"{tag}mu", f"{tag}rstd"], [out_res]))


def matmul_tm(p, C, xT, w_sb, w_cols, ps, fg0, nfg, kchunks, bias_row=None,
              start_clear=True):
    def pe(e):
        ins = None
        for d in range(kchunks):
            for i in range(nfg):
                fg = fg0 + i
                ins = e.matmul(
                    ps[:, i * 512:(i + 1) * 512],
                    xT[:, d * 128:(d + 1) * 128],
                    w_sb[:, d * w_cols + fg * 512: d * w_cols + fg * 512 + 512],
                    start=(d == 0 and start_clear),
                    stop=(d == kchunks - 1 and bias_row is None))
        if bias_row is not None:
            for i in range(nfg):
                fg = fg0 + i
                ins = e.matmul(ps[:, i * 512:(i + 1) * 512],
                               C.ones_col[:1, :],
                               bias_row[:1, fg * 512:fg * 512 + 512],
                               start=False, stop=True)
        return ins
    p.eng("tensor", pe)


def build_l0_attn(b, C, sctx, xs, ys):
    nc = b.nc
    qkv_w = b.inp("L0_attn_qkv_w", [D, 3 * D])
    qkv_b = b.inp("L0_attn_qkv_b", [3 * D])
    out_w = b.inp("L0_attn_out_w", [D, D])
    out_b = b.inp("L0_attn_out_b", [D])
    w1 = b.inp("L0_attn_w1", [D, DFF])
    b1 = b.inp("L0_attn_b1", [DFF])
    w2 = b.inp("L0_attn_w2", [DFF, D])
    b2 = b.inp("L0_attn_b2", [D])
    g1 = b.inp("L0_attn_g1", [D])
    be1 = b.inp("L0_attn_be1", [D])
    g2 = b.inp("L0_attn_g2", [D])
    be2 = b.inp("L0_attn_be2", [D])
    bsel = b.inp("aux_bsel", [128, 4])

    qf_sp = b.dram("l0a_qf", [TOK, D], BF16)
    x1_sp = b.dram("l0a_x1", [TOK, D], F32)
    xn2_sp = b.dram("l0a_xn2", [TOK, D], BF16)
    cc_in = b.dram("l0a_ccin", [4 * 65, D], F32)
    cc_out = b.dram("l0a_ccout", [4 * 65, D], F32)

    passes = []

    kv_mat = sctx.enter_context(nc.sbuf_tensor("l0a_kvmat", [128, 512], BF16))
    z_mat = sctx.enter_context(nc.sbuf_tensor("l0a_zmat", [128, 8], BF16))

    from contextlib import ExitStack
    with ExitStack() as actx:
        sb = lambda n, s, d=F32: actx.enter_context(nc.sbuf_tensor(n, s, d))
        ps = lambda n, s, d=F32: actx.enter_context(nc.psum_tensor(n, s, d))
        p = Pass(b, "l0a1")

        Wq = sb("a1_w", [128, 8 * 3072], BF16)
        stage = sb("a1_stage", [128, 1024])
        qkvb_row = sb("a1_qb", [1, 3072], BF16)
        rowstage = sb("a1_rs", [1, 1024])
        g1bc = sb("a1_g1bc", [128, D])
        be1bc = sb("a1_be1bc", [128, D])
        bselp = sb("a1_bsel", [128, 4])
        xin = sb("a1_xin", [128, D])
        xn_bf = sb("a1_xnbf", [128, D], BF16)
        xnT = sb("a1_xnT", [128, D], BF16)
        qmin = sb("a1_qmin", [128, 512])
        qexp = sb("a1_qexp", [128, 512])
        qf_sb = sb("a1_qf", [128, D], BF16)
        kf_sb = sb("a1_kf", [128, D], BF16)
        v_sb = sb("a1_v", [128, D], BF16)
        kv_f = sb("a1_kvf", [128, 512])
        z_f = sb("a1_zf", [128, 8])
        zrow = sb("a1_zrow", [1, D])
        kvs = sb("a1_kvs", [128, 512])
        zs = sb("a1_zs", [1, D])
        st = {k: sb("a1_st_" + k, [128, 1]) for k in
              ["ssq", "ssum", "mu", "var", "rstd"]}
        st["scr"] = sb("a1_scr", [128, D])
        tmp = sb("a1_tmp", [128, D])

        ps_tr = ps("a1_ptr", [128, 1024], BF16)
        ps_mm = ps("a1_pmm", [128, 2 * 512])
        ps_kv = ps("a1_pkv", [128, 512])
        ps_za = ps("a1_pza", [1, 512])
        ps_zb = ps("a1_pzb", [1, 512])
        ps_bc = ps("a1_pbc", [128, 512])

        load_weight_bf16(p, qkv_w, Wq, stage, D, 3072)
        load_row_bf16(p, qkv_b, qkvb_row, rowstage, 3072)
        grow = sb("a1_grow", [1, D])
        brow = sb("a1_brow", [1, D])
        p.dma("sync", lambda e: e.dma_start(grow[:1, :], g1[None, :]))
        p.dma("sync", lambda e: e.dma_start(brow[:1, :], be1[None, :]))
        p.dma("sync", lambda e: e.dma_start(bselp[:], bsel[:, :]))
        bcast_row(p, C, grow, g1bc, ps_bc, D)
        bcast_row(p, C, brow, be1bc, ps_bc, D)

        for j in range(NT):
            p.dma("sync", lambda e, j=j: e.dma_start(
                xin[:], xs[j * 128:(j + 1) * 128, :]))
            ln_stats(p, xin, D, st, C.eps_ln)
            ln_apply_gb(p, xin, xn_bf, st, g1bc, be1bc, tmp)
            tr8(p, C, xn_bf, xnT, ps_tr)
            for half in range(3):
                matmul_tm(p, C, xnT, Wq, 3072, ps_mm, 2 * half, 2, 8,
                          bias_row=qkvb_row)
                if half < 2:
                    dst = qf_sb if half == 0 else kf_sb
                    for gg in range(2):
                        col = gg * 512
                        p.eng("vector", lambda e, gg=gg: e.tensor_scalar(
                            qmin[:], ps_mm[:, gg * 512:(gg + 1) * 512],
                            0.0, None, ALU.min))
                        p.eng("scalar", lambda e: e.activation(
                            qexp[:], qmin[:], AF.Exp))
                        p.eng("vector", lambda e, gg=gg, dst=dst, col=col:
                              e.scalar_tensor_tensor(
                                  dst[:, col:col + 512],
                                  ps_mm[:, gg * 512:(gg + 1) * 512],
                                  0.0, qexp[:], ALU.max, ALU.add))
                else:
                    p.eng("vector", lambda e: e.tensor_copy(
                        v_sb[:], ps_mm[:]))
            def kvmm(e, j=j):
                ins = None
                for h in range(NH):
                    ins = e.matmul(
                        ps_kv[64 * (h % 2):64 * (h % 2) + 64,
                              (h // 2) * 64:(h // 2) * 64 + 64],
                        kf_sb[:, h * 64:(h + 1) * 64],
                        v_sb[:, h * 64:(h + 1) * 64],
                        start=(j == 0), stop=(j == NT - 1))
                ins = e.matmul(ps_za[:1, :], C.ones_t[:, :1],
                               kf_sb[:, 0:512], start=(j == 0),
                               stop=(j == NT - 1))
                ins = e.matmul(ps_zb[:1, :], C.ones_t[:, :1],
                               kf_sb[:, 512:1024], start=(j == 0),
                               stop=(j == NT - 1))
                return ins
            p.eng("tensor", kvmm)
            p.dma("gpsimd", lambda e, j=j: e.dma_start(
                qf_sp[j * 128:(j + 1) * 128, :], qf_sb[:]))

        p.eng("vector", lambda e: e.tensor_copy(kv_f[:], ps_kv[:]))
        def zev(e):
            e.tensor_copy(zrow[:1, 0:512], ps_za[:1, :])
            return e.tensor_copy(zrow[:1, 512:1024], ps_zb[:1, :])
        p.eng("vector", zev)
        for bb in range(4):
            p.eng("vector", lambda e, bb=bb: e.tensor_scalar(
                kvs[:], kv_f[:], bselp[:, bb:bb + 1], None, ALU.mult))
            p.eng("vector", lambda e, bb=bb: e.tensor_scalar(
                zs[:1, :], zrow[:1, :], bselp[0:1, bb:bb + 1], None, ALU.mult))
            for par in range(2):
                src = kvs[64 * par:64 * par + 64, :].rearrange(
                    "d (u e) -> d u e", e=64)
                dst = cc_in[bb * 65:bb * 65 + 64, :].rearrange(
                    "d (u two e) -> two d u e", two=2, e=64)[par]
                p.dma("gpsimd", lambda e, src=src, dst=dst: e.dma_start(dst, src))
            p.dma("gpsimd", lambda e, bb=bb: e.dma_start(
                cc_in[bb * 65 + 64:bb * 65 + 65, :], zs[:1, :]))
        p.coll(lambda e: e.collective_compute(
            "AllReduce", ALU.add, [list(range(NCORE))],
            [cc_in[:, :]], [cc_out[:, :]]))
        kvz_acc = sb("a1_kvza", [65, D])
        kvz_t = sb("a1_kvzt", [65, D])
        p.eng("vector", lambda e: e.memset(kvz_acc[:], 0.0))
        for bb in range(4):
            p.dma("sync", lambda e, bb=bb: e.dma_start(
                kvz_t[:], cc_out[bb * 65:(bb + 1) * 65, :]))
            p.eng("vector", lambda e, bb=bb: e.scalar_tensor_tensor(
                kvz_acc[:], kvz_t[:], bselp[0:65, bb:bb + 1], kvz_acc[:],
                ALU.mult, ALU.add))
        for par in range(2):
            src = kvz_acc[0:64, :].rearrange("d (u two e) -> d u two e",
                                             two=2, e=64)[:, :, par, :]
            p.eng("vector", lambda e, par=par, src=src: e.tensor_copy(
                kv_mat[64 * par:64 * par + 64, :].rearrange(
                    "d (u e) -> d u e", e=64), src))
        ps_z = ps("a1_pz", [128, 8])
        def ztr(e):
            ins = None
            for u in range(8):
                ins = e.transpose(ps_z[:, u:u + 1],
                                  kvz_acc[64:65, u * 128:(u + 1) * 128],
                                  C.ident_f[64:65, 64:65])
            return ins
        p.eng("tensor", ztr)
        p.eng("vector", lambda e: e.tensor_copy(z_mat[:], ps_z[:]))
        p.emit()
        passes.append("l0a1")

    with ExitStack() as actx:
        sb = lambda n, s, d=F32: actx.enter_context(nc.sbuf_tensor(n, s, d))
        ps = lambda n, s, d=F32: actx.enter_context(nc.psum_tensor(n, s, d))
        p = Pass(b, "l0a2")

        Wo = sb("a2_w", [128, 8 * D], BF16)
        stage = sb("a2_stage", [128, D])
        ob_row = sb("a2_ob", [1, D], BF16)
        rowstage = sb("a2_rs", [1, D])
        g2bc = sb("a2_g2bc", [128, D])
        be2bc = sb("a2_be2bc", [128, D])
        xin = sb("a2_xin", [128, D])
        qf_bf = sb("a2_qf", [128, D], BF16)
        qfT = sb("a2_qfT", [128, D], BF16)
        y_bf = sb("a2_y", [128, D], BF16)
        yT = sb("a2_yT", [128, D], BF16)
        x1 = sb("a2_x1", [128, D])
        xn2_bf = sb("a2_xn2", [128, D], BF16)
        recip = sb("a2_recip", [128, 16])
        den_t = sb("a2_dent", [128, 16])
        st = {k: sb("a2_st_" + k, [128, 1]) for k in
              ["ssq", "ssum", "mu", "var", "rstd"]}
        st["scr"] = sb("a2_scr", [128, D])
        tmp = sb("a2_tmp", [128, D])

        ps_tr = ps("a2_ptr", [128, 1024], BF16)
        ps_yn = ps("a2_pyn", [128, 1024])
        ps_yd = ps("a2_pyd", [128, 16])
        ps_o = ps("a2_po", [128, 2 * 512])
        ps_bc = ps("a2_pbc", [128, 512])

        load_weight_bf16(p, out_w, Wo, stage, D, D)
        load_row_bf16(p, out_b, ob_row, rowstage, D)
        grow = sb("a2_grow", [1, D])
        brow = sb("a2_brow", [1, D])
        p.dma("sync", lambda e: e.dma_start(grow[:1, :], g2[None, :]))
        p.dma("sync", lambda e: e.dma_start(brow[:1, :], be2[None, :]))
        bcast_row(p, C, grow, g2bc, ps_bc, D)
        bcast_row(p, C, brow, be2bc, ps_bc, D)

        for j in range(NT):
            p.dma("sync", lambda e, j=j: e.dma_start(
                qf_bf[:], qf_sp[j * 128:(j + 1) * 128, :]))
            p.dma("sync", lambda e, j=j: e.dma_start(
                xin[:], xs[j * 128:(j + 1) * 128, :]))
            tr8(p, C, qf_bf, qfT, ps_tr)
            def ymm(par):
                def fn(e):
                    ins = None
                    for h in range(par, NH, 2):
                        lhs = qfT[64 * par:64 * par + 64,
                                  (h // 2) * 128:(h // 2) * 128 + 128]
                        rhs = kv_mat[64 * par:64 * par + 64,
                                     (h // 2) * 64:(h // 2) * 64 + 64]
                        ins = e.matmul(ps_yn[:, h * 64:(h + 1) * 64], lhs,
                                       rhs, start=True, stop=True)
                        rz = z_mat[64 * par:64 * par + 64,
                                   (h // 2):(h // 2) + 1]
                        ins = e.matmul(ps_yd[:, h:h + 1], lhs, rz,
                                       start=True, stop=True)
                    return ins
                return fn
            p.eng("tensor", ymm(0))
            p.eng("tensor", ymm(1))
            p.eng("vector", lambda e: e.tensor_scalar(
                den_t[:], ps_yd[:], float(ATTN_EPS), None, ALU.max))
            p.eng("vector", lambda e: e.reciprocal(recip[:], den_t[:]))
            def ydiv(e):
                ins = None
                for h in range(NH):
                    ins = e.tensor_scalar(
                        y_bf[:, h * 64:(h + 1) * 64],
                        ps_yn[:, h * 64:(h + 1) * 64],
                        recip[:, h:h + 1], None, ALU.mult)
                return ins
            p.eng("vector", ydiv)
            tr8(p, C, y_bf, yT, ps_tr)
            matmul_tm(p, C, yT, Wo, D, ps_o, 0, 2, 8, bias_row=ob_row)
            def resid(e):
                e.tensor_tensor(x1[:, 0:512], ps_o[:, 0:512], xin[:, 0:512],
                                ALU.add)
                return e.tensor_tensor(x1[:, 512:1024], ps_o[:, 512:1024],
                                       xin[:, 512:1024], ALU.add)
            p.eng("vector", resid)
            ln_stats(p, x1, D, st, C.eps_ln)
            ln_apply_gb(p, x1, xn2_bf, st, g2bc, be2bc, tmp)
            p.dma("gpsimd", lambda e, j=j: e.dma_start(
                x1_sp[j * 128:(j + 1) * 128, :], x1[:]))
            p.dma("gpsimd", lambda e, j=j: e.dma_start(
                xn2_sp[j * 128:(j + 1) * 128, :], xn2_bf[:]))
        p.emit()
        passes.append("l0a2")

    with ExitStack() as actx:
        sb = lambda n, s, d=F32: actx.enter_context(nc.sbuf_tensor(n, s, d))
        ps = lambda n, s, d=F32: actx.enter_context(nc.psum_tensor(n, s, d))
        p = Pass(b, "l0a3")

        W1s = sb("a3_w1", [128, 8 * DFF], BF16)
        W2s = sb("a3_w2", [128, 32 * D], BF16)
        stage = sb("a3_stage", [128, 1024])
        b1_row = sb("a3_b1", [1, DFF], BF16)
        b2_row = sb("a3_b2", [1, D], BF16)
        rowstage = sb("a3_rs", [1, 1024])
        xin = sb("a3_xin", [128, D])
        x1t = sb("a3_x1", [128, D])
        xn2 = sb("a3_xn2", [128, D], BF16)
        xn2T = sb("a3_xn2T", [128, D], BF16)
        h_bf = sb("a3_h", [128, DFF], BF16)
        hT = sb("a3_hT", [128, DFF], BF16)
        arg = sb("a3_arg", [128, D])
        outt = sb("a3_out", [128, D])
        st = {k: sb("a3_st_" + k, [128, 1]) for k in
              ["ssq", "ssum", "mu", "var", "rstd"]}
        st["scr"] = sb("a3_scr", [128, D])

        ps_tr = ps("a3_ptr", [128, 1024], BF16)
        ps_h = ps("a3_ph", [128, 2 * 512])
        ps_y = ps("a3_py", [128, 2 * 512])

        load_weight_bf16(p, w1, W1s, stage, D, DFF)
        load_weight_bf16(p, w2, W2s, stage, DFF, D)
        load_row_bf16(p, b1, b1_row, rowstage, DFF)
        load_row_bf16(p, b2, b2_row, rowstage, D)

        for j in range(NT):
            p.dma("sync", lambda e, j=j: e.dma_start(
                xn2[:], xn2_sp[j * 128:(j + 1) * 128, :]))
            p.dma("sync", lambda e, j=j: e.dma_start(
                x1t[:], x1_sp[j * 128:(j + 1) * 128, :]))
            p.dma("sync", lambda e, j=j: e.dma_start(
                xin[:], xs[j * 128:(j + 1) * 128, :]))
            tr8(p, C, xn2, xn2T, ps_tr)
            for half in range(4):
                matmul_tm(p, C, xn2T, W1s, DFF, ps_h, 2 * half, 2, 8,
                          bias_row=b1_row)
                p.eng("scalar", lambda e, half=half: e.activation(
                    h_bf[:, half * 1024:(half + 1) * 1024], ps_h[:],
                    AF.Gelu))
            for q in range(4):
                tr8(p, C, h_bf[:, q * 1024:(q + 1) * 1024],
                    hT[:, q * 1024:(q + 1) * 1024], ps_tr)
            matmul_tm(p, C, hT, W2s, D, ps_y, 0, 2, 32, bias_row=b2_row)
            def res(e):
                e.tensor_tensor(arg[:, 0:512], ps_y[:, 0:512], x1t[:, 0:512],
                                ALU.add)
                e.tensor_tensor(arg[:, 512:1024], ps_y[:, 512:1024],
                                x1t[:, 512:1024], ALU.add)
                return e.tensor_tensor(arg[:], arg[:], xin[:], ALU.add)
            p.eng("vector", res)
            ln_stats(p, arg, D, st, C.eps_ln)
            ln_apply_plain(p, arg, outt, st)
            p.dma("gpsimd", lambda e, j=j: e.dma_start(
                ys[j * 128:(j + 1) * 128, :], outt[:]))
        p.emit()
        passes.append("l0a3")

    return passes


def build_l0_ffn(b, C, sctx, xs, ys):
    from contextlib import ExitStack
    nc = b.nc
    gate_w = b.inp("L0_ffn_gate_w", [E0, D])
    cb_in = b.inp("L0_ffn_correction_bias", [E0])
    Wg = b.inp("L0_ffn_Wg", [E0, D, I0])
    Wu = b.inp("L0_ffn_Wu", [E0, D, I0])
    Wd = b.inp("L0_ffn_Wd", [E0, I0, D])
    Wg_s = b.inp("L0_ffn_Wg_s", [D, I0])
    Wu_s = b.inp("L0_ffn_Wu_s", [D, I0])
    Wd_s = b.inp("L0_ffn_Wd_s", [I0, D])

    xT_sp = b.dram("l0f_xT", [TOK, D], BF16)
    racc = b.dram("l0f_racc", [TOK, D], F32)

    cw_em = sctx.enter_context(nc.sbuf_tensor("l0f_cw", [128, E0 * NT], F32))

    with ExitStack() as actx:
        sb = lambda n, s, d=F32: actx.enter_context(nc.sbuf_tensor(n, s, d))
        ps = lambda n, s, d=F32: actx.enter_context(nc.psum_tensor(n, s, d))
        p = Pass(b, "l0f0")

        gwT = sb("f0_gwT", [128, 8 * E0])
        gstage = sb("f0_gstage", [8, D])
        xTf = sb("f0_xTf", [128, D])
        cbrow = sb("f0_cbrow", [1, E0])
        cb_bc = sb("f0_cbbc", [128, E0])
        xin = sb("f0_xin", [128, D])
        x_bf = sb("f0_xbf", [128, D], BF16)
        xT = sb("f0_xT", [128, D], BF16)
        lgs_em = sb("f0_lgs", [128, E0 * NT])
        sc_em = sb("f0_sc", [128, E0 * NT])
        gs_em = sb("f0_gs", [128, 4 * NT])
        cnt4 = sb("f0_cnt4", [128, 4 * NT])
        tmpb = sb("f0_tmpb", [128, NT])
        smask = sb("f0_smask", [128, E0 * NT])
        sfc = sb("f0_sfc", [128, E0 * NT])
        cnt8 = sb("f0_cnt8", [128, E0 * NT])
        sel4 = sb("f0_sel4", [128, E0 * NT])
        w4 = sb("f0_w4", [128, E0 * NT])
        wsum = sb("f0_wsum", [128, NT])
        wfac = sb("f0_wfac", [128, NT])

        ps_tr = ps("f0_ptr", [128, 1024], BF16)
        ps_trf = ps("f0_ptrf", [128, 1024])
        ps_l = ps("f0_pl", [128, E0])
        ps_g8 = ps("f0_pg8", [128, 8 * E0])
        ps_bc = ps("f0_pbc", [128, 512])

        p.dma("sync", lambda e: e.dma_start(gstage[:, :], gate_w[:, :]))
        def gtr(e):
            ins = None
            for d in range(8):
                ins = e.transpose(ps_g8[:, d * E0:(d + 1) * E0],
                                  gstage[:, d * 128:(d + 1) * 128],
                                  C.ident_f[0:8, 0:8])
            return ins
        p.eng("tensor", gtr)
        p.eng("vector", lambda e: e.tensor_copy(gwT[:, :], ps_g8[:, :]))
        p.dma("sync", lambda e: e.dma_start(cbrow[:1, :], cb_in[None, :]))
        p.eng("tensor", lambda e: e.matmul(ps_bc[:, :E0], C.ones_colf[:1, :],
                                           cbrow[:1, :], start=True, stop=True))
        p.eng("vector", lambda e: e.tensor_copy(cb_bc[:, :], ps_bc[:, :E0]))

        for j in range(NT):
            p.dma("sync", lambda e, j=j: e.dma_start(
                xin[:], xs[j * 128:(j + 1) * 128, :]))
            p.eng("vector", lambda e: e.tensor_copy(x_bf[:], xin[:]))
            tr8(p, C, x_bf, xT, ps_tr)
            p.dma("gpsimd", lambda e, j=j: e.dma_start(
                xT_sp[j * 128:(j + 1) * 128, :], xT[:]))
            tr8(p, C, xin, xTf, ps_trf)
            def lmm(e):
                ins = None
                for d in range(8):
                    ins = e.matmul(ps_l[:, :], xTf[:, d * 128:(d + 1) * 128],
                                   gwT[:, d * E0:(d + 1) * E0],
                                   start=(d == 0), stop=(d == 7))
                return ins
            p.eng("tensor", lmm)
            p.eng("vector", lambda e, j=j: e.tensor_copy(
                lgs_em[:, j::NT], ps_l[:, :]))

        blk = lambda t, e0: t[:, e0 * NT:(e0 + 1) * NT]
        p.eng("scalar", lambda e: e.activation(sc_em[:, :], lgs_em[:, :],
                                               AF.Sigmoid))
        def addcb(e):
            ins = None
            for e0 in range(E0):
                ins = e.tensor_scalar(blk(sc_em, e0), blk(sc_em, e0),
                                      cb_bc[:, e0:e0 + 1], None, ALU.add)
            return ins
        p.eng("vector", addcb)
        def gsum(e):
            ins = None
            for g in range(4):
                ins = e.tensor_tensor(blk(gs_em, g), blk(sc_em, 2 * g),
                                      blk(sc_em, 2 * g + 1), ALU.add)
            return ins
        p.eng("vector", gsum)
        p.eng("vector", lambda e: e.memset(cnt4[:, :], 0.0))
        def grank(e):
            ins = None
            for g in range(4):
                for gp in range(4):
                    if gp == g:
                        continue
                    op = ALU.is_ge if gp < g else ALU.is_gt
                    e.tensor_tensor(tmpb[:, :], blk(gs_em, gp), blk(gs_em, g),
                                    op)
                    ins = e.tensor_tensor(blk(cnt4, g), blk(cnt4, g),
                                          tmpb[:, :], ALU.add)
            return ins
        p.eng("vector", grank)
        def gmask(e):
            e.tensor_single_scalar(cnt4[:, :], cnt4[:, :], 2.0, ALU.is_lt)
            ins = None
            for e0 in range(E0):
                ins = e.tensor_copy(blk(smask, e0), blk(cnt4, e0 // 2))
            return ins
        p.eng("vector", gmask)
        p.eng("vector", lambda e: e.tensor_tensor(sfc[:, :], smask[:, :],
                                                  sc_em[:, :], ALU.mult))
        p.eng("vector", lambda e: e.memset(cnt8[:, :], 0.0))
        for e0 in range(E0):
            def erank(e, e0=e0):
                ins = None
                for ep in range(E0):
                    if ep == e0:
                        continue
                    op = ALU.is_ge if ep < e0 else ALU.is_gt
                    e.tensor_tensor(tmpb[:, :], blk(sfc, ep), blk(sfc, e0), op)
                    ins = e.tensor_tensor(blk(cnt8, e0), blk(cnt8, e0),
                                          tmpb[:, :], ALU.add)
                return ins
            p.eng("vector", erank)
        def finw(e):
            e.tensor_single_scalar(sel4[:, :], cnt8[:, :], 4.0, ALU.is_lt)
            return sel4
        p.eng("vector", lambda e: e.tensor_single_scalar(
            sel4[:, :], cnt8[:, :], 4.0, ALU.is_lt))
        def subcb(e):
            ins = None
            for e0 in range(E0):
                ins = e.tensor_scalar(blk(sc_em, e0), blk(sc_em, e0),
                                      cb_bc[:, e0:e0 + 1], None, ALU.subtract)
            return ins
        p.eng("vector", subcb)
        p.eng("vector", lambda e: e.tensor_tensor(w4[:, :], sel4[:, :],
                                                  sc_em[:, :], ALU.mult))
        def wsumf(e):
            e.tensor_tensor(wsum[:, :], blk(w4, 0), blk(w4, 1), ALU.add)
            for e0 in range(2, E0):
                e.tensor_tensor(wsum[:, :], wsum[:, :], blk(w4, e0), ALU.add)
            return e.tensor_single_scalar(wsum[:, :], wsum[:, :], 1e-20,
                                          ALU.add)
        p.eng("vector", wsumf)
        p.eng("vector", lambda e: e.reciprocal(wfac[:, :], wsum[:, :]))
        p.eng("vector", lambda e: e.tensor_single_scalar(
            wfac[:, :], wfac[:, :], float(SCALE0), ALU.mult))
        def cwf(e):
            ins = None
            for e0 in range(E0):
                ins = e.tensor_tensor(blk(cw_em, e0), blk(w4, e0),
                                      wfac[:, :], ALU.mult)
            return ins
        p.eng("vector", cwf)
        p.emit()

    with ExitStack() as actx:
        sb = lambda n, s, d=F32: actx.enter_context(nc.sbuf_tensor(n, s, d))
        ps = lambda n, s, d=F32: actx.enter_context(nc.psum_tensor(n, s, d))
        p = Pass(b, "l0f1")

        Wgs = [sb(f"f1_wg{i}", [128, 8 * I0], BF16) for i in range(2)]
        Wus = [sb(f"f1_wu{i}", [128, 8 * I0], BF16) for i in range(2)]
        Wds = [sb(f"f1_wd{i}", [128, 4 * D], BF16) for i in range(2)]
        stage = [sb(f"f1_stage{i}", [128, 1024]) for i in range(2)]
        xT = [sb(f"f1_xT{i}", [128, D], BF16) for i in range(2)]
        gs_bf = [sb(f"f1_gs{i}", [128, I0], BF16) for i in range(2)]
        h_bf = [sb(f"f1_h{i}", [128, I0], BF16) for i in range(2)]
        hT = [sb(f"f1_hT{i}", [128, I0], BF16) for i in range(2)]
        ysb = [sb(f"f1_ysb{i}", [128, D]) for i in range(2)]
        ps_g = [ps(f"f1_pg{i}", [128, I0]) for i in range(2)]
        ps_u = [ps(f"f1_pu{i}", [128, I0]) for i in range(2)]
        ps_tr = ps("f1_ptr", [128, 512], BF16)
        ps_y = ps("f1_py", [128, 2 * 512])

        def loadw(ex):
            wp = ex % 2
            shared = ex == E0
            srcs = [(Wg_s if shared else Wg[ex], Wgs[wp], D, I0),
                    (Wu_s if shared else Wu[ex], Wus[wp], D, I0),
                    (Wd_s if shared else Wd[ex], Wds[wp], I0, D)]
            lp = 0
            for (src, dst, rows, cols) in srcs:
                for k in range(rows // 128):
                    p.dma("sync", lambda e, src=src, k=k, cols=cols, lp=lp:
                          e.dma_start(stage[lp][:, :cols],
                                      src[k * 128:(k + 1) * 128, :]),
                          group=4 + lp, rw=([], [f"stg{lp}"]))
                    p.eng("scalar", lambda e, dst=dst, k=k, cols=cols, lp=lp:
                          e.copy(dst[:, k * cols:(k + 1) * cols],
                                 stage[lp][:, :cols]),
                          rw=([f"stg{lp}"], [f"wbuf{wp}"]))
                    lp = 1 - lp

        for ex in range(E0 + 1):
            wp = ex % 2
            loadw(ex)
            for j in range(NT):
                q = j % 2
                p.dma("sync", lambda e, j=j, q=q: e.dma_start(
                    xT[q][:], xT_sp[j * 128:(j + 1) * 128, :]),
                    group=q, rw=(["xTsp"], [f"xT{q}"]))
                def gumm(e, q=q, wp=wp):
                    ins = None
                    for d in range(8):
                        xTd = xT[q][:, d * 128:(d + 1) * 128]
                        ins = e.matmul(ps_g[q][:, :], xTd,
                                       Wgs[wp][:, d * I0:(d + 1) * I0],
                                       start=(d == 0), stop=(d == 7))
                        ins = e.matmul(ps_u[q][:, :], xTd,
                                       Wus[wp][:, d * I0:(d + 1) * I0],
                                       start=(d == 0), stop=(d == 7))
                    return ins
                p.eng("tensor", gumm,
                      rw=([f"xT{q}", f"wbuf{wp}"], [f"pg{q}", f"pu{q}"]))
                p.eng("scalar", lambda e, q=q: e.activation(
                    gs_bf[q][:], ps_g[q][:], AF.Silu),
                    rw=([f"pg{q}"], [f"gs{q}"]))
                if ex == E0:
                    p.eng("vector", lambda e, q=q: e.tensor_tensor(
                        h_bf[q][:], ps_u[q][:], gs_bf[q][:], ALU.mult),
                        rw=([f"pu{q}", f"gs{q}"], [f"h{q}"]))
                else:
                    p.eng("vector", lambda e, ex=ex, j=j, q=q:
                          e.scalar_tensor_tensor(
                              h_bf[q][:], ps_u[q][:],
                              cw_em[:, ex * NT + j:ex * NT + j + 1],
                              gs_bf[q][:], ALU.mult, ALU.mult),
                          rw=([f"pu{q}", f"gs{q}"], [f"h{q}"]))
                def htr(e, q=q):
                    ins = None
                    for k in range(4):
                        ins = e.transpose(ps_tr[:, k * 128:(k + 1) * 128],
                                          h_bf[q][:, k * 128:(k + 1) * 128],
                                          C.ident_bf[:, :])
                    return ins
                p.eng("tensor", htr, rw=([f"h{q}"], ["ptr"]))
                p.eng("vector", lambda e, q=q: e.tensor_copy(
                    hT[q][:], ps_tr[:, 0:512]), rw=(["ptr"], [f"hT{q}"]))
                def ymm(e, q=q, wp=wp):
                    ins = None
                    for i in range(4):
                        for fg in range(2):
                            ins = e.matmul(
                                ps_y[:, fg * 512:(fg + 1) * 512],
                                hT[q][:, i * 128:(i + 1) * 128],
                                Wds[wp][:, i * D + fg * 512:
                                         i * D + fg * 512 + 512],
                                start=(i == 0), stop=(i == 3))
                    return ins
                p.eng("tensor", ymm, rw=([f"hT{q}", f"wbuf{wp}"], ["py"]))
                p.eng("vector", lambda e, q=q: e.tensor_copy(
                    ysb[q][:], ps_y[:]), rw=(["py"], [f"ysb{q}"]))
                if ex == 0:
                    fnst = (lambda e, j=j, q=q: e.dma_start(
                        racc[j * 128:(j + 1) * 128, :], ysb[q][:]))
                else:
                    fnst = (lambda e, j=j, q=q: e.dma_start(
                        racc[j * 128:(j + 1) * 128, :], ysb[q][:],
                        accum_op=ALU.add))
                p.dma("gpsimd", fnst, group=2 + q,
                      rw=([f"ysb{q}", f"racc{j}", f"stq{2 + q}"],
                          [f"racc{j}", f"stq{2 + q}"]))
        p.emit()

    with ExitStack() as actx:
        sb = lambda n, s, d=F32: actx.enter_context(nc.sbuf_tensor(n, s, d))
        p = Pass(b, "l0f2")
        xin = sb("f2_xin", [128, D])
        rin = sb("f2_rin", [128, D])
        arg = sb("f2_arg", [128, D])
        outt = sb("f2_out", [128, D])
        st = {k: sb("f2_st_" + k, [128, 1]) for k in
              ["ssq", "ssum", "mu", "var", "rstd"]}
        st["scr"] = sb("f2_scr", [128, D])
        for j in range(NT):
            p.dma("sync", lambda e, j=j: e.dma_start(
                xin[:], xs[j * 128:(j + 1) * 128, :]))
            p.dma("sync", lambda e, j=j: e.dma_start(
                rin[:], racc[j * 128:(j + 1) * 128, :]))
            p.eng("vector", lambda e: e.tensor_tensor(arg[:], xin[:], rin[:],
                                                      ALU.add))
            ln_stats(p, arg, D, st, C.eps_ln)
            ln_apply_plain(p, arg, outt, st)
            p.dma("gpsimd", lambda e, j=j: e.dma_start(
                ys[j * 128:(j + 1) * 128, :], outt[:]))
        p.emit()
    return ["l0f0", "l0f1", "l0f2"]


def build_l1_attn(b, C, sctx, xs, ys):
    from contextlib import ExitStack
    nc = b.nc
    ln1_g = b.inp("L1_attn_ln1_g", [D])
    ln1_b = b.inp("L1_attn_ln1_b", [D])
    ln2_g = b.inp("L1_attn_ln2_g", [D])
    ln2_b = b.inp("L1_attn_ln2_b", [D])
    W1 = b.inp("L1_attn_W1", [D, DFF])
    b1 = b.inp("L1_attn_b1", [DFF])
    W2 = b.inp("L1_attn_W2", [DFF, D])
    b2 = b.inp("L1_attn_b2", [D])
    lsel = b.inp("aux_lsel", [128, NCORE])
    rsel = b.inp("aux_rsel", [128, NCORE])
    maskin = b.inp("aux_maskcap", [6 * 128, 768])

    hal_in = b.dram("l1a_halin", [512, D], BF16)
    b.dram("l1a_attn", [TOK, D], F32)
    hal_out = b.dram("l1a_halout", [NCORE * 512, D], BF16)
    x1_sp = b.dram("l1a_x1", [TOK, D], F32)

    with ExitStack() as actx:
        sb = lambda n, s, d=F32: actx.enter_context(nc.sbuf_tensor(n, s, d))
        ps = lambda n, s, d=F32: actx.enter_context(nc.psum_tensor(n, s, d))
        p = Pass(b, "l1a1")

        mask_sb = sb("c1_mask", [128, 6 * 768])
        selp = sb("c1_selp", [128, 2 * NCORE])
        xin = sb("c1_xin", [128, D])
        halo_x = [sb(f"c1_hx{i}", [128, D], BF16) for i in range(4)]
        halo_xT = [sb(f"c1_hxT{i}", [128, D], BF16) for i in range(4)]
        ring_x = [sb(f"c1_rx{i}", [128, D], BF16) for i in range(8)]
        ring_xT = [sb(f"c1_rxT{i}", [128, D], BF16) for i in range(8)]
        accf = sb("c1_accf", [128, D])
        tmpbf = sb("c1_tmpbf", [128, D], BF16)
        s_sb = sb("c1_ssb", [128, 768])
        negmax = sb("c1_negmax", [128, 1])
        den = sb("c1_den", [128, 1])
        recip = sb("c1_recip", [128, 1])
        p_bf = sb("c1_pbf", [128, 768], BF16)
        pT = sb("c1_pT", [128, 768], BF16)
        attn_t = sb("c1_attn", [128, D])
        t1 = sb("c1_t1", [128, D])
        x1 = sb("c1_x1", [128, D])
        g1bc = sb("c1_g1bc", [128, D])
        b1bc = sb("c1_b1bc", [128, D])
        grow = sb("c1_grow", [1, D])
        tmp = sb("c1_tmp", [128, D])
        st = {k: sb("c1_st_" + k, [128, 1]) for k in
              ["ssq", "ssum", "mu", "var", "rstd"]}
        st["scr"] = sb("c1_scr", [128, D])

        ps_tr = ps("c1_ptr", [128, 1024], BF16)
        ps_s = ps("c1_ps", [128, 768])
        ps_o = ps("c1_po", [128, 64])
        ps_bc = ps("c1_pbc", [128, 512])

        for i in range(6):
            p.dma("sync", lambda e, i=i: e.dma_start(
                mask_sb[:, i * 768:(i + 1) * 768],
                maskin[i * 128:(i + 1) * 128, :]))
        p.dma("sync", lambda e: e.dma_start(selp[:, 0:NCORE], lsel[:, :]))
        p.dma("sync", lambda e: e.dma_start(selp[:, NCORE:], rsel[:, :]))
        p.dma("sync", lambda e: e.dma_start(grow[:1, :], ln1_g[None, :]))
        bcast_row(p, C, grow, g1bc, ps_bc, D)
        p.dma("sync", lambda e: e.dma_start(grow[:1, :], ln1_b[None, :]))
        bcast_row(p, C, grow, b1bc, ps_bc, D)

        for i, j in enumerate([0, 1, 30, 31]):
            p.dma("sync", lambda e, j=j: e.dma_start(
                xin[:], xs[j * 128:(j + 1) * 128, :]))
            p.eng("vector", lambda e: e.tensor_copy(tmpbf[:], xin[:]))
            p.dma("gpsimd", lambda e, i=i: e.dma_start(
                hal_in[i * 128:(i + 1) * 128, :], tmpbf[:]))
        p.coll(lambda e: e.collective_compute(
            "AllGather", ALU.bypass, [list(range(NCORE))],
            [hal_in[:, :]], [hal_out[:, :]]))
        for side in range(2):
            for r in range(2):
                p.eng("vector", lambda e: e.memset(accf[:], 0.0))
                for cc in range(NCORE):
                    src = hal_out[cc * 512 + (1 - side) * 256 + r * 128:
                                  cc * 512 + (1 - side) * 256 + r * 128 + 128, :]
                    p.dma("sync", lambda e, src=src: e.dma_start(
                        tmpbf[:], src))
                    p.eng("vector", lambda e, cc=cc, side=side:
                          e.scalar_tensor_tensor(
                              accf[:], tmpbf[:],
                              selp[:, side * NCORE + cc:side * NCORE + cc + 1],
                              accf[:], ALU.mult, ALU.add))
                idx = side * 2 + r
                p.eng("vector", lambda e, idx=idx: e.tensor_copy(
                    halo_x[idx][:], accf[:]))
            for r in range(2):
                idx = side * 2 + r
                tr8(p, C, halo_x[idx], halo_xT[idx], ps_tr)

        def wtile(w):
            if w < 0:
                return halo_x[w + 2], halo_xT[w + 2]
            if w >= NT:
                return halo_x[w - NT + 2], halo_xT[w - NT + 2]
            return ring_x[w % 8], ring_xT[w % 8]

        def materialize(j):
            p.dma("sync", lambda e, j=j: e.dma_start(
                xin[:], xs[j * 128:(j + 1) * 128, :]))
            p.eng("vector", lambda e, j=j: e.tensor_copy(
                ring_x[j % 8][:], xin[:]))
            tr8(p, C, ring_x[j % 8], ring_xT[j % 8], ps_tr)

        for j in range(4):
            materialize(j)

        for j in range(NT):
            if j + 4 < NT:
                materialize(j + 4)
            ws = j - 2 - (j % 2)
            if j <= 1:
                mslot = j
            elif j >= 30:
                mslot = j - 26
            else:
                mslot = 2 + (j % 2)
            mask_ap = mask_sb[:, mslot * 768:(mslot + 1) * 768]
            for h in range(NH):
                par = h % 2
                hi = h // 2
                def smm(e, h=h, par=par, hi=hi, ws=ws, j=j):
                    ins = None
                    for wi in range(6):
                        _, xT_w = wtile(ws + wi)
                        ins = e.matmul(
                            ps_s[:, wi * 128:(wi + 1) * 128],
                            wtile(j)[1][64 * par:64 * par + 64,
                                        hi * 128:hi * 128 + 128],
                            xT_w[64 * par:64 * par + 64,
                                 hi * 128:hi * 128 + 128],
                            start=True, stop=True)
                    return ins
                p.eng("tensor", smm)
                p.eng("vector", lambda e, mask_ap=mask_ap: e.tensor_tensor(
                    s_sb[:], ps_s[:], mask_ap, ALU.min))
                p.eng("vector", lambda e: e.reduce_max(
                    negmax[:], s_sb[:], AX.X, negate=True))
                p.eng("scalar", lambda e: e.activation(
                    p_bf[:], s_sb[:], AF.Exp, bias=negmax[:],
                    accum_out=den[:]))
                p.eng("vector", lambda e: e.reciprocal(recip[:], den[:]))
                def ptr(e):
                    ins = None
                    for wi in range(6):
                        ins = e.transpose(
                            ps_tr[:, wi * 128:(wi + 1) * 128],
                            p_bf[:, wi * 128:(wi + 1) * 128],
                            C.ident_bf[:, :])
                    return ins
                p.eng("tensor", ptr)
                p.eng("vector", lambda e: e.tensor_copy(
                    pT[:], ps_tr[:, 0:768]))
                def omm(e, h=h, ws=ws):
                    ins = None
                    for wi in range(6):
                        x_w, _ = wtile(ws + wi)
                        ins = e.matmul(
                            ps_o[:, :], pT[:, wi * 128:(wi + 1) * 128],
                            x_w[:, h * 64:(h + 1) * 64],
                            start=(wi == 0), stop=(wi == 5))
                    return ins
                p.eng("tensor", omm)
                p.eng("vector", lambda e, h=h: e.tensor_scalar(
                    attn_t[:, h * 64:(h + 1) * 64], ps_o[:, :], recip[:],
                    None, ALU.mult))
            if DEBUG_DUMPS:
                p.dma("gpsimd", lambda e, j=j: e.dma_start(
                    b.drams["l1a_attn"][j * 128:(j + 1) * 128, :], attn_t[:]))
            p.dma("sync", lambda e, j=j: e.dma_start(
                xin[:], xs[j * 128:(j + 1) * 128, :]))
            p.eng("vector", lambda e: e.tensor_tensor(t1[:], attn_t[:],
                                                      xin[:], ALU.add))
            ln_stats(p, t1, D, st, C.eps_ln)
            ln_apply_gb(p, t1, x1, st, g1bc, b1bc, tmp)
            p.dma("gpsimd", lambda e, j=j: e.dma_start(
                x1_sp[j * 128:(j + 1) * 128, :], x1[:]))
        p.emit()

    with ExitStack() as actx:
        sb = lambda n, s, d=F32: actx.enter_context(nc.sbuf_tensor(n, s, d))
        ps = lambda n, s, d=F32: actx.enter_context(nc.psum_tensor(n, s, d))
        p = Pass(b, "l1a2")

        W1s = sb("c2_w1", [128, 8 * DFF], BF16)
        W2s = sb("c2_w2", [128, 32 * D], BF16)
        stage = sb("c2_stage", [128, 1024])
        b1_row = sb("c2_b1", [1, DFF], BF16)
        b2_row = sb("c2_b2", [1, D], BF16)
        rowstage = sb("c2_rs", [1, 1024])
        g2bc = sb("c2_g2bc", [128, D])
        b2bc = sb("c2_b2bc", [128, D])
        grow = rowstage
        xin = sb("c2_xin", [128, D])
        x1t = sb("c2_x1", [128, D])
        x1bf = sb("c2_x1bf", [128, D], BF16)
        x1T = sb("c2_x1T", [128, D], BF16)
        h_bf = sb("c2_h", [128, DFF], BF16)
        hT = sb("c2_hT", [128, DFF], BF16)
        t2 = sb("c2_t2", [128, D])
        so = sb("c2_so", [128, D])
        arg = sb("c2_arg", [128, D])
        outt = sb("c2_out", [128, D])
        st = {k: sb("c2_st_" + k, [128, 1]) for k in
              ["ssq", "ssum", "mu", "var", "rstd"]}
        st["scr"] = sb("c2_scr", [128, D])
        tmp = st["scr"]

        ps_tr = ps("c2_ptr", [128, 1024], BF16)
        ps_h = ps("c2_ph", [128, 2 * 512])
        ps_y = ps("c2_py", [128, 2 * 512])
        ps_bc = ps("c2_pbc", [128, 512])

        load_weight_bf16(p, W1, W1s, stage, D, DFF)
        load_weight_bf16(p, W2, W2s, stage, DFF, D)
        load_row_bf16(p, b1, b1_row, rowstage, DFF)
        load_row_bf16(p, b2, b2_row, rowstage, D)
        p.dma("sync", lambda e: e.dma_start(grow[:1, :], ln2_g[None, :]))
        bcast_row(p, C, grow, g2bc, ps_bc, D)
        p.dma("sync", lambda e: e.dma_start(grow[:1, :], ln2_b[None, :]))
        bcast_row(p, C, grow, b2bc, ps_bc, D)

        for j in range(NT):
            p.dma("sync", lambda e, j=j: e.dma_start(
                x1t[:], x1_sp[j * 128:(j + 1) * 128, :]))
            p.eng("vector", lambda e: e.tensor_copy(x1bf[:], x1t[:]))
            tr8(p, C, x1bf, x1T, ps_tr)
            for half in range(4):
                matmul_tm(p, C, x1T, W1s, DFF, ps_h, 2 * half, 2, 8,
                          bias_row=b1_row)
                p.eng("scalar", lambda e, half=half: e.activation(
                    h_bf[:, half * 1024:(half + 1) * 1024], ps_h[:],
                    AF.Relu))
            for q in range(4):
                tr8(p, C, h_bf[:, q * 1024:(q + 1) * 1024],
                    hT[:, q * 1024:(q + 1) * 1024], ps_tr)
            matmul_tm(p, C, hT, W2s, D, ps_y, 0, 2, 32, bias_row=b2_row)
            def res(e):
                e.tensor_tensor(t2[:, 0:512], ps_y[:, 0:512], x1t[:, 0:512],
                                ALU.add)
                return e.tensor_tensor(t2[:, 512:1024], ps_y[:, 512:1024],
                                       x1t[:, 512:1024], ALU.add)
            p.eng("vector", res)
            ln_stats(p, t2, D, st, C.eps_ln)
            ln_apply_gb(p, t2, so, st, g2bc, b2bc, tmp)
            p.dma("sync", lambda e, j=j: e.dma_start(
                xin[:], xs[j * 128:(j + 1) * 128, :]))
            p.eng("vector", lambda e: e.tensor_tensor(arg[:], so[:], xin[:],
                                                      ALU.add))
            ln_stats(p, arg, D, st, C.eps_ln)
            ln_apply_plain(p, arg, outt, st)
            p.dma("gpsimd", lambda e, j=j: e.dma_start(
                ys[j * 128:(j + 1) * 128, :], outt[:]))
        p.emit()
    return ["l1a1", "l1a2"]


def build_l1_ffn(b, C, sctx, xs, ys):
    from contextlib import ExitStack
    nc = b.nc
    router_w = b.inp("L1_ffn_router_w", [E1, D])
    my_rg = b.inp("aux_my_rg", [D, I1R])
    my_ru = b.inp("aux_my_ru", [D, I1R])
    my_rd = b.inp("aux_my_rd", [I1R, D])
    sg = b.inp("L1_ffn_sg", [D, I1S])
    su = b.inp("L1_ffn_su", [D, I1S])
    sd = b.inp("L1_ffn_sd", [I1S, D])
    pm_in = b.inp("aux_pm", [E1, 1])

    xT_sp = b.dram("l1f_xT", [TOK, D], BF16)
    xbf_sp = b.dram("l1f_xbf", [TOK, D], BF16)
    g_sp = b.dram("l1f_g", [TOK, I1S], BF16)
    h_sp = b.dram("l1f_h", [TOK, I1S], BF16)
    hR_sp = b.dram("l1f_hR", [CAP, I1R], BF16)
    comp_d = b.dram("l1f_comp", [E1 * CAP, D], BF16)
    mycomp = b.dram("l1f_mycomp", [CAP, D], BF16)
    contrib = b.dram("l1f_contrib", [CAP, D], BF16)
    ag_contrib = b.dram("l1f_agc", [E1 * CAP, D], BF16)
    cnt_d = b.dram("l1f_cnt", [E1, 1], F32)
    cnt_ag = b.dram("l1f_cntag", [NCORE * E1, 1], F32)

    weff_em = sctx.enter_context(nc.sbuf_tensor("l1f_weff", [128, E1 * NT], F32))
    idx_em = sctx.enter_context(nc.sbuf_tensor("l1f_idx", [128, E1 * NT], U32))
    idx2_em = sctx.enter_context(nc.sbuf_tensor("l1f_idx2", [128, E1 * NT], U32))

    EM = lambda t, e0: t[:, e0 * NT:(e0 + 1) * NT]

    with ExitStack() as actx:
        sb = lambda n, s, d=F32: actx.enter_context(nc.sbuf_tensor(n, s, d))
        ps = lambda n, s, d=F32: actx.enter_context(nc.psum_tensor(n, s, d))
        p = Pass(b, "l1f0")
        rwT = sb("r0_rwT", [128, 8 * E1])
        rstage = sb("r0_rstage", [E1, D])
        xin = sb("r0_xin", [128, D])
        x_bf = sb("r0_xbf", [128, D], BF16)
        xT = sb("r0_xT", [128, D], BF16)
        xnf = sb("r0_xnf", [128, D])
        xnT = sb("r0_xnT", [128, D])
        lgs_tm = sb("r0_lgs", [128, NT * E1])
        st = {k: sb("r0_st_" + k, [128, 1]) for k in ["ssq", "rstd"]}
        scr = sb("r0_scr", [128, D])
        ps_tr = ps("r0_ptr", [128, 1024], BF16)
        ps_trf = ps("r0_ptrf", [128, 1024])
        ps_r = ps("r0_pr", [128, E1])
        ps_g8 = ps("r0_pg8", [128, 8 * E1])

        p.dma("sync", lambda e: e.dma_start(rstage[:, :], router_w[:, :]))
        def rtr(e):
            ins = None
            for d in range(8):
                ins = e.transpose(ps_g8[:, d * E1:(d + 1) * E1],
                                  rstage[:, d * 128:(d + 1) * 128],
                                  C.ident_f[0:E1, 0:E1])
            return ins
        p.eng("tensor", rtr)
        p.eng("vector", lambda e: e.tensor_copy(rwT[:, :], ps_g8[:, :]))

        for j in range(NT):
            p.dma("sync", lambda e, j=j: e.dma_start(
                xin[:], xs[j * 128:(j + 1) * 128, :]))
            p.eng("vector", lambda e: e.tensor_copy(x_bf[:], xin[:]))
            tr8(p, C, x_bf, xT, ps_tr)
            p.dma("gpsimd", lambda e, j=j: e.dma_start(
                xbf_sp[j * 128:(j + 1) * 128, :], x_bf[:]))
            p.dma("gpsimd", lambda e, j=j: e.dma_start(
                xT_sp[j * 128:(j + 1) * 128, :], xT[:]))
            p.eng("scalar", lambda e: e.activation(scr[:], xin[:], AF.Square))
            p.eng("vector", lambda e: e.reduce_sum(st["ssq"][:], scr[:], AX.X))
            def rstd_fn(e):
                e.activation(st["rstd"][:], st["ssq"][:], AF.Ln,
                             bias=C.eps_rms[:], scale=1.0 / D)
                return e.activation(st["rstd"][:], st["rstd"][:], AF.Exp,
                                    scale=-0.5)
            p.eng("scalar", rstd_fn)
            p.eng("vector", lambda e: e.tensor_scalar(
                xnf[:], xin[:], st["rstd"][:], None, ALU.mult))
            tr8(p, C, xnf, xnT, ps_trf)
            def rmm(e):
                ins = None
                for d in range(8):
                    ins = e.matmul(ps_r[:, :], xnT[:, d * 128:(d + 1) * 128],
                                   rwT[:, d * E1:(d + 1) * E1],
                                   start=(d == 0), stop=(d == 7))
                return ins
            p.eng("tensor", rmm)
            p.eng("vector", lambda e, j=j: e.tensor_copy(
                lgs_tm[:, j * E1:(j + 1) * E1], ps_r[:, :]))

        maxs = sb("r0_maxs", [128, NT])
        esub = sb("r0_esub", [128, NT * E1])
        ep_tm = sb("r0_eptm", [128, NT * E1])
        ep_em = sb("r0_epem", [128, E1 * NT])
        cnt_em = sb("r0_cnt", [128, E1 * NT])
        sel_em = sb("r0_sel", [128, E1 * NT])
        wsel = sb("r0_wsel", [128, E1 * NT])
        wsum = sb("r0_wsum", [128, NT])
        recipw = sb("r0_recipw", [128, NT])
        rw_em = sb("r0_rw", [128, E1 * NT])
        tmpb = sb("r0_tmpb", [128, NT])
        colsums = sb("r0_cols", [128, E1])
        cnt8 = sb("r0_cnt8", [E1, 1])
        offs_row = sb("r0_offs", [1, E1])
        offs_bc = sb("r0_offsbc", [128, E1])
        cntl = sb("r0_cntl", [E1, E1])
        pm_sb = sb("r0_pm", [E1, 1])
        selT = sb("r0_selT", [32, 128])
        zeros32 = sb("r0_z32", [32, 128])
        rsT = sb("r0_rsT", [32, 128])
        totrow = sb("r0_totrow", [1, 32])
        inclrow = sb("r0_incl", [1, 32])
        exclcol = sb("r0_excl", [32, 1])
        rank_b = sb("r0_rankb", [128, NT])
        grank_b = sb("r0_grankb", [128, NT])
        valid_b = sb("r0_validb", [128, NT])
        sv_b = sb("r0_svb", [128, NT])
        inv_b = sb("r0_invb", [128, NT])
        idxf_b = sb("r0_idxfb", [128, NT])
        ps_sm = ps("r0_psm", [128, 512])
        ps_t32 = ps_sm[0:32, 0:128]
        ps_t128 = ps_sm[:, 0:32]
        ps_r1 = ps_sm[0:1, 128:160]
        ps_c1 = ps_sm[0:32, 160:161]
        ps_off = ps_sm[0:1, 168:168 + E1]
        ps_obc = ps_sm[:, 176:176 + E1]

        TM = lambda t, e0: t[:, e0::E1]
        p.eng("vector", lambda e: e.reduce_max(
            maxs[:], lgs_tm[:, :].rearrange("p (j e) -> p j e", e=E1), AX.X))
        def esubf(e):
            ins = None
            for e0 in range(E1):
                ins = e.tensor_tensor(TM(esub, e0), TM(lgs_tm, e0), maxs[:],
                                      ALU.subtract)
            return ins
        p.eng("vector", esubf)
        p.eng("scalar", lambda e: e.activation(ep_tm[:, :], esub[:, :], AF.Exp))
        p.eng("vector", lambda e: e.memset(cnt_em[:, :], 0.0))
        for e0 in range(E1):
            def erank(e, e0=e0):
                ins = None
                for ep in range(E1):
                    if ep == e0:
                        continue
                    op = ALU.is_ge if ep < e0 else ALU.is_gt
                    e.tensor_tensor(tmpb[:, :], TM(ep_tm, ep), TM(ep_tm, e0),
                                    op)
                    ins = e.tensor_tensor(EM(cnt_em, e0), EM(cnt_em, e0),
                                          tmpb[:, :], ALU.add)
                return ins
            p.eng("vector", erank)
        p.eng("vector", lambda e: e.tensor_single_scalar(
            sel_em[:, :], cnt_em[:, :], 2.0, ALU.is_lt))
        def epem(e):
            ins = None
            for e0 in range(E1):
                ins = e.tensor_copy(EM(ep_em, e0), TM(ep_tm, e0))
            return ins
        p.eng("vector", epem)
        p.eng("vector", lambda e: e.tensor_tensor(wsel[:, :], sel_em[:, :],
                                                  ep_em[:, :], ALU.mult))
        def wsumf(e):
            e.tensor_tensor(wsum[:, :], EM(wsel, 0), EM(wsel, 1), ALU.add)
            ins = None
            for e0 in range(2, E1):
                ins = e.tensor_tensor(wsum[:, :], wsum[:, :], EM(wsel, e0),
                                      ALU.add)
            return ins
        p.eng("vector", wsumf)
        p.eng("vector", lambda e: e.reciprocal(recipw[:, :], wsum[:, :]))
        def rwf(e):
            ins = None
            for e0 in range(E1):
                ins = e.tensor_tensor(EM(rw_em, e0), EM(wsel, e0),
                                      recipw[:, :], ALU.mult)
            return ins
        p.eng("vector", rwf)
        def colsf(e):
            ins = None
            for e0 in range(E1):
                ins = e.reduce_sum(colsums[:, e0:e0 + 1], EM(sel_em, e0),
                                   AX.X)
            return ins
        p.eng("vector", colsf)
        p.eng("tensor", lambda e: e.transpose(
            ps_t32[0:E1, :], colsums[:, :], C.ident_f[:, :]))
        p.eng("vector", lambda e: e.reduce_sum(cnt8[:, :], ps_t32[0:E1, :],
                                               AX.X))
        p.dma("gpsimd", lambda e: e.dma_start(cnt_d[:, :], cnt8[:, :]))
        p.coll(lambda e: e.collective_compute(
            "AllGather", ALU.bypass, [list(range(NCORE))],
            [cnt_d[:, :]], [cnt_ag[:, :]]))
        p.dma("sync", lambda e: e.dma_start(
            cntl[:, :], cnt_ag.rearrange("(c e) one -> c (e one)", e=E1)))
        p.dma("sync", lambda e: e.dma_start(pm_sb[:, :], pm_in[:, :]))
        p.eng("tensor", lambda e: e.matmul(ps_off[:, :], pm_sb[:, :],
                                           cntl[:, :], start=True, stop=True))
        p.eng("vector", lambda e: e.tensor_copy(offs_row[:, :], ps_off[:, :]))
        p.eng("tensor", lambda e: e.matmul(ps_obc[:, :], C.ones_colf[:1, :],
                                           offs_row[:1, :], start=True,
                                           stop=True))
        p.eng("vector", lambda e: e.tensor_copy(offs_bc[:, :], ps_obc[:, :]))
        p.eng("vector", lambda e: e.memset(zeros32[:, :], 0.0))
        for e0 in range(E1):
            p.eng("tensor", lambda e, e0=e0: e.transpose(
                ps_t32[:, :], EM(sel_em, e0), C.ident_f[:, :]))
            p.eng("vector", lambda e: e.tensor_copy(selT[:, :], ps_t32[:, :]))
            p.eng("vector", lambda e: e.tensor_tensor_scan(
                rsT[:, :], zeros32[:, :], selT[:, :], 0.0, ALU.add, ALU.add))
            p.eng("tensor", lambda e: e.transpose(
                ps_r1[:, :], rsT[:, 127:128], C.ident_f[0:32, 0:32]))
            p.eng("vector", lambda e: e.tensor_copy(totrow[:, :], ps_r1[:, :]))
            p.eng("vector", lambda e: e.tensor_tensor_scan(
                inclrow[:, :], zeros32[0:1, 0:32], totrow[:, :], 0.0,
                ALU.add, ALU.add))
            p.eng("vector", lambda e: e.tensor_tensor(
                inclrow[:, :], inclrow[:, :], totrow[:, :], ALU.subtract))
            p.eng("tensor", lambda e: e.transpose(
                ps_c1[:, :], inclrow[:, :], C.ident_f[0:1, 0:1]))
            p.eng("vector", lambda e: e.tensor_copy(exclcol[:, :],
                                                    ps_c1[:, :]))
            p.eng("vector", lambda e: e.tensor_scalar(
                rsT[:, :], rsT[:, :], exclcol[:, :], None, ALU.add))
            p.eng("tensor", lambda e: e.transpose(
                ps_t128[:, :], rsT[:, :], C.ident_f[0:32, 0:32]))
            p.eng("vector", lambda e: e.tensor_copy(rank_b[:, :],
                                                    ps_t128[:, :]))
            p.eng("vector", lambda e, e0=e0: e.tensor_tensor(
                grank_b[:, :], rank_b[:, :], EM(sel_em, e0), ALU.subtract))
            p.eng("vector", lambda e, e0=e0: e.tensor_scalar(
                grank_b[:, :], grank_b[:, :], offs_bc[:, e0:e0 + 1], None,
                ALU.add))
            p.eng("vector", lambda e: e.tensor_single_scalar(
                valid_b[:, :], grank_b[:, :], float(CAP), ALU.is_lt))
            p.eng("vector", lambda e, e0=e0: e.tensor_tensor(
                sv_b[:, :], EM(sel_em, e0), valid_b[:, :], ALU.mult))
            p.eng("vector", lambda e, e0=e0: e.tensor_tensor(
                EM(weff_em, e0), EM(rw_em, e0), sv_b[:, :], ALU.mult))
            p.eng("vector", lambda e: e.tensor_tensor(
                idxf_b[:, :], grank_b[:, :], sv_b[:, :], ALU.mult))
            p.eng("vector", lambda e: e.tensor_scalar(
                inv_b[:, :], sv_b[:, :], -1e9, 1e9, ALU.mult, ALU.add))
            p.eng("vector", lambda e: e.tensor_tensor(
                idxf_b[:, :], idxf_b[:, :], inv_b[:, :], ALU.add))
            p.eng("vector", lambda e, e0=e0: e.tensor_copy(
                EM(idx_em, e0), idxf_b[:, :]))
            p.eng("vector", lambda e, e0=e0: e.tensor_scalar(
                idxf_b[:, :], idxf_b[:, :], float(e0 * CAP), None, ALU.add))
            p.eng("vector", lambda e, e0=e0: e.tensor_copy(
                EM(idx2_em, e0), idxf_b[:, :]))
        p.emit()

    with ExitStack() as actx:
        sb = lambda n, s, d=F32: actx.enter_context(nc.sbuf_tensor(n, s, d))
        ps = lambda n, s, d=F32: actx.enter_context(nc.psum_tensor(n, s, d))
        p = Pass(b, "l1f2a")
        bcreg = {}
        def prelude(g):
            r = g.alloc_register("bcreg2a")
            g.reg_mov(r, E1 * CAP - 1)
            bcreg["r"] = r
        p.gpsimd_prelude = prelude
        zz = sb("r2_zz", [128, 4096], BF16)
        xbt = sb("r2_xbt", [128, D], BF16)
        Rg = sb("r2_rg", [128, 8 * I1R], BF16)
        Ru = sb("r2_ru", [128, 8 * I1R], BF16)
        stage = sb("r2_stage", [128, 1024])
        xgT = sb("r2_xgT", [128, D], BF16)
        gs_bf = sb("r2_gs", [128, 512], BF16)
        hT_t = sb("r2_hT", [128, I1R], BF16)
        ps_tr = ps("r2_ptr", [128, 1024], BF16)
        ps_g = ps("r2_pg", [128, 512])
        ps_u = ps("r2_pu", [128, 512])

        p.eng("vector", lambda e: e.memset(zz[:, :], 0.0))
        for n in range(0, E1 * CAP // 128, 4):
            p.dma("gpsimd", lambda e, n=n: e.dma_start(
                comp_d.rearrange("(m p) d -> p m d", p=128)[:, n:n + 4, :],
                zz[:, :].rearrange("p (m d) -> p m d", d=D)))
        for j in range(NT):
            p.dma("sync", lambda e, j=j: e.dma_start(
                xbt[:], xbf_sp[j * 128:(j + 1) * 128, :]))
            for e0 in range(E1):
                p.dma("gpsimd", lambda e, j=j, e0=e0: e.indirect_dma_start(
                    comp_d[:, :],
                    bass.IndirectOffsetOnAxis(
                        idx2_em[:, e0 * NT + j:e0 * NT + j + 1], 0),
                    xbt[:, :], None,
                    bounds_check=bcreg["r"], oob_is_err=False))
        p.coll(lambda e: e.collective_compute(
            "ReduceScatter", ALU.add, [list(range(NCORE))],
            [comp_d[:, :]], [mycomp[:, :]]))
        load_weight_bf16(p, my_rg, Rg, stage, D, I1R)
        load_weight_bf16(p, my_ru, Ru, stage, D, I1R)
        for t in range(CAP // 128):
            p.dma("sync", lambda e, t=t: e.dma_start(
                xbt[:], mycomp[t * 128:(t + 1) * 128, :]))
            tr8(p, C, xbt, xgT, ps_tr)
            for fg in range(8):
                def gumm(e, fg=fg):
                    ins = None
                    for d in range(8):
                        xTd = xgT[:, d * 128:(d + 1) * 128]
                        ins = e.matmul(ps_g[:, :], xTd,
                                       Rg[:, d * I1R + fg * 512:
                                          d * I1R + fg * 512 + 512],
                                       start=(d == 0), stop=(d == 7))
                        ins = e.matmul(ps_u[:, :], xTd,
                                       Ru[:, d * I1R + fg * 512:
                                          d * I1R + fg * 512 + 512],
                                       start=(d == 0), stop=(d == 7))
                    return ins
                p.eng("tensor", gumm)
                p.eng("scalar", lambda e: e.activation(gs_bf[:], ps_g[:],
                                                       AF.Silu))
                p.eng("vector", lambda e: e.tensor_tensor(
                    gs_bf[:, :], ps_u[:], gs_bf[:], ALU.mult))
                def htr(e, fg=fg):
                    ins = None
                    for k in range(4):
                        ins = e.transpose(
                            ps_tr[:, k * 128:(k + 1) * 128],
                            gs_bf[:, k * 128:(k + 1) * 128],
                            C.ident_bf[:, :])
                    return ins
                p.eng("tensor", htr)
                p.eng("vector", lambda e, fg=fg: e.tensor_copy(
                    hT_t[:, fg * 512:(fg + 1) * 512], ps_tr[:, 0:512]))
            p.dma("gpsimd", lambda e, t=t: e.dma_start(
                hR_sp[t * 128:(t + 1) * 128, :], hT_t[:]))
        p.emit()

    with ExitStack() as actx:
        sb = lambda n, s, d=F32: actx.enter_context(nc.sbuf_tensor(n, s, d))
        ps = lambda n, s, d=F32: actx.enter_context(nc.psum_tensor(n, s, d))
        p = Pass(b, "l1f2b")
        Rd = sb("r2b_rd", [128, 32 * D], BF16)
        stage = sb("r2b_stage", [128, 1024])
        hT_t = sb("r2b_hT", [128, I1R], BF16)
        ctile = sb("r2b_ct", [128, D], BF16)
        ps_y = ps("r2b_py", [128, 2 * 512])
        load_weight_bf16(p, my_rd, Rd, stage, I1R, D)
        for t in range(CAP // 128):
            p.dma("sync", lambda e, t=t: e.dma_start(
                hT_t[:], hR_sp[t * 128:(t + 1) * 128, :]))
            def ymm(e):
                ins = None
                for i in range(32):
                    for fg in range(2):
                        ins = e.matmul(
                            ps_y[:, fg * 512:(fg + 1) * 512],
                            hT_t[:, i * 128:(i + 1) * 128],
                            Rd[:, i * D + fg * 512:i * D + fg * 512 + 512],
                            start=(i == 0), stop=(i == 31))
                return ins
            p.eng("tensor", ymm)
            p.eng("vector", lambda e: e.tensor_copy(ctile[:], ps_y[:]))
            p.dma("gpsimd", lambda e, t=t: e.dma_start(
                contrib[t * 128:(t + 1) * 128, :], ctile[:]))
        p.coll(lambda e: e.collective_compute(
            "AllGather", ALU.bypass, [list(range(NCORE))],
            [contrib[:, :]], [ag_contrib[:, :]]))
        p.emit()

    with ExitStack() as actx:
        sb = lambda n, s, d=F32: actx.enter_context(nc.sbuf_tensor(n, s, d))
        ps = lambda n, s, d=F32: actx.enter_context(nc.psum_tensor(n, s, d))
        p = Pass(b, "l1f3")
        Sg = sb("r3_sg", [128, 8 * I1S], BF16)
        stages = [sb(f"r3_stage{i}", [128, 1024]) for i in range(2)]
        xT = [sb(f"r3_xT{i}", [128, D], BF16) for i in range(2)]
        g_t = [sb(f"r3_gt{i}", [128, I1S], BF16) for i in range(2)]
        ps_g = [ps(f"r3_pg{i}", [128, 512]) for i in range(2)]
        load_weight_pl(p, sg, Sg, stages, D, I1S, "w", 8)
        for j in range(NT):
            q = j % 2
            p.dma("sync", lambda e, j=j, q=q: e.dma_start(
                xT[q][:], xT_sp[j * 128:(j + 1) * 128, :]),
                group=q, rw=(["xTsp"], [f"xT{q}"]))
            for fg in range(16):
                fq = fg % 2
                def gmm(e, fg=fg, q=q, fq=fq):
                    ins = None
                    for d in range(8):
                        ins = e.matmul(ps_g[fq][:, :],
                                       xT[q][:, d * 128:(d + 1) * 128],
                                       Sg[:, d * I1S + fg * 512:
                                          d * I1S + fg * 512 + 512],
                                       start=(d == 0), stop=(d == 7))
                    return ins
                p.eng("tensor", gmm,
                      rw=([f"xT{q}"] + wres("w", 8, I1S, fg * 512,
                                            fg * 512 + 512), [f"pg{fq}"]))
                p.eng("scalar", lambda e, fg=fg, q=q, fq=fq: e.activation(
                    g_t[q][:, fg * 512:(fg + 1) * 512], ps_g[fq][:],
                    AF.Silu), rw=([f"pg{fq}"], [f"gt{q}"]))
            p.dma("gpsimd", lambda e, j=j, q=q: e.dma_start(
                g_sp[j * 128:(j + 1) * 128, :], g_t[q][:]),
                group=2 + q,
                rw=([f"gt{q}", f"stq{2 + q}"], [f"stq{2 + q}"]))
        p.emit()

    with ExitStack() as actx:
        sb = lambda n, s, d=F32: actx.enter_context(nc.sbuf_tensor(n, s, d))
        ps = lambda n, s, d=F32: actx.enter_context(nc.psum_tensor(n, s, d))
        p = Pass(b, "l1f4")
        Su = sb("r4_su", [128, 8 * I1S], BF16)
        stages = [sb(f"r4_stage{i}", [128, 1024]) for i in range(2)]
        xT = [sb(f"r4_xT{i}", [128, D], BF16) for i in range(2)]
        g_t = [sb(f"r4_gt{i}", [128, I1S], BF16) for i in range(2)]
        h_bf = [sb(f"r4_h{i}", [128, 512], BF16) for i in range(2)]
        hT_t1 = sb("r4_hT", [128, I1S], BF16)
        hT_t = [hT_t1, hT_t1]
        ps_u = [ps(f"r4_pu{i}", [128, 512]) for i in range(2)]
        ps_tr = [ps(f"r4_ptr{i}", [128, 512], BF16) for i in range(2)]
        load_weight_pl(p, su, Su, stages, D, I1S, "w", 8)
        for j in range(NT):
            q = j % 2
            p.dma("sync", lambda e, j=j, q=q: e.dma_start(
                xT[q][:], xT_sp[j * 128:(j + 1) * 128, :]),
                group=q, rw=(["xTsp"], [f"xT{q}"]))
            p.dma("sync", lambda e, j=j, q=q: e.dma_start(
                g_t[q][:], g_sp[j * 128:(j + 1) * 128, :]),
                group=4 + q, rw=(["gsp"], [f"gt{q}"]))
            for fg in range(16):
                fq = fg % 2
                def umm(e, fg=fg, q=q, fq=fq):
                    ins = None
                    for d in range(8):
                        ins = e.matmul(ps_u[fq][:, :],
                                       xT[q][:, d * 128:(d + 1) * 128],
                                       Su[:, d * I1S + fg * 512:
                                          d * I1S + fg * 512 + 512],
                                       start=(d == 0), stop=(d == 7))
                    return ins
                p.eng("tensor", umm,
                      rw=([f"xT{q}"] + wres("w", 8, I1S, fg * 512,
                                            fg * 512 + 512), [f"pu{fq}"]))
                p.eng("vector", lambda e, fg=fg, q=q, fq=fq: e.tensor_tensor(
                    h_bf[fq][:, :], ps_u[fq][:],
                    g_t[q][:, fg * 512:(fg + 1) * 512], ALU.mult),
                    rw=([f"pu{fq}", f"gt{q}"], [f"h{fq}"]))
                def htr(e, fq=fq):
                    ins = None
                    for k in range(4):
                        ins = e.transpose(ps_tr[fq][:, k * 128:(k + 1) * 128],
                                          h_bf[fq][:, k * 128:(k + 1) * 128],
                                          C.ident_bf[:, :])
                    return ins
                p.eng("tensor", htr, rw=([f"h{fq}"], [f"ptr{fq}"]))
                p.eng("vector", lambda e, fg=fg, q=q, fq=fq: e.tensor_copy(
                    hT_t[q][:, fg * 512:(fg + 1) * 512], ps_tr[fq][:, 0:512]),
                    rw=([f"ptr{fq}"], ["hTt"]))
            p.dma("gpsimd", lambda e, j=j, q=q: e.dma_start(
                h_sp[j * 128:(j + 1) * 128, :], hT_t[q][:]),
                group=6,
                rw=(["hTt", "stq6"], ["stq6"]))
        p.emit()

    with ExitStack() as actx:
        sb = lambda n, s, d=F32: actx.enter_context(nc.sbuf_tensor(n, s, d))
        ps = lambda n, s, d=F32: actx.enter_context(nc.psum_tensor(n, s, d))
        p = Pass(b, "l1f5")
        bcreg = {}
        def prelude(g):
            r = g.alloc_register("bcreg5")
            g.reg_mov(r, E1 * CAP - 1)
            bcreg["r"] = r
        p.gpsimd_prelude = prelude
        Sd = sb("r5_sd", [128, 64 * D], BF16)
        stages = [sb(f"r5_stage{i}", [128, 1024]) for i in range(2)]
        hT_t = [sb(f"r5_hT{i}", [128, I1S], BF16) for i in range(2)]
        gath = [sb(f"r5_gath{i}", [128, D], BF16) for i in range(2)]
        acc = [sb(f"r5_acc{i}", [128, D]) for i in range(2)]
        xin = [sb(f"r5_xin{i}", [128, D]) for i in range(2)]
        outt = [sb(f"r5_out{i}", [128, D]) for i in range(2)]
        scr_sh = sb("r5_scr", [128, D])
        st2 = []
        for i in range(2):
            d_ = {k: sb(f"r5_st{i}_" + k, [128, 1]) for k in
                  ["ssq", "ssum", "mu", "var", "rstd"]}
            d_["scr"] = scr_sh
            st2.append(d_)
        ps_y = [ps(f"r5_py{i}", [128, 2 * 512]) for i in range(2)]
        load_weight_pl(p, sd, Sd, stages, I1S, D, "w", 8)
        p.eng("vector", lambda e: e.memset(gath[0][:, :], 0.0),
              rw=([], ["gath0"]))
        p.eng("vector", lambda e: e.memset(gath[1][:, :], 0.0),
              rw=([], ["gath1"]))
        for j in range(NT):
            q = j % 2
            p.dma("sync", lambda e, j=j, q=q: e.dma_start(
                hT_t[q][:], h_sp[j * 128:(j + 1) * 128, :]),
                group=q, rw=(["hsp"], [f"hTt{q}"]))
            def ymm(e, q=q):
                ins = None
                for i in range(64):
                    for fg in range(2):
                        ins = e.matmul(
                            ps_y[q][:, fg * 512:(fg + 1) * 512],
                            hT_t[q][:, i * 128:(i + 1) * 128],
                            Sd[:, i * D + fg * 512:i * D + fg * 512 + 512],
                            start=(i == 0), stop=(i == 63))
                return ins
            p.eng("tensor", ymm,
                  rw=([f"hTt{q}"] + wres("w", I1S // 128, D, 0, D),
                      [f"py{q}"]))
            p.eng("vector", lambda e, q=q: e.tensor_copy(acc[q][:],
                                                         ps_y[q][:]),
                  rw=([f"py{q}"], [f"acc{q}"]))
            for e0 in range(E1):
                gq = e0 % 2
                p.dma("gpsimd", lambda e, j=j, e0=e0, gq=gq:
                      e.indirect_dma_start(
                          gath[gq][:, :], None, ag_contrib[:, :],
                          bass.IndirectOffsetOnAxis(
                              idx2_em[:, e0 * NT + j:e0 * NT + j + 1], 0),
                          bounds_check=bcreg["r"], oob_is_err=False),
                      group=4 + gq, rw=(["agc"], [f"gath{gq}"]))
                p.eng("vector", lambda e, j=j, e0=e0, q=q, gq=gq:
                      e.scalar_tensor_tensor(
                          acc[q][:, :], gath[gq][:, :],
                          weff_em[:, e0 * NT + j:e0 * NT + j + 1],
                          acc[q][:, :], ALU.mult, ALU.add),
                      rw=([f"gath{gq}", f"acc{q}"], [f"acc{q}"]))
            p.dma("sync", lambda e, j=j, q=q: e.dma_start(
                xin[q][:], xs[j * 128:(j + 1) * 128, :]),
                group=2 + q, rw=(["xsrc"], [f"xin{q}"]))
            p.eng("vector", lambda e, q=q: e.tensor_tensor(
                acc[q][:], acc[q][:], xin[q][:], ALU.add),
                rw=([f"acc{q}", f"xin{q}"], [f"acc{q}"]))
            ln_pl(p, C, acc[q], f"acc{q}", outt[q], f"outt{q}", st2[q],
                  f"ln{q}", scr_res="scrSH")
            p.dma("gpsimd", lambda e, j=j, q=q: e.dma_start(
                ys[j * 128:(j + 1) * 128, :], outt[q][:]),
                group=6 + q,
                rw=([f"outt{q}", f"stq{6 + q}"], [f"stq{6 + q}"]))
        p.emit()
    return ["l1f0", "l1f2a", "l1f2b", "l1f3", "l1f4", "l1f5"]


def build_program(slots, fused):
    from contextlib import ExitStack
    nc = bass.Bass("TRN2", target_bir_lowering=False)
    b = Builder(nc)
    xs_ext = b.inp("xs", [TOK, D], F32)
    ys_ext = b.out("ys", [TOK, D], F32)
    bufs = [xs_ext]
    for i in range(len(slots) - 1):
        bufs.append(b.dram(f"stream{i}", [TOK, D], F32))
    bufs.append(ys_ext)

    with ExitStack() as sctx:
        C, initp = setup_common(b, sctx)
        initp.emit()
        for i, s in enumerate(slots):
            src, dst = bufs[i], bufs[i + 1]
            if s == "l0_attn":
                build_l0_attn(b, C, sctx, src, dst)
            elif s == "l0_ffn":
                build_l0_ffn(b, C, sctx, src, dst)
            elif s == "l1_attn":
                build_l1_attn(b, C, sctx, src, dst)
            elif s == "l1_ffn":
                build_l1_ffn(b, C, sctx, src, dst)
            else:
                raise ValueError(s)
        if DEBUG_DUMPS:
            p = Pass(b, "dbgdump")
            for name in DEBUG_DUMPS:
                ap = b.drams[name]
                o = b.out("dump_" + name, ap.shape, ap.dtype)
                p.dma("gpsimd", lambda e, o=o, ap=ap: e.dma_start(o[:, :],
                                                                 ap[:, :]))
            p.steps.append(("sync", lambda e: e.nop(), "eng"))
            p.emit()
    return nc, b


def build_maskcap(core):
    out = np.empty((6, 128, 768), np.float32)
    col = np.arange(768)[None, :]
    pp = np.arange(128)[:, None]
    for s, j in enumerate([0, 1, 2, 3, 30, 31]):
        pj = j % 2
        rel = (col // 128 - 2 - pj) * 128 + (col % 128) - pp
        valid = np.abs(rel) <= W1W
        if j <= 1 and core % 2 == 0:
            valid = valid & (col >= 256)
        if j >= 30 and core % 2 == 1:
            valid = valid & (col < 512)
        out[s] = np.where(valid, 1e30, -1e9)
    return np.ascontiguousarray(out.reshape(6 * 128, 768).astype(np.float32))


def aux_inputs(core, inputs):
    bsel = np.zeros((128, 4), np.float32)
    bsel[:, core // 2] = 1.0
    lsel = np.zeros((128, NCORE), np.float32)
    rsel = np.zeros((128, NCORE), np.float32)
    if core % 2 == 1:
        lsel[:, core - 1] = 1.0
    else:
        rsel[:, core + 1] = 1.0
    pm = np.zeros((E1, 1), np.float32)
    pm[:core, 0] = 1.0
    out = {"aux_bsel": bsel, "aux_lsel": lsel, "aux_rsel": rsel,
           "aux_maskcap": build_maskcap(core), "aux_pm": pm}
    if "L1_ffn_rg" in inputs:
        out["aux_my_rg"] = np.ascontiguousarray(
            np.asarray(inputs["L1_ffn_rg"][core], np.float32))
        out["aux_my_ru"] = np.ascontiguousarray(
            np.asarray(inputs["L1_ffn_ru"][core], np.float32))
        out["aux_my_rd"] = np.ascontiguousarray(
            np.asarray(inputs["L1_ffn_rd"][core], np.float32))
    return out


def shard_rows(x):
    out = []
    for c in range(NCORE):
        bidx, h = c // 2, c % 2
        out.append(np.ascontiguousarray(
            x[bidx, h * TOK:(h + 1) * TOK, :]))
    return out


def kernel(**inputs):
    x = np.asarray(inputs["L0_attn_src"], np.float32)
    shards = shard_rows(x)

    slots = SLOTS
    nc, b = build_program(slots, fused=True)

    in_maps = []
    for c in range(NCORE):
        aux = aux_inputs(c, inputs)
        m = {"xs": shards[c]}
        for name in b.inputs:
            if name in aux:
                m[name] = aux[name]
            elif name in inputs:
                m[name] = np.ascontiguousarray(
                    np.asarray(inputs[name], np.float32))
        in_maps.append(m)
    res = run_bass_kernel_spmd(nc, in_maps, core_ids=list(range(NCORE)))
    out = np.empty((4, 8192, D), np.float32)
    for c in range(NCORE):
        bidx, h = c // 2, c % 2
        out[bidx, h * TOK:(h + 1) * TOK, :] = res.results[c]["ys"]
    return out
